# version 18
# baseline (speedup 1.0000x reference)
import numpy as np
import ml_dtypes
from contextlib import ExitStack
import concourse.bass as bass
import concourse.mybir as mybir
from concourse.bass_utils import run_bass_kernel_spmd

F32 = mybir.dt.float32
BF16 = mybir.dt.bfloat16
AF = mybir.ActivationFunctionType
ALU = mybir.AluOpType
AX = mybir.AxisListType.X

P = 128
T, D, E, NS, RK, FF = 2048, 1024, 2048, 16, 64, 4096
TC = 512
NCH = T // TC
NET = E // P
NXT = 2 * E // P
NKD = D // P
NTT = T // P
NFT = FF // P
GRP = 4
NGR = NS // GRP
EPSQ = 1e-5
ENGS = ("sync", "gpsimd", "vector", "scalar", "tensor")


class Sem:
    def __init__(self, nc, ctx, name):
        self.h = ctx.enter_context(nc.semaphore(name))
        self.n = 0


class Bld:
    def __init__(self, nc):
        self.nc = nc
        self.st = {e: [] for e in ENGS}

    def em(self, eng, fn):
        self.st[eng].append(fn)

    def wait(self, eng, sem, v):
        if v and v > 0:
            self.em(eng, lambda e, h=sem.h, vv=v: e.wait_ge(h, vv))

    def dma(self, eng, sem, out, in_):
        sem.n += 16
        self.em(eng, lambda e, o=out, i=in_, h=sem.h:
                e.dma_start(out=o, in_=i).then_inc(h, 16))
        return sem.n

    def op(self, eng, fn, sem=None):
        if sem is None:
            self.em(eng, fn)
            return None
        sem.n += 1
        self.em(eng, lambda e, f=fn, h=sem.h: f(e).then_inc(h, 1))
        return sem.n

    def barrier(self, bar, dma_sems=()):
        for eng, sem in dma_sems:
            self.wait(eng, sem, sem.n)
        for eng in ENGS:
            self.em(eng, lambda e, h=bar.h: e.sem_inc(h, 1))
        bar.n += 5
        tgt = bar.n
        for eng in ENGS:
            self.wait(eng, bar, tgt)

    def flush(self):
        st = self.st
        self.st = {e: [] for e in ENGS}
        with self.nc.Block() as blk:
            for eng in ENGS:
                def runner(e, fns=tuple(st[eng])):
                    for f in fns:
                        f(e)
                getattr(blk, eng)(runner)


def build_program():
    nc = bass.Bass("TRN2", target_bir_lowering=False)
    dr = {}

    def din(name, shape):
        dr[name] = nc.dram_tensor(name, list(shape), F32, kind="ExternalInput").ap()

    din("x", (T, D))
    for s in "fb":
        din(f"win_sl_{s}", (NXT, P, D))
        din(f"wout_sl_{s}", (NET, P, D))
        din(f"wx_sl_{s}", (NET, P, 96))
        din(f"wdt_{s}", (RK, E))
        din(f"cvec_{s}", (P, 112))
        din(f"alog_{s}", (P, NET * NS))
    din("w1t_sl", (NKD, P, FF))
    din("w2t_sl", (NFT, P, D))
    din("wproj_sl", (NET, P, D))
    din("b1col", (P, NFT))
    din("b2row", (1, D))
    din("bprojrow", (1, D))
    din("identity", (P, P))

    cat_scr = nc.dram_tensor("cat_scr", [2 * D, T], BF16, kind="Internal").ap()
    bc_scr = nc.dram_tensor("bc_scr", [2, NS * TC], BF16, kind="Internal").ap()
    w1q_scr = nc.dram_tensor("w1q_scr", [NKD, P, FF], BF16, kind="Internal").ap()
    w2q_scr = nc.dram_tensor("w2q_scr", [NFT, P, D], BF16, kind="Internal").ap()
    h1_scr = nc.dram_tensor("h1_scr", [NFT, P, T], BF16, kind="Internal").ap()
    y_out = nc.dram_tensor("y", [T, D], F32, kind="ExternalOutput").ap()

    with ExitStack() as ctx:
        sPE = Sem(nc, ctx, "sPE")
        sV = Sem(nc, ctx, "sV")
        sA = Sem(nc, ctx, "sA")
        bar = Sem(nc, ctx, "bar")
        sBulkH = Sem(nc, ctx, "sBulkH")
        sBulkG = Sem(nc, ctx, "sBulkG")
        sX = [Sem(nc, ctx, f"sX{i}") for i in range(2)]
        sGam = [Sem(nc, ctx, f"sGam{i}") for i in range(2)]
        sPreL = [Sem(nc, ctx, f"sPreL{i}") for i in range(2)]
        sPreS = [Sem(nc, ctx, f"sPreS{i}") for i in range(2)]
        sWin = [Sem(nc, ctx, f"sWin{i}") for i in range(2)]
        sBC = Sem(nc, ctx, "sBC")
        sQl = Sem(nc, ctx, "sQl")
        sQs = Sem(nc, ctx, "sQs")
        sCat = [Sem(nc, ctx, f"sCat{i}") for i in range(2)]
        sW2s = [Sem(nc, ctx, f"sW2s{i}") for i in range(2)]
        sCatL = [Sem(nc, ctx, f"sCatL{i}") for i in range(2)]
        sSt = [Sem(nc, ctx, f"sSt{i}") for i in range(2)]
        sGp = Sem(nc, ctx, "sGp")
        sBC2 = Sem(nc, ctx, "sBC2")
        sCatG = [Sem(nc, ctx, f"sCatG{i}") for i in range(2)]
        sXG = [Sem(nc, ctx, f"sXG{i}") for i in range(2)]
        sWinG = [Sem(nc, ctx, f"sWinG{i}") for i in range(2)]
        sW1f = [Sem(nc, ctx, f"sW1f{i}") for i in range(2)]
        sH1p = [Sem(nc, ctx, f"sH1p{i}") for i in range(2)]
        sH1G = [Sem(nc, ctx, f"sH1G{i}") for i in range(2)]
        sH1G2 = [Sem(nc, ctx, f"sH1G2{i}") for i in range(2)]

        xn = ctx.enter_context(nc.sbuf_tensor("xn_cm", [P, NKD * T], BF16))
        ident = ctx.enter_context(nc.sbuf_tensor("ident", [P, P], BF16))
        gfin = ctx.enter_context(nc.sbuf_tensor("gfin", [P, 16], F32))

        px = [ctx.enter_context(nc.psum_tensor(f"px{i}", [P, TC], F32)) for i in range(2)]
        pdb = ctx.enter_context(nc.psum_tensor("pdb", [P, TC], F32))
        pd = [ctx.enter_context(nc.psum_tensor(f"pd{i}", [P, TC], F32)) for i in range(2)]
        py = [ctx.enter_context(nc.psum_tensor(f"py{i}", [P, TC], F32)) for i in range(2)]
        pw = ctx.enter_context(nc.psum_tensor("pw", [P, TC], F32))
        pall = [px[0], px[1], pdb, pd[0], pd[1], py[0], py[1], pw]

        b = Bld(nc)

        def xn_sl(k, t0, tl):
            return xn[:, k * T + t0: k * T + t0 + tl]

        with ExitStack() as p0:
            xst = p0.enter_context(nc.sbuf_tensor("xst", [P, 2 * D], F32))
            sqst = p0.enter_context(nc.sbuf_tensor("sqst", [P, D], F32))
            stats = p0.enter_context(nc.sbuf_tensor("stats", [P, 8 * NTT], F32))
            xnt = p0.enter_context(nc.sbuf_tensor("xnt", [P, 2 * D], F32))
            wstg = p0.enter_context(nc.sbuf_tensor("wstg", [P, 2 * FF], F32))
            gacc = p0.enter_context(nc.sbuf_tensor("gacc", [P, 48], F32))
            onesf = p0.enter_context(nc.sbuf_tensor("onesf", [P, P], F32))

            identf = p0.enter_context(nc.sbuf_tensor("identf", [P, P], F32))
            b.dma("gpsimd", sBulkG, ident[:], dr["identity"])
            v_id = b.dma("sync", sBulkH, identf[:], dr["identity"])
            b.op("vector", lambda e: e.memset(onesf[:], 1.0))
            b.op("vector", lambda e: e.memset(gfin[:, 15:16], 1e-5), sV)

            st1 = []
            for i in range(NTT):
                if i >= 2:
                    b.wait("sync", sV, st1[i - 2][0])
                    b.wait("sync", sA, st1[i - 2][1])
                sl = xst[:, (i % 2) * D:(i % 2 + 1) * D]
                vld = b.dma("sync", sX[i % 2], sl, dr["x"][i * P:(i + 1) * P, :])
                b.wait("vector", sX[i % 2], vld)
                vr = b.op("vector", lambda e, o=stats[:, i:i + 1], s=sl:
                          e.reduce_sum(out=o, in_=s, axis=AX), sV)
                b.wait("scalar", sX[i % 2], vld)
                if i >= 1:
                    b.wait("scalar", sA, st1[i - 1][1])
                va = b.op("scalar", lambda e, o=sqst[:], s=sl,
                          a=stats[:, NTT + i:NTT + i + 1]:
                          e.activation(out=o, in_=s, func=AF.Square, accum_out=a), sA)
                st1.append((vr, va))

            sum_c = stats[:, 0:NTT]
            m2_c = stats[:, NTT:2 * NTT]
            mean_c = stats[:, 2 * NTT:3 * NTT]
            q_c = stats[:, 3 * NTT:4 * NTT]
            var_c = stats[:, 4 * NTT:5 * NTT]
            std_c = stats[:, 5 * NTT:6 * NTT]
            rstd_c = stats[:, 6 * NTT:7 * NTT]
            nmr_c = stats[:, 7 * NTT:8 * NTT]
            b.wait("vector", sA, sA.n)
            b.wait("vector", sV, st1[-1][0])
            v_mean = b.op("vector", lambda e: e.tensor_scalar(out=mean_c, in0=sum_c,
                          scalar1=1.0 / D, scalar2=None, op0=ALU.mult), sV)
            b.wait("vector", sV, v_mean)
            v_q = b.op("vector", lambda e: e.tensor_mul(out=q_c, in0=mean_c,
                       in1=mean_c), sV)
            b.wait("vector", sV, v_q)
            b.op("vector", lambda e: e.scalar_tensor_tensor(out=var_c, in0=m2_c,
                 scalar=1.0 / D, in1=q_c, op0=ALU.mult, op1=ALU.subtract), sV)
            b.wait("scalar", sV, sV.n)
            va_std = b.op("scalar", lambda e: e.activation(out=std_c, in_=var_c,
                          func=AF.Sqrt, bias=gfin[:, 15:16]), sA)
            b.wait("vector", sA, va_std)
            v_rec = b.op("vector", lambda e: e.reciprocal(out=rstd_c, in_=std_c), sV)
            b.op("vector", lambda e: e.tensor_copy(out=sqst[:, 0:NTT], in_=std_c))
            b.wait("vector", sV, v_rec)
            v_nmr = b.op("vector", lambda e: e.scalar_tensor_tensor(out=nmr_c,
                         in0=mean_c, scalar=-1.0, in1=rstd_c, op0=ALU.mult,
                         op1=ALU.mult), sV)

            ev_tr = []
            ap2 = []
            b.wait("sync", sV, sV.n)
            b.wait("sync", sA, sA.n)
            b.wait("tensor", sBulkH, v_id)
            b.wait("tensor", sBulkG, 16)
            for i in range(NTT):
                if i >= 2:
                    b.wait("sync", sV, ap2[i - 2][0])
                sl = xst[:, (i % 2) * D:(i % 2 + 1) * D]
                xnsl = xnt[:, (i % 2) * D:(i % 2 + 1) * D]
                vld = b.dma("sync", sX[i % 2], sl, dr["x"][i * P:(i + 1) * P, :])
                b.wait("vector", sX[i % 2], vld)
                b.wait("vector", sV, v_nmr)
                if i >= 2:
                    b.wait("vector", sPE, ap2[i - 2][1])
                va = b.op("vector", lambda e, o=xnsl, s=sl, r=rstd_c[:, i:i + 1],
                          nm=nmr_c[:, i:i + 1]:
                          e.tensor_scalar(out=o, in0=s, scalar1=r, scalar2=nm,
                                          op0=ALU.mult, op1=ALU.add), sV)
                b.wait("tensor", sV, va)
                vpe_last = 0
                for k in range(NKD):
                    idx = i * NKD + k
                    bank = px[idx % 2]
                    if idx >= 2:
                        b.wait("tensor", sA, ev_tr[idx - 2])
                    vpe = b.op("tensor", lambda e, o=bank[:, 0:P],
                               inp=xnsl[:, k * P:(k + 1) * P]:
                               e.transpose(out=o, in_=inp, identity=identf[:]), sPE)
                    b.wait("scalar", sPE, vpe)
                    ev = b.op("scalar", lambda e, o=xn_sl(k, i * P, P), pb=bank:
                              e.activation(out=o, in_=pb[:, 0:P], func=AF.Copy), sA)
                    ev_tr.append(ev)
                    vpe_last = vpe
                ap2.append((va, vpe_last))


            gld = []
            for k in range(NKD):
                if k >= 2:
                    b.wait("gpsimd", sV, gld[k - 2])
                sl = wstg[:, (k % 2) * FF:(k % 2 + 1) * FF]
                vld = b.dma("gpsimd", sGam[k % 2], sl, dr["w1t_sl"][k])
                b.wait("vector", sGam[k % 2], vld)
                vr = b.op("vector", lambda e, o=gacc[:, k:k + 1], s=sl:
                          e.tensor_reduce(out=o, in_=s, axis=AX, op=ALU.add,
                                          apply_absolute_value=True), sV)
                gld.append(vr)
            g2 = []
            b.wait("gpsimd", sV, gld[-1])
            for j in range(NFT):
                if j >= 2:
                    b.wait("gpsimd", sV, g2[j - 2])
                sl = wstg[:, (j % 2) * D:(j % 2 + 1) * D]
                vld = b.dma("gpsimd", sGam[j % 2], sl, dr["w2t_sl"][j])
                b.wait("vector", sGam[j % 2], vld)
                vr = b.op("vector", lambda e, o=gacc[:, 8 + j:9 + j], s=sl:
                          e.tensor_reduce(out=o, in_=s, axis=AX, op=ALU.add,
                                          apply_absolute_value=True), sV)
                g2.append(vr)
            b.wait("vector", sV, gld[-1])
            b.wait("vector", sV, g2[-1])
            v_r1 = b.op("vector", lambda e: e.reduce_sum(out=gacc[:, 40:41],
                        in_=gacc[:, 0:NKD], axis=AX), sV)
            v_ga = b.op("vector", lambda e: e.reduce_sum(out=gacc[:, 41:42],
                        in_=gacc[:, 8:8 + NFT], axis=AX), sV)
            b.wait("tensor", sV, v_ga)
            vmm = b.op("tensor", lambda e: e.matmul(out=pw[:, 0:2], lhsT=onesf[:],
                       rhs=gacc[:, 40:42], start=True, stop=True), sPE)
            b.wait("vector", sPE, vmm)
            v_gc = b.op("vector", lambda e: e.tensor_copy(out=gfin[:, 10:12],
                        in_=pw[:, 0:2]), sV)
            b.wait("vector", sV, v_gc)
            v_t1 = b.op("vector", lambda e: e.tensor_scalar(out=gfin[:, 0:1],
                 in0=gfin[:, 10:11], scalar1=0.5 / float(D * FF), scalar2=0.5 * EPSQ,
                 op0=ALU.mult, op1=ALU.add), sV)
            v_t2 = b.op("vector", lambda e: e.tensor_scalar(out=gfin[:, 1:2],
                 in0=gfin[:, 11:12], scalar1=0.5 / float(FF * D), scalar2=0.5 * EPSQ,
                 op0=ALU.mult, op1=ALU.add), sV)
            b.wait("vector", sV, v_t2)
            b.op("vector", lambda e: e.tensor_scalar(out=gfin[:, 2:3], in0=gfin[:, 0:1],
                 scalar1=-1.0, scalar2=None, op0=ALU.mult), sV)
            b.op("vector", lambda e: e.tensor_scalar(out=gfin[:, 3:4], in0=gfin[:, 1:2],
                 scalar1=-1.0, scalar2=None, op0=ALU.mult), sV)
            b.barrier(bar, [("sync", sBulkH), ("sync", sX[0]), ("sync", sX[1]), ("sync", sPreS[0]), ("sync", sPreS[1]), ("sync", sWin[0]), ("sync", sWin[1]), ("sync", sW2s[0]), ("sync", sW2s[1]), ("sync", sCatL[0]), ("sync", sCatL[1]), ("gpsimd", sBulkG), ("gpsimd", sGam[0]), ("gpsimd", sGam[1]), ("gpsimd", sPreL[0]), ("gpsimd", sPreL[1]), ("gpsimd", sBC), ("gpsimd", sBC2), ("gpsimd", sQl), ("gpsimd", sQs), ("gpsimd", sCatG[0]), ("gpsimd", sCatG[1]), ("gpsimd", sXG[0]), ("gpsimd", sXG[1]), ("gpsimd", sWinG[0]), ("gpsimd", sWinG[1]), ("gpsimd", sCat[0]), ("gpsimd", sCat[1])])
            b.flush()

        with ExitStack() as p12:
            wslot = p12.enter_context(nc.sbuf_tensor("wslot", [P, 2 * D], BF16))
            wout_sb = p12.enter_context(nc.sbuf_tensor("wout_sb", [P, NET * D], BF16))
            wx_sb = p12.enter_context(nc.sbuf_tensor("wx_sb", [P, 2 * NET * 96], BF16))
            wdt_sb = p12.enter_context(nc.sbuf_tensor("wdt_sb", [RK, E], BF16))
            cvec = p12.enter_context(nc.sbuf_tensor("cvec", [P, 2 * 112], F32))
            atmp = p12.enter_context(nc.sbuf_tensor("atmp", [P, NET * NS], F32))
            A_sb = p12.enter_context(nc.sbuf_tensor("A_sb", [P, 2 * NET * NS], F32))
            u_ext = p12.enter_context(nc.sbuf_tensor("u_ext", [P, 2 * (TC + 16)], BF16))
            utail = p12.enter_context(nc.sbuf_tensor("utail", [P, 4 * NET], BF16))
            cacc = p12.enter_context(nc.sbuf_tensor("cacc", [P, 2 * TC], BF16))
            u_sil = p12.enter_context(nc.sbuf_tensor("u_sil", [P, NET * TC], BF16))
            z_sb = p12.enter_context(nc.sbuf_tensor("z_sb", [P, NET * TC], BF16))
            dbc_sb = p12.enter_context(nc.sbuf_tensor("dbc_sb", [P, TC], BF16))
            B_bc = p12.enter_context(nc.sbuf_tensor("B_bc", [P, NS * TC], BF16))
            C_bc = p12.enter_context(nc.sbuf_tensor("C_bc", [P, NS * TC], BF16))
            delta_sb = p12.enter_context(nc.sbuf_tensor("delta_sb", [P, 2 * TC], BF16))
            spe = p12.enter_context(nc.sbuf_tensor("spe", [P, TC], BF16))
            du_sb = p12.enter_context(nc.sbuf_tensor("du_sb", [P, 2 * TC], BF16))
            TCP = TC + 2
            dA_sl = p12.enter_context(nc.sbuf_tensor("dA_sl", [P, 2 * GRP * TCP], BF16))
            h_sl = p12.enter_context(nc.sbuf_tensor("h_sl", [P, 2 * GRP * TCP], BF16))
            dBu_sb = p12.enter_context(nc.sbuf_tensor("dBu_sb", [P, GRP * TCP], BF16))
            hcar = p12.enter_context(nc.sbuf_tensor("hcar", [P, NET * NS], F32))
            ytmp = p12.enter_context(nc.sbuf_tensor("ytmp", [P, TC], BF16))
            Y_sb = p12.enter_context(nc.sbuf_tensor("Y_sb", [P, NET * TC], BF16))
            cstage = p12.enter_context(nc.sbuf_tensor("cstage", [P, 2 * TC], BF16))
            qst = p12.enter_context(nc.sbuf_tensor("qst", [P, TC], F32))
            qsg = p12.enter_context(nc.sbuf_tensor("qsg", [P, 2 * TC], BF16))
            w1f = p12.enter_context(nc.sbuf_tensor("w1f", [P, 2 * NKD * P], BF16))
            h1st = p12.enter_context(nc.sbuf_tensor("h1st", [P, 2 * TC], BF16))
            b1c12 = p12.enter_context(nc.sbuf_tensor("b1c12", [P, NFT], F32))

            jobs = []
            for half, s in enumerate("fb"):
                crange = range(NCH) if s == "f" else range(NCH - 1, -1, -1)
                for c in crange:
                    jobs.append((s, c, half))

            qunits = [("w1", k, q) for k in range(NKD) for q in range(8)] + \
                     [("w2", jj, q) for jj in range(NFT) for q in range(2)]
            qstate = {"ld": 0, "cp": 0, "sign": 0, "store": 0, "vld": {}, "stv": {}}

            def q_src_dst(i):
                kind, a, q = qunits[i]
                if kind == "w1":
                    return (dr["w1t_sl"][a][:, q * TC:(q + 1) * TC],
                            w1q_scr[a][:, q * TC:(q + 1) * TC],
                            gfin[:, 2:3], gfin[:, 0:1])
                return (dr["w2t_sl"][a][:, q * TC:(q + 1) * TC],
                        w2q_scr[a][:, q * TC:(q + 1) * TC],
                        gfin[:, 3:4], gfin[:, 1:2])

            def pump_quant(nu):
                for _ in range(nu):
                    if qstate["cp"] < qstate["ld"]:
                        i = qstate["cp"]
                        qstate["cp"] = i + 1
                        srcp, dst, thn, thp = q_src_dst(i)
                        qsl = qst[:, 0:TC]
                        s1 = qsg[:, 0:TC]
                        s2 = qsg[:, TC:2 * TC]
                        b.wait("scalar", sQl, qstate["vld"][i])
                        b.wait("scalar", sQs, qstate["store"])
                        b.op("scalar", lambda e, o=s1, s=qsl, t=thn:
                             e.activation(out=o, in_=s, func=AF.Sign, bias=t))
                        qstate["sign"] = b.op("scalar", lambda e, o=s2, s=qsl, t=thp:
                                              e.activation(out=o, in_=s, func=AF.Sign,
                                                           bias=t), sA)
                        b.wait("vector", sA, qstate["sign"])
                        v_ad = b.op("vector", lambda e, o=s1, a1=s1, a2=s2:
                                    e.tensor_add(out=o, in0=a1, in1=a2), sV)
                        b.wait("vector", sV, v_ad)
                        v_ts = b.op("vector", lambda e, o=s1, a1=s1:
                                    e.tensor_scalar(out=o, in0=a1, scalar1=0.5,
                                                    scalar2=None, op0=ALU.mult), sV)
                        b.wait("gpsimd", sV, v_ts)
                        qstate["store"] = b.dma("gpsimd", sQs, dst, s1)
                        qstate["stv"][i] = qstate["store"]
                    if qstate["ld"] < len(qunits):
                        i = qstate["ld"]
                        qstate["ld"] = i + 1
                        srcp, dst, thn, thp = q_src_dst(i)
                        qsl = qst[:, 0:TC]
                        if i >= 1:
                            b.wait("gpsimd", sA, qstate["sign"])
                        qstate["vld"][i] = b.dma("gpsimd", sQl, qsl, srcp)

            h1s = {"qm": 0, "qe": 0, "vld": {}, "mmv": {}, "mmft": {},
                   "relu": {}, "stv": {}}

            def h1_evac():
                    if h1s["qe"] < h1s["qm"]:
                        q = h1s["qe"]
                        h1s["qe"] = q + 1
                        ft, c = q // NCH, q % NCH
                        b.wait("scalar", sPE, h1s["mmv"][q])
                        pst = h1s["stv"].get(q - 2)
                        if pst:
                            b.wait("scalar", sH1p[q % 2], pst)
                        h1s["relu"][q] = b.op("scalar", lambda e,
                            o=h1st[:, (q % 2) * TC:(q % 2 + 1) * TC], pb=pw,
                            bc=b1c12[:, ft:ft + 1]:
                            e.activation(out=o, in_=pb[:], func=AF.Relu, bias=bc),
                            sA)
                        mk["pw_ev"] = h1s["relu"][q]
                        b.wait("sync", sA, h1s["relu"][q])
                        h1s["stv"][q] = b.dma("sync", sH1p[q % 2],
                                              h1_scr[ft][:, c * TC:(c + 1) * TC],
                                              h1st[:, (q % 2) * TC:(q % 2 + 1) * TC])

            def pump_h1(nu):
                for _ in range(nu):
                    h1_evac()
                    q = h1s["qm"]
                    if q >= NFT * NCH:
                        continue
                    h1s["qm"] = q + 1
                    ft, c = q // NCH, q % NCH
                    fpar = ft % 2
                    if c == 0:
                        prev_mm = h1s["mmft"].get(ft - 2)
                        if prev_mm:
                            b.wait("sync", sPE, prev_mm)
                        b.wait("sync", sQs, qstate["stv"][7 * NKD + ft // 4])
                        h1s["vld"][ft] = b.dma("sync", sW1f[fpar],
                            w1f[:, fpar * NKD * P:(fpar + 1) * NKD * P],
                            w1q_scr[0:NKD, :, ft * P:(ft + 1) * P]
                            .transpose([1, 0, 2]))
                        b.wait("tensor", sW1f[fpar], h1s["vld"][ft])
                    pv = mk.get("pw_ev")
                    if pv:
                        b.wait("tensor", sA, pv)
                    for k in range(NKD):
                        lastk = (k == NKD - 1)
                        v = b.op("tensor", lambda e, o=pw[:],
                                 w=w1f[:, fpar * NKD * P + k * P:
                                       fpar * NKD * P + (k + 1) * P],
                                 r=xn_sl(k, c * TC, TC), st=(k == 0), sp=lastk:
                                 e.matmul(out=o, lhsT=w, rhs=r, start=st, stop=sp),
                                 sPE if lastk else None)
                    h1s["mmv"][q] = v
                    h1s["mmft"][ft] = v

            mk = {
                "xz_mm": {}, "xz_ev": {}, "conv": {}, "usil": {},
                "dbc_ev": {}, "bc_dma": {}, "minev": {}, "lnsp": {}, "dA": {},
                "scan": {}, "yacc": {}, "gate": {}, "wout_last": {}, "wout_ev": {},
                "cat_dma": {}, "delta_mm": {}, "zsil": {},
                "bc_users": {}, "prep": {}, "b_dma": {},
            }
            GG = {"n": 0}
            WINLD = {"cnt": [0, 0]}

            def dirpar(j):
                return jobs[j][2]

            def emit_scan_memsets(fwd):
                bk = 0 if fwd else (TC + 1)
                fl = 1 if fwd else TC
                v = None
                for gp in range(2):
                    base = gp * GRP * TCP
                    v = b.op("vector", lambda e, o=dA_sl[:, base + bk:base + bk
                             + (GRP - 1) * TCP + 1:TCP]: e.memset(o, 0.0), sV)
                    v = b.op("vector", lambda e, o=dA_sl[:, base + fl:base + fl
                             + (GRP - 1) * TCP + 1:TCP]: e.memset(o, 1.0), sV)
                v = b.op("vector", lambda e, o=dBu_sb[:, fl:fl
                         + (GRP - 1) * TCP + 1:TCP]: e.memset(o, 0.0), sV)
                return v

            def emit_dir_consts(s, half):
                b.dma("sync", sBulkH, cvec[:, half * 112:(half + 1) * 112],
                      dr[f"cvec_{s}"])
                v_al = b.dma("sync", sBulkH, atmp[:], dr[f"alog_{s}"])
                b.wait("scalar", sBulkH, v_al)
                asl = A_sb[:, half * NET * NS:(half + 1) * NET * NS]
                v_ex = b.op("scalar", lambda e, o=asl: e.activation(
                    out=o, in_=atmp[:], func=AF.Exp), sA)
                b.wait("vector", sA, v_ex)
                v_neg = b.op("vector", lambda e, o=asl: e.tensor_scalar(
                    out=o, in0=o, scalar1=-1.0, scalar2=None, op0=ALU.mult), sV)
                for jj in range(2):
                    b.dma("gpsimd", sBulkG,
                          wx_sb[:, (half * NET + jj * 8) * 96:(half * NET + (jj + 1) * 8) * 96],
                          dr[f"wx_sl_{s}"][8 * jj:8 * (jj + 1)].transpose([1, 0, 2]))
                return v_neg, sBulkH.n, sBulkG.n

            def emit_wdt_load(s, gate_pe):
                if gate_pe:
                    b.wait("gpsimd", sPE, gate_pe)
                b.dma("gpsimd", sBulkG, wdt_sb[:], dr[f"wdt_{s}"])
                return sBulkG.n

            def emit_wout_load(s, gate_pe):
                b.wait("gpsimd", sPE, gate_pe)
                for jj in range(NKD):
                    b.dma("gpsimd", sBulkG, wout_sb[:, jj * 2 * D:(jj + 1) * 2 * D],
                          dr[f"wout_sl_{s}"][2 * jj:2 * (jj + 1)].transpose([1, 0, 2]))
                return sBulkG.n

            def emit_A_mm(j, et):
                s, c, half = jobs[j]
                fwd = (s == "f")
                t0 = c * TC
                par = et % 2
                for xt, bank in ((et, px[par]), (NET + et, pdb)):
                    pk = WINLD["cnt"][par]
                    prev_use = mk["xz_mm"].get(("ld", par, pk - 1))
                    if prev_use:
                        b.wait("gpsimd", sPE, prev_use)
                    vld = b.dma("gpsimd", sWinG[par], wslot[:, par * D:(par + 1) * D],
                                dr[f"win_sl_{s}"][xt])
                    WINLD["cnt"][par] = pk + 1
                    b.wait("tensor", sWinG[par], vld)
                    if xt < NET:
                        prev_ev = mk["xz_ev"].get((j, et - 2)) or \
                            (mk["xz_ev"].get((j - 1, NET - 1)) if et < 2 else None)
                    else:
                        prev_ev = mk["zsil"].get((j, et - 1)) or \
                            (mk["zsil"].get((j - 1, NET - 1)) if et == 0 else None)
                    if prev_ev:
                        b.wait("tensor", sA, prev_ev)
                    for k in range(NKD):
                        last = (k == NKD - 1)
                        v = b.op("tensor", lambda e, o=bank[:],
                                 w=wslot[:, par * D + k * P:par * D + (k + 1) * P],
                                 r=xn_sl(k, t0, TC), st=(k == 0), sp=last:
                                 e.matmul(out=o, lhsT=w, rhs=r, start=st, stop=sp),
                                 sPE if last else None)
                    mk["xz_mm"][("ld", par, pk)] = v
                    mk["xz_mm"][("mm", j, xt)] = v

            def emit_A_evac_u(j, et):
                s, c, half = jobs[j]
                fwd = (s == "f")
                par = et % 2
                uo = u_ext[:, par * (TC + 16) + (3 if fwd else 0):
                           par * (TC + 16) + (3 if fwd else 0) + TC]
                b.wait("scalar", sPE, mk["xz_mm"][("mm", j, et)])
                prev_conv = mk["conv"].get((j, et - 2)) or \
                    (mk["conv"].get((j - 1, NET - 2 + et)) if et < 2 else None)
                if prev_conv:
                    b.wait("scalar", sV, prev_conv)
                if j > 0:
                    g8 = mk["gate"].get((j - 1, et))
                    if g8:
                        b.wait("scalar", sV, g8)
                mk["xz_ev"][(j, et)] = b.op("scalar", lambda e, o=uo,
                    pb=px[par]: e.activation(out=o, in_=pb[:], func=AF.Copy), sA)

            def emit_A_evac_z(j, et):
                b.wait("scalar", sPE, mk["xz_mm"][("mm", j, NET + et)])
                if j > 0:
                    g8 = mk["gate"].get((j - 1, et))
                    if g8:
                        b.wait("scalar", sV, g8)
                mk["zsil"][(j, et)] = b.op("scalar", lambda e,
                    o=z_sb[:, et * TC:(et + 1) * TC], pb=pdb:
                    e.activation(out=o, in_=pb[:], func=AF.Copy), sA)

            def emit_A_conv(j, et):
                s, c, half = jobs[j]
                fwd = (s == "f")
                par = et % 2
                b.wait("vector", sA, mk["xz_ev"][(j, et)])
                prev_sil = mk["usil"].get((j, et - 2)) or \
                    (mk["usil"].get((j - 1, NET - 2 + et)) if et < 2 else None)
                if prev_sil:
                    b.wait("vector", sA, prev_sil)
                ue0 = par * (TC + 16)
                tl = utail[:, et * 4:et * 4 + 3]
                if fwd:
                    v_h = b.op("vector", lambda e, o=u_ext[:, ue0:ue0 + 3], i=tl:
                               e.tensor_copy(out=o, in_=i), sV)
                else:
                    v_h = b.op("vector", lambda e, o=u_ext[:, ue0 + TC:ue0 + TC + 3],
                               i=tl: e.tensor_copy(out=o, in_=i), sV)
                ca = cacc[:, par * TC:(par + 1) * TC]
                wks = [half * 112 + 16 + et * 4 + k for k in range(4)]
                order = [0, 1, 2, 3] if fwd else [3, 2, 1, 0]
                b.wait("vector", sV, v_h)
                v_c = b.op("vector", lambda e, o=ca, i=u_ext[:, ue0:ue0 + TC],
                           w=cvec[:, wks[order[0]]:wks[order[0]] + 1]:
                           e.tensor_scalar(out=o, in0=i, scalar1=w, scalar2=None,
                                           op0=ALU.mult), sV)
                for jj in (1, 2, 3):
                    b.wait("vector", sV, v_c)
                    dst = ca if jj < 3 else u_sil[:, et * TC:(et + 1) * TC]
                    v_c = b.op("vector", lambda e, o=dst,
                               i=u_ext[:, ue0 + jj:ue0 + jj + TC],
                               w=cvec[:, wks[order[jj]]:wks[order[jj]] + 1], acc=ca:
                               e.scalar_tensor_tensor(out=o, in0=i, scalar=w,
                                                      in1=acc, op0=ALU.mult,
                                                      op1=ALU.add), sV)
                b.wait("vector", sV, v_c)
                if fwd:
                    mk["conv"][(j, et)] = b.op("vector", lambda e, o=tl,
                        i=u_ext[:, ue0 + TC:ue0 + TC + 3]:
                        e.tensor_copy(out=o, in_=i), sV)
                else:
                    mk["conv"][(j, et)] = b.op("vector", lambda e, o=tl,
                        i=u_ext[:, ue0:ue0 + 3]:
                        e.tensor_copy(out=o, in_=i), sV)

            def emit_silu_batch(j, ets, zets):
                s, c, half = jobs[j]
                if ets:
                    b.wait("scalar", sV, mk["conv"][(j, ets[-1])])
                if zets:
                    b.wait("scalar", sA, mk["zsil"][(j, zets[-1])])
                for et in ets:
                    usl = u_sil[:, et * TC:(et + 1) * TC]
                    mk["usil"][(j, et)] = b.op("scalar", lambda e, o=usl,
                        bc=cvec[:, half * 112 + et:half * 112 + et + 1]:
                        e.activation(out=o, in_=o, func=AF.Silu, bias=bc), sA)
                for et in zets:
                    zsl = z_sb[:, et * TC:(et + 1) * TC]
                    mk["usil"][(j, NET + et)] = b.op("scalar", lambda e, o=zsl:
                        e.activation(out=o, in_=o, func=AF.Silu), sA)


            def emit_dbc(j):
                s, c, half = jobs[j]
                b.wait("tensor", sA, mk["usil"][(j, NET - 1)])
                b.wait("tensor", sA, mk["zsil"][(j, NET - 1)])
                if j > 0:
                    b.wait("tensor", sA, mk["dbc_ev"][j - 1])
                for et in range(NET):
                    last = (et == NET - 1)
                    v = b.op("tensor", lambda e, o=pdb[0:96, :],
                             w=wx_sb[:, (half * NET + et) * 96:(half * NET + et + 1) * 96],
                             r=u_sil[:, et * TC:(et + 1) * TC],
                             st=(et == 0), sp=last:
                             e.matmul(out=o, lhsT=w, rhs=r, start=st, stop=sp),
                             sPE if last else None)
                b.wait("scalar", sPE, v)
                if j > 0:
                    b.wait("scalar", sBC2, mk["bc_dma"][j - 1])
                mk["dbc_ev"][j] = b.op("scalar", lambda e:
                                       e.activation(out=dbc_sb[0:96, 0:TC],
                                                    in_=pdb[0:96, :], func=AF.Copy), sA)
                b.wait("gpsimd", sA, mk["dbc_ev"][j])
                if j > 0:
                    b.wait("gpsimd", sV, mk["bc_users"][j - 1])
                vf1 = b.dma("gpsimd", sBC, bc_scr[0:1, :], dbc_sb[RK:RK + NS, 0:TC])
                vf2 = b.dma("gpsimd", sBC, bc_scr[1:2, :],
                            dbc_sb[RK + NS:RK + 2 * NS, 0:TC])
                b.wait("gpsimd", sBC, vf2)
                mk["b_dma"][j] = b.dma("gpsimd", sBC, B_bc[:],
                                       bc_scr[0:1, :]
                                       .unsqueeze(1).broadcast_to([1, P, NS * TC]))
                mk["bc_dma"][j] = b.dma("gpsimd", sBC2, C_bc[:],
                                        bc_scr[1:2, :]
                                        .unsqueeze(1).broadcast_to([1, P, NS * TC]))

            def emit_delta(j, et):
                s, c, half = jobs[j]
                par = et % 2
                b.wait("tensor", sA, mk["dbc_ev"][j])
                wl = mk.get("wdt_pend")
                if wl is not None and j >= 4:
                    b.wait("tensor", sBulkG, wl)
                    mk["wdt_pend"] = None
                prev = mk["minev"].get((j, et - 2)) or \
                    (mk["minev"].get((j - 1, NET - 2 + et)) if et < 2 else None)
                if prev:
                    b.wait("tensor", sV, prev)
                mk["delta_mm"][(j, et)] = b.op("tensor", lambda e, o=pd[par][:],
                    w=wdt_sb[:, et * P:(et + 1) * P],
                    r=dbc_sb[0:RK, 0:TC]:
                    e.matmul(out=o, lhsT=w, rhs=r, start=True, stop=True), sPE)

            def emit_prep_v(j, et):
                s, c, half = jobs[j]
                par = et % 2
                dsl = delta_sb[:, par * TC:(par + 1) * TC]
                b.wait("vector", sPE, mk["delta_mm"][(j, et)])
                prev_dA = mk["dA"].get((j, et - 2, NGR - 1)) or \
                    (mk["dA"].get((j - 1, NET - 2 + et, NGR - 1)) if et < 2 else None)
                if prev_dA:
                    b.wait("vector", sA, prev_dA)
                mk["minev"][(j, et)] = b.op("vector", lambda e, o=dsl, pb=pd[par],
                    bd=cvec[:, half * 112 + 80 + et:half * 112 + 81 + et]:
                    e.tensor_scalar(out=o, in0=pb[:], scalar1=bd, scalar2=25.0,
                                    op0=ALU.add, op1=ALU.min), sV)

            def emit_prep_a(j, et):
                par = et % 2
                dsl = delta_sb[:, par * TC:(par + 1) * TC]
                b.wait("scalar", sV, mk["minev"][(j, et)])
                v_se = b.op("scalar", lambda e, o=spe[:], i=dsl:
                            e.activation(out=o, in_=i, func=AF.Exp), sA)
                b.wait("scalar", sA, v_se)
                mk["lnsp"][(j, et)] = b.op("scalar", lambda e, o=dsl, i=spe[:]:
                                           e.activation(out=o, in_=i, func=AF.Ln,
                                                        bias=1.0), sA)

            def emit_B_et(j, et):
                s, c, half = jobs[j]
                fwd = (s == "f")
                par = et % 2
                dsl = delta_sb[:, par * TC:(par + 1) * TC]
                dusl = du_sb[:, par * TC:(par + 1) * TC]
                b.wait("vector", sA, mk["lnsp"][(j, et)])
                b.wait("scalar", sA, mk["lnsp"][(j, et)])
                v_du = b.op("vector", lambda e, o=dusl, a1=dsl,
                            a2=u_sil[:, et * TC:(et + 1) * TC]:
                            e.tensor_mul(out=o, in0=a1, in1=a2), sV)
                b.wait("vector", sV, v_du)
                for g in range(NGR):
                    gg = GG["n"]
                    GG["n"] += 1
                    gp = gg % 2
                    if gg >= 2:
                        b.wait("scalar", sV, mk["scan"][gg - 2])
                    doff = 2 if fwd else 0
                    for jj in range(GRP):
                        n = g * GRP + jj
                        last = (jj == GRP - 1)
                        sl0 = (gp * GRP + jj) * TCP + doff
                        v = b.op("scalar", lambda e,
                                 o=dA_sl[:, sl0:sl0 + TC],
                                 i=dsl, sc=A_sb[:, half * NET * NS + et * NS + n:
                                                half * NET * NS + et * NS + n + 1]:
                                 e.activation(out=o, in_=i, func=AF.Exp, scale=sc),
                                 sA if last else None)
                    mk["dA"][(j, et, g)] = v
                    b.wait("vector", sA, v)
                    if gg >= 2:
                        b.wait("vector", sPE, mk["yacc"][gg - 2])
                    if et == 0 and g == 0:
                        b.wait("vector", sBC, mk["b_dma"][j])
                    base = gp * GRP * TCP
                    v_db = b.op("vector", lambda e,
                                o=dBu_sb[:].rearrange("p (s t) -> p s t", s=GRP)
                                [:, :, doff:doff + TC],
                                a1=dusl.unsqueeze(1).broadcast_to([P, GRP, TC]),
                                a2=B_bc[:, g * GRP * TC:(g + 1) * GRP * TC]
                                .rearrange("p (s t) -> p s t", s=GRP):
                                e.tensor_mul(out=o, in0=a1, in1=a2), sV)
                    b.wait("vector", sV, v_db)
                    for jj in range(GRP):
                        n = g * GRP + jj
                        sl0 = (gp * GRP + jj) * TCP + doff
                        db0 = jj * TCP + doff
                        ic = hcar[:, et * NS + n:et * NS + n + 1]
                        if fwd:
                            v_sc = b.op("vector", lambda e, o=h_sl[:, sl0:sl0 + TC],
                                        d0=dA_sl[:, sl0:sl0 + TC],
                                        d1=dBu_sb[:, db0:db0 + TC], i=ic:
                                        e.tensor_tensor_scan(out=o, data0=d0, data1=d1,
                                        initial=i, op0=ALU.mult, op1=ALU.add),
                                        sV if jj == GRP - 1 else None)
                        else:
                            v_sc = b.op("vector", lambda e, o=h_sl[:, sl0:sl0 + TC],
                                        d0=dA_sl[:, sl0:sl0 + TC],
                                        d1=dBu_sb[:, db0:db0 + TC], i=ic:
                                        e.tensor_tensor_scan(out=o[:, ::-1],
                                        data0=d0[:, ::-1], data1=d1[:, ::-1],
                                        initial=i, op0=ALU.mult, op1=ALU.add),
                                        sV if jj == GRP - 1 else None)
                    b.wait("vector", sV, v_sc)
                    off = (doff + TC - 1) if fwd else doff
                    v_tl = b.op("vector", lambda e,
                                o=hcar[:, et * NS + g * GRP:et * NS + (g + 1) * GRP],
                                i=h_sl[:, base + off:base + off
                                       + (GRP - 1) * TCP + 1:TCP]:
                                e.tensor_copy(out=o, in_=i), sV)
                    b.wait("vector", sV, v_tl)
                    if et == 0 and g == 0:
                        b.wait("vector", sBC2, mk["bc_dma"][j])
                    v = b.op("vector", lambda e,
                             o=h_sl[:, base:base + GRP * TCP]
                             .rearrange("p (s t) -> p s t", s=GRP)[:, :, doff:doff + TC],
                             a2=C_bc[:, g * GRP * TC:(g + 1) * GRP * TC]
                             .rearrange("p (s t) -> p s t", s=GRP):
                             e.tensor_mul(out=o, in0=o, in1=a2), sV)
                    mk["scan"][gg] = v
                    b.wait("tensor", sV, mk["scan"][gg])
                    if g == 0:
                        prev_gate = mk["gate"].get((j, et - 2)) or \
                            (mk["gate"].get((j - 1, NET - 2 + et)) if et < 2 else None)
                        if prev_gate:
                            b.wait("tensor", sV, prev_gate)
                    for jj in range(GRP):
                        n = g * GRP + jj
                        last = (jj == GRP - 1)
                        sl0 = (gp * GRP + jj) * TCP + doff
                        v = b.op("tensor", lambda e, o=py[par][:],
                                 r=h_sl[:, sl0:sl0 + TC],
                                 st=(n == 0), sp=(n == NS - 1):
                                 e.matmul(out=o, lhsT=ident[:], rhs=r, start=st,
                                          stop=sp), sPE if last else None)
                    mk["yacc"][gg] = v
                    if et == 0 and mk.get("cpend") is not None and g < 3:
                        emit_C_dt(mk["cpend"], 2 * g + 2)
                        emit_C_dt(mk["cpend"], 2 * g + 3)
                        if g == 2:
                            mk["cpend"] = None
                    if g == 0 and et + 1 < NET:
                        emit_prep_v(j, et + 1)
                    if g == 0 and et == 1 and j >= 4:
                        pump_h1(1)
                    if g == 1:
                        if j < 7 and et >= 1:
                            emit_A_evac_u(j + 1, et - 1)
                        pump_quant(1)
                        if j >= 4 and not (et == 0
                                           and mk.get("cpend") is not None):
                            pump_h1(1)
                    if g == 2:
                        if j < 7 and et >= 1:
                            emit_A_conv(j + 1, et - 1)
                        if et + 1 < NET:
                            emit_prep_a(j, et + 1)
                    if g == 3:
                        if j < 7 and et >= 1:
                            emit_A_evac_z(j + 1, et - 1)
                        if j >= 4:
                            pump_h1(1)
                        if j < 7 and et == NET - 1:
                            emit_silu_batch(j + 1, list(range(NET - 1)),
                                            list(range(NET - 1)))
                b.wait("vector", sPE, mk["yacc"][GG["n"] - 1])
                v_g1 = b.op("vector", lambda e, o=ytmp[:],
                            i=u_sil[:, et * TC:(et + 1) * TC],
                            dp=cvec[:, half * 112 + 96 + et:half * 112 + 97 + et],
                            pb=py[par]:
                            e.scalar_tensor_tensor(out=o, in0=i, scalar=dp, in1=pb[:],
                                                   op0=ALU.mult, op1=ALU.add), sV)
                b.wait("vector", sV, v_g1)
                mk["gate"][(j, et)] = b.op("vector", lambda e,
                    o=Y_sb[:, et * TC:(et + 1) * TC], a1=ytmp[:],
                    a2=z_sb[:, et * TC:(et + 1) * TC]:
                    e.tensor_mul(out=o, in0=a1, in1=a2), sV)
                mk["bc_users"][j] = mk["scan"][GG["n"] - 1]


            def emit_C_dt(j, dt):
                s, c, half = jobs[j]
                fwd = (s == "f")
                t0 = c * TC
                row0 = 0 if fwd else D
                if dt == 0:
                    b.wait("tensor", sV, mk["gate"][(j, NET - 1)])
                    if j > 0:
                        b.wait("tensor", sBulkG, mk["wout_ld"])
                pv = mk.get("pw_ev")
                if pv:
                    b.wait("tensor", sA, pv)
                for et in range(NET):
                    last = (et == NET - 1)
                    v = b.op("tensor", lambda e, o=pw[:],
                             w=wout_sb[:, et * D + dt * P:et * D + (dt + 1) * P],
                             r=Y_sb[:, et * TC:(et + 1) * TC],
                             st=(et == 0), sp=last:
                             e.matmul(out=o, lhsT=w, rhs=r, start=st, stop=sp),
                             sPE if last else None)
                b.wait("scalar", sPE, v)
                mk["wout_last"][j] = v
                if dt >= 2:
                    b.wait("scalar", sCat[dt % 2], mk["cat_dma"][dt - 2])
                elif j > 0:
                    b.wait("scalar", sCat[dt % 2], mk["cat_dma"][NKD - 2 + dt])
                mk["wout_ev"][dt] = b.op("scalar", lambda e,
                    o=cstage[:, (dt % 2) * TC:(dt % 2 + 1) * TC], pb=pw:
                    e.activation(out=o, in_=pb[:], func=AF.Copy), sA)
                mk["pw_ev"] = mk["wout_ev"][dt]
                b.wait("gpsimd", sA, mk["wout_ev"][dt])
                mk["cat_dma"][dt] = b.dma("gpsimd", sCat[dt % 2],
                    cat_scr[row0 + dt * P:row0 + (dt + 1) * P, t0:t0 + TC],
                    cstage[:, (dt % 2) * TC:(dt % 2 + 1) * TC])

            def emit_C(j):
                for dt in range(NKD):
                    emit_C_dt(j, dt)

            b.dma("gpsimd", sBulkG, b1c12[:], dr["b1col"])
            v_neg, vbh, vbg = emit_dir_consts("f", 0)
            emit_wdt_load("f", 0)
            mk["wout_ld"] = emit_wout_load("f", 0)
            vbg = sBulkG.n
            for eng in ENGS:
                b.wait(eng, sBulkH, vbh)
                b.wait(eng, sBulkG, vbg)
                b.wait(eng, sV, v_neg)
            v_ms = b.op("vector", lambda e: e.memset(hcar[:], 0.0), sV)
            v_ms = b.op("vector", lambda e: e.memset(utail[:], 0.0), sV)
            v_ms = emit_scan_memsets(True)
            b.wait("scalar", sV, v_ms)
            b.wait("tensor", sV, v_ms)

            for et in range(NET):
                emit_A_mm(0, et)
                emit_A_evac_u(0, et)
                emit_A_evac_z(0, et)
                emit_A_conv(0, et)
            emit_silu_batch(0, list(range(NET)), list(range(NET)))
            emit_dbc(0)
            emit_delta(0, 0)
            emit_delta(0, 1)
            emit_prep_v(0, 0)
            emit_prep_a(0, 0)

            for j in range(8):
                b.wait("vector", sV, sV.n)
                b.wait("scalar", sA, sA.n)
                if j == 3:
                    v_neg, vbh, vbg = emit_dir_consts("b", 1)
                    for eng in ("vector", "scalar", "tensor", "sync", "gpsimd"):
                        b.wait(eng, sBulkH, vbh)
                        b.wait(eng, sBulkG, vbg)
                        b.wait(eng, sV, v_neg)
                    v_ms = b.op("vector", lambda e: e.memset(utail[:], 0.0), sV)
                    b.wait("scalar", sV, v_ms)
                    mk["bwd_ms"] = None
                if j >= 5:
                    h1_evac()
                if j >= 1:
                    emit_C_dt(j - 1, 0)
                    emit_C_dt(j - 1, 1)
                    mk["cpend"] = j - 1
                if j == 4:
                    v_ms = b.op("vector", lambda e: e.memset(hcar[:], 0.0), sV)
                    emit_scan_memsets(False)
                for et in range(NET):
                    emit_B_et(j, et)
                    if j == 4 and et == 0:
                        mk["wout_ld"] = emit_wout_load("b", mk["wout_last"][3])
                    if et + 2 < NET:
                        emit_delta(j, et + 2)
                    if j == 3 and et == 14:
                        mk["wdt_pend"] = emit_wdt_load("b",
                                                       mk["delta_mm"][(3, 15)])
                    if j < 7:
                        emit_A_mm(j + 1, et)
                if j < 7:
                    emit_A_evac_u(j + 1, NET - 1)
                    emit_A_evac_z(j + 1, NET - 1)
                    emit_A_conv(j + 1, NET - 1)
                    emit_silu_batch(j + 1, [NET - 1], [NET - 1])
                    emit_dbc(j + 1)
                    emit_delta(j + 1, 0)
                    emit_delta(j + 1, 1)
                    emit_prep_v(j + 1, 0)
                    emit_prep_a(j + 1, 0)
            pump_quant(4)
            pump_h1(1)
            emit_C(7)
            b.barrier(bar, [("sync", sBulkH), ("sync", sX[0]), ("sync", sX[1]), ("sync", sPreS[0]), ("sync", sPreS[1]), ("sync", sWin[0]), ("sync", sWin[1]), ("sync", sW2s[0]), ("sync", sW2s[1]), ("sync", sCatL[0]), ("sync", sCatL[1]), ("sync", sW1f[0]), ("sync", sW1f[1]), ("sync", sH1p[0]), ("sync", sH1p[1]), ("gpsimd", sBulkG), ("gpsimd", sGam[0]), ("gpsimd", sGam[1]), ("gpsimd", sPreL[0]), ("gpsimd", sPreL[1]), ("gpsimd", sBC), ("gpsimd", sBC2), ("gpsimd", sQl), ("gpsimd", sQs), ("gpsimd", sCatG[0]), ("gpsimd", sCatG[1]), ("gpsimd", sXG[0]), ("gpsimd", sXG[1]), ("gpsimd", sWinG[0]), ("gpsimd", sWinG[1]), ("gpsimd", sCat[0]), ("gpsimd", sCat[1])])
            b.flush()
        with ExitStack() as p34:
            b2bc = p34.enter_context(nc.sbuf_tensor("b2bc", [P, D], F32))
            bpbc = p34.enter_context(nc.sbuf_tensor("bpbc", [P, D], F32))
            gstage = p34.enter_context(nc.sbuf_tensor("gstage", [P, 2 * TC], BF16))
            w2q_sb = p34.enter_context(nc.sbuf_tensor("w2q_sb", [P, NFT * D], BF16))
            h1sl = p34.enter_context(nc.sbuf_tensor("h1sl", [P, 8 * TC], BF16))
            g_sb = p34.enter_context(nc.sbuf_tensor("g_sb", [P, 4 * D], BF16))
            wproj_sb = p34.enter_context(nc.sbuf_tensor("wproj_sb", [P, NET * D], BF16))
            catsl = p34.enter_context(nc.sbuf_tensor("catsl", [P, 2 * NET * P], BF16))
            xsl = p34.enter_context(nc.sbuf_tensor("xsl", [P, 2 * D], F32))
            sstage = p34.enter_context(nc.sbuf_tensor("sstage", [P, 2 * TC], BF16))

            b.dma("gpsimd", sBulkG, b2bc[:],
                  dr["b2row"].unsqueeze(1).broadcast_to([1, P, D]))
            b.dma("gpsimd", sBulkG, bpbc[:],
                  dr["bprojrow"].unsqueeze(1).broadcast_to([1, P, D]))
            for jj in range(NKD):
                b.dma("gpsimd", sBulkG, wproj_sb[:, jj * 2 * D:(jj + 1) * 2 * D],
                      dr["wproj_sl"][2 * jj:2 * (jj + 1)].transpose([1, 0, 2]))
            v_wp = sBulkG.n
            w2half = []
            for hl in range(2):
                for fp in range(NFT // 4):
                    fpp = hl * (NFT // 4) + fp
                    v = b.dma("sync", sW2s[hl],
                              w2q_sb[:, 2 * fpp * D:(2 * fpp + 2) * D],
                              w2q_scr[2 * fpp:2 * fpp + 2].transpose([1, 0, 2]))
                w2half.append(v)

            HB = TC // 2
            h1mm = {}
            gsil = {}
            pj_mm = {}
            pj_out = {}
            pj_st = {}
            PASSES = [([pdb, py[0], py[1], pw], [(0, 0), (0, 1), (1, 0), (1, 1)],
                       sH1G),
                      ([px[0], px[1], pd[0], pd[1]],
                       [(2, 0), (2, 1), (3, 0), (3, 1)], sH1G2)]
            NG4 = NFT // 4
            for c in range(NCH):
                t0 = c * TC
                for ps, (banks, tiles, sems) in enumerate(PASSES):
                    if ps == 0:
                        for i4, (tt, hf) in enumerate(tiles):
                            pv = gsil.get((c - 1, tt, hf))
                            if pv:
                                b.wait("tensor", sA, pv)
                    else:
                        if c > 0:
                            b.wait("tensor", sV, pj_out[4 * (c - 1) + 2])
                            b.wait("tensor", sV, pj_out[4 * (c - 1) + 3])
                    for g4 in range(NG4):
                        par = g4 % 2
                        prev = h1mm.get((c, ps, g4 - 2)) or \
                               (h1mm.get((c - 1, ps, NG4 - 2 + g4))
                                if g4 < 2 and c > 0 else None)
                        if prev:
                            b.wait("gpsimd", sPE, prev)
                        vh = b.dma("gpsimd", sems[par],
                                   h1sl[:, (ps * 2 + par) * 4 * HB:
                                        (ps * 2 + par + 1) * 4 * HB],
                                   h1_scr[4 * g4:4 * (g4 + 1),
                                          :, t0 + ps * 2 * P:t0 + ps * 2 * P + 2 * P]
                                   .transpose([1, 0, 2]))
                        b.wait("tensor", sems[par], vh)
                        if c == 0 and ps == 0:
                            b.wait("tensor", sW2s[g4 // (NG4 // 2)],
                                   w2half[g4 // (NG4 // 2)])
                        for sub in range(4):
                            ft = 4 * g4 + sub
                            for i4, (tt, hf) in enumerate(tiles):
                                last = (i4 == 3 and sub == 3)
                                v = b.op("tensor", lambda e, o=banks[i4][:],
                                         w=h1sl[:, (ps * 2 + par) * 4 * HB
                                                + sub * HB + (tt - 2 * ps) * P:
                                                (ps * 2 + par) * 4 * HB
                                                + sub * HB + (tt - 2 * ps + 1) * P],
                                         r=w2q_sb[:, ft * D + hf * TC:
                                                  ft * D + (hf + 1) * TC],
                                         st=(ft == 0), sp=(ft == NFT - 1):
                                         e.matmul(out=o, lhsT=w, rhs=r, start=st,
                                                  stop=sp), sPE if last else None)
                        h1mm[(c, ps, g4)] = v
                    b.wait("vector", sPE, v)
                    if c == 0 and ps == 0:
                        b.wait("vector", sBulkG, v_wp)
                    for i4, (tt, hf) in enumerate(tiles):
                        gsl = gstage[:, hf * TC:(hf + 1) * TC]
                        ps_prev = gsil.get((c, tt - 1, hf)) if i4 >= 2 else \
                            (gsil.get((c - 1, 3, hf)) if ps == 0 and c > 0 else
                             gsil.get((c, 1, hf)) if ps == 1 else None)
                        if ps_prev:
                            b.wait("vector", sA, ps_prev)
                        vv = b.op("vector", lambda e, o=gsl, pb=banks[i4],
                                  bb=b2bc[:, hf * TC:(hf + 1) * TC]:
                                  e.tensor_add(out=o, in0=pb[:], in1=bb), sV)
                        b.wait("scalar", sV, vv)
                        if c > 0:
                            b.wait("scalar", sV, pj_out[4 * (c - 1) + tt])
                        gsil[(c, tt, hf)] = b.op("scalar", lambda e,
                            o=g_sb[:, tt * D + hf * TC:tt * D + (hf + 1) * TC],
                            i=gsl:
                            e.activation(out=o, in_=i, func=AF.Silu), sA)
                b.wait("tensor", sBulkG, v_wp)
                for tl in range(4):
                    ti = 4 * c + tl
                    par = ti % 2
                    prev = pj_mm.get(ti - 2)
                    if prev:
                        b.wait("gpsimd", sPE, prev)
                    if ti >= 2:
                        b.wait("gpsimd", sV, pj_out[ti - 2])
                        b.wait("gpsimd", sSt[par], pj_st[ti - 2])
                    vc = b.dma("gpsimd", sCatG[par],
                               catsl[:, par * NET * P:(par + 1) * NET * P],
                               cat_scr[:, ti * P:(ti + 1) * P]
                               .rearrange("(k p) t -> p k t", p=P))
                    vx = b.dma("gpsimd", sXG[par], xsl[:, par * D:(par + 1) * D],
                               dr["x"][ti * P:(ti + 1) * P, :])
                    b.wait("tensor", sCatG[par], vc)
                    if ti >= 2:
                        b.wait("tensor", sV, pj_out[ti - 2])
                    pjbanks = [px[0], px[1]] if tl % 2 == 0 else [pd[0], pd[1]]
                    pjtt = 2 if tl % 2 == 0 else 3
                    for hf in range(2):
                        bank = pjbanks[hf]
                        b.wait("tensor", sA, gsil[(c, pjtt, hf)])
                        for k in range(16):
                            last = (k == 15)
                            v = b.op("tensor", lambda e, o=bank[:],
                                     w=catsl[:, par * NET * P + k * P:
                                             par * NET * P + (k + 1) * P],
                                     r=wproj_sb[:, k * D + hf * TC:
                                                k * D + (hf + 1) * TC],
                                     st=(k == 0), sp=last:
                                     e.matmul(out=o, lhsT=w, rhs=r, start=st,
                                              stop=sp),
                                     sPE if last else None)
                    pj_mm[ti] = v
                    b.wait("vector", sPE, v)
                    b.wait("vector", sXG[par], vx)
                    b.wait("vector", sA, gsil[(c, tl, 1)])
                    if ti >= 2:
                        b.wait("vector", sSt[par], pj_st[ti - 2])
                    for hf in range(2):
                        bank = pjbanks[hf]
                        ssl = sstage[:, hf * TC:(hf + 1) * TC]
                        v_a = b.op("vector", lambda e, o=ssl, pb=bank,
                                   bb=bpbc[:, hf * TC:(hf + 1) * TC]:
                                   e.tensor_add(out=o, in0=pb[:], in1=bb), sV)
                        b.wait("vector", sV, v_a)
                        v_m = b.op("vector", lambda e, o=ssl, a1=ssl,
                                   a2=g_sb[:, tl * D + hf * TC:
                                           tl * D + (hf + 1) * TC]:
                                   e.tensor_mul(out=o, in0=a1, in1=a2), sV)
                        b.wait("vector", sV, v_m)
                        v2 = b.op("vector", lambda e,
                                  o=xsl[:, par * D + hf * TC:par * D + (hf + 1) * TC],
                                  a1=ssl, a2=xsl[:, par * D + hf * TC:
                                                 par * D + (hf + 1) * TC]:
                                  e.tensor_add(out=o, in0=a1, in1=a2), sV)
                    pj_out[ti] = v2
                    b.wait("sync", sV, v2)
                    pj_st[ti] = b.dma("sync", sSt[par], y_out[ti * P:(ti + 1) * P, :],
                                      xsl[:, par * D:(par + 1) * D])
            b.wait("sync", sSt[0], sSt[0].n)
            b.wait("sync", sSt[1], sSt[1].n)
            b.flush()
    return nc


_PROG = None


def _get_prog():
    global _PROG
    if _PROG is None:
        _PROG = build_program()
    return _PROG


def kernel(**inputs):
    f32 = np.float32

    def cc(a):
        return np.ascontiguousarray(a, dtype=f32)

    ins = {k[len("L0_attn_"):]: v for k, v in inputs.items() if k.startswith("L0_attn_")}
    ins.update({k[len("L0_ffn_"):]: v for k, v in inputs.items() if k.startswith("L0_ffn_")})

    base = {}
    for s in "fb":
        win = np.asarray(ins[f"win_{s}"], dtype=f32)
        base[f"win_sl_{s}"] = cc(win.reshape(NXT, P, NKD, P).transpose(0, 3, 2, 1)
                                 .reshape(NXT, P, D))
        base[f"wout_sl_{s}"] = cc(np.asarray(ins[f"wout_{s}"], dtype=f32).T
                                  .reshape(NET, P, D))
        base[f"wx_sl_{s}"] = cc(np.asarray(ins[f"wx_{s}"], dtype=f32).T
                                .reshape(NET, P, 96))
        base[f"wdt_{s}"] = cc(np.asarray(ins[f"wdt_{s}"], dtype=f32).T)
        bconv = np.asarray(ins[f"bconv_{s}"], dtype=f32).reshape(NET, P).T
        wconv = np.asarray(ins[f"wconv_{s}"], dtype=f32)[:, 0, :] \
            .reshape(NET, P, 4).transpose(1, 0, 2).reshape(P, NET * 4)
        bdt = np.asarray(ins[f"bdt_{s}"], dtype=f32).reshape(NET, P).T
        Dp = np.asarray(ins[f"Dp_{s}"], dtype=f32).reshape(NET, P).T
        base[f"cvec_{s}"] = cc(np.concatenate([bconv, wconv, bdt, Dp], axis=1))
        base[f"alog_{s}"] = cc(np.asarray(ins[f"Alog_{s}"], dtype=f32)
                               .reshape(NET, P, NS).transpose(1, 0, 2).reshape(P, NET * NS))
    base["w1t_sl"] = cc(np.asarray(ins["W1"], dtype=f32).T.reshape(NKD, P, FF))
    base["w2t_sl"] = cc(np.asarray(ins["W2"], dtype=f32).T.reshape(NFT, P, D))
    base["wproj_sl"] = cc(np.asarray(ins["wproj"], dtype=f32).T.reshape(NET, P, D))
    base["b1col"] = cc(np.asarray(ins["b1"], dtype=f32).reshape(NFT, P).T)
    base["b2row"] = cc(np.asarray(ins["b2"], dtype=f32).reshape(1, D))
    base["bprojrow"] = cc(np.asarray(ins["bproj"], dtype=f32).reshape(1, D))
    base["identity"] = np.eye(P, dtype=f32)

    x_all = np.asarray(ins["x"], dtype=f32)
    n_cores = x_all.shape[0]
    in_maps = []
    for c in range(n_cores):
        m = dict(base)
        m["x"] = cc(x_all[c])
        in_maps.append(m)

    nc = _get_prog()
    res = run_bass_kernel_spmd(nc, in_maps, core_ids=list(range(n_cores)))
    out = np.stack([np.asarray(r["y"], dtype=np.float32) for r in res.results], axis=0)
    return out


# version 19
# speedup vs baseline: 1.0277x; 1.0277x over previous
import numpy as np
import ml_dtypes
from contextlib import ExitStack
import concourse.bass as bass
import concourse.mybir as mybir
from concourse.bass_utils import run_bass_kernel_spmd

F32 = mybir.dt.float32
BF16 = mybir.dt.bfloat16
AF = mybir.ActivationFunctionType
ALU = mybir.AluOpType
AX = mybir.AxisListType.X

P = 128
T, D, E, NS, RK, FF = 2048, 1024, 2048, 16, 64, 4096
TC = 512
NCH = T // TC
NET = E // P
NXT = 2 * E // P
NKD = D // P
NTT = T // P
NFT = FF // P
GRP = 4
NGR = NS // GRP
EPSQ = 1e-5
ENGS = ("sync", "gpsimd", "vector", "scalar", "tensor")


class Sem:
    def __init__(self, nc, ctx, name):
        self.h = ctx.enter_context(nc.semaphore(name))
        self.n = 0


class Bld:
    def __init__(self, nc):
        self.nc = nc
        self.st = {e: [] for e in ENGS}

    def em(self, eng, fn):
        self.st[eng].append(fn)

    def wait(self, eng, sem, v):
        if v and v > 0:
            self.em(eng, lambda e, h=sem.h, vv=v: e.wait_ge(h, vv))

    def dma(self, eng, sem, out, in_):
        sem.n += 16
        self.em(eng, lambda e, o=out, i=in_, h=sem.h:
                e.dma_start(out=o, in_=i).then_inc(h, 16))
        return sem.n

    def op(self, eng, fn, sem=None):
        if sem is None:
            self.em(eng, fn)
            return None
        sem.n += 1
        self.em(eng, lambda e, f=fn, h=sem.h: f(e).then_inc(h, 1))
        return sem.n

    def barrier(self, bar, dma_sems=()):
        for eng, sem in dma_sems:
            self.wait(eng, sem, sem.n)
        for eng in ENGS:
            self.em(eng, lambda e, h=bar.h: e.sem_inc(h, 1))
        bar.n += 5
        tgt = bar.n
        for eng in ENGS:
            self.wait(eng, bar, tgt)

    def flush(self):
        st = self.st
        self.st = {e: [] for e in ENGS}
        with self.nc.Block() as blk:
            for eng in ENGS:
                def runner(e, fns=tuple(st[eng])):
                    for f in fns:
                        f(e)
                getattr(blk, eng)(runner)


def build_program():
    nc = bass.Bass("TRN2", target_bir_lowering=False)
    dr = {}

    def din(name, shape):
        dr[name] = nc.dram_tensor(name, list(shape), F32, kind="ExternalInput").ap()

    din("x", (T, D))
    for s in "fb":
        din(f"win_sl_{s}", (NXT, P, D))
        din(f"wout_sl_{s}", (NET, P, D))
        din(f"wx_sl_{s}", (NET, P, 96))
        din(f"wdt_{s}", (RK, E))
        din(f"cvec_{s}", (P, 112))
        din(f"alog_{s}", (P, NET * NS))
    din("w1t_sl", (NKD, P, FF))
    din("w2t_sl", (NFT, P, D))
    din("wproj_sl", (NET, P, D))
    din("b1col", (P, NFT))
    din("b2row", (1, D))
    din("bprojrow", (1, D))
    din("identity", (P, P))

    cat_scr = nc.dram_tensor("cat_scr", [2 * D, T], BF16, kind="Internal").ap()
    bc_scr = nc.dram_tensor("bc_scr", [2, NS * TC], BF16, kind="Internal").ap()
    w1q_scr = nc.dram_tensor("w1q_scr", [NKD, P, FF], BF16, kind="Internal").ap()
    w2q_scr = nc.dram_tensor("w2q_scr", [NFT, P, D], BF16, kind="Internal").ap()
    h1_scr = nc.dram_tensor("h1_scr", [NFT, P, T], BF16, kind="Internal").ap()
    y_out = nc.dram_tensor("y", [T, D], F32, kind="ExternalOutput").ap()

    with ExitStack() as ctx:
        sPE = Sem(nc, ctx, "sPE")
        sV = Sem(nc, ctx, "sV")
        sA = Sem(nc, ctx, "sA")
        bar = Sem(nc, ctx, "bar")
        sBulkH = Sem(nc, ctx, "sBulkH")
        sBulkG = Sem(nc, ctx, "sBulkG")
        sX = [Sem(nc, ctx, f"sX{i}") for i in range(2)]
        sGam = [Sem(nc, ctx, f"sGam{i}") for i in range(2)]
        sPreL = [Sem(nc, ctx, f"sPreL{i}") for i in range(2)]
        sPreS = [Sem(nc, ctx, f"sPreS{i}") for i in range(2)]
        sWin = [Sem(nc, ctx, f"sWin{i}") for i in range(2)]
        sBC = Sem(nc, ctx, "sBC")
        sQl = Sem(nc, ctx, "sQl")
        sQs = Sem(nc, ctx, "sQs")
        sCat = [Sem(nc, ctx, f"sCat{i}") for i in range(2)]
        sW2s = [Sem(nc, ctx, f"sW2s{i}") for i in range(2)]
        sCatL = [Sem(nc, ctx, f"sCatL{i}") for i in range(2)]
        sSt = [Sem(nc, ctx, f"sSt{i}") for i in range(2)]
        sGp = Sem(nc, ctx, "sGp")
        sBC2 = Sem(nc, ctx, "sBC2")
        sCatG = [Sem(nc, ctx, f"sCatG{i}") for i in range(2)]
        sXG = [Sem(nc, ctx, f"sXG{i}") for i in range(2)]
        sWinG = [Sem(nc, ctx, f"sWinG{i}") for i in range(2)]
        sW1f = [Sem(nc, ctx, f"sW1f{i}") for i in range(2)]
        sH1p = [Sem(nc, ctx, f"sH1p{i}") for i in range(2)]
        sH1G = [Sem(nc, ctx, f"sH1G{i}") for i in range(2)]
        sH1G2 = [Sem(nc, ctx, f"sH1G2{i}") for i in range(2)]

        xn = ctx.enter_context(nc.sbuf_tensor("xn_cm", [P, NKD * T], BF16))
        ident = ctx.enter_context(nc.sbuf_tensor("ident", [P, P], BF16))
        gfin = ctx.enter_context(nc.sbuf_tensor("gfin", [P, 16], F32))

        px = [ctx.enter_context(nc.psum_tensor(f"px{i}", [P, TC], F32)) for i in range(2)]
        pdb = ctx.enter_context(nc.psum_tensor("pdb", [P, TC], F32))
        pd = [ctx.enter_context(nc.psum_tensor(f"pd{i}", [P, TC], F32)) for i in range(2)]
        py = [ctx.enter_context(nc.psum_tensor(f"py{i}", [P, TC], F32)) for i in range(2)]
        pw = ctx.enter_context(nc.psum_tensor("pw", [P, TC], F32))
        pall = [px[0], px[1], pdb, pd[0], pd[1], py[0], py[1], pw]

        b = Bld(nc)

        def xn_sl(k, t0, tl):
            return xn[:, k * T + t0: k * T + t0 + tl]

        with ExitStack() as p0:
            xst = p0.enter_context(nc.sbuf_tensor("xst", [P, 2 * D], F32))
            sqst = p0.enter_context(nc.sbuf_tensor("sqst", [P, D], F32))
            stats = p0.enter_context(nc.sbuf_tensor("stats", [P, 8 * NTT], F32))
            xnt = p0.enter_context(nc.sbuf_tensor("xnt", [P, 2 * D], F32))
            wstg = p0.enter_context(nc.sbuf_tensor("wstg", [P, 2 * FF], F32))
            gacc = p0.enter_context(nc.sbuf_tensor("gacc", [P, 48], F32))
            onesf = p0.enter_context(nc.sbuf_tensor("onesf", [P, P], F32))

            identf = p0.enter_context(nc.sbuf_tensor("identf", [P, P], F32))
            b.dma("gpsimd", sBulkG, ident[:], dr["identity"])
            v_id = b.dma("sync", sBulkH, identf[:], dr["identity"])
            b.op("vector", lambda e: e.memset(onesf[:], 1.0))
            b.op("vector", lambda e: e.memset(gfin[:, 15:16], 1e-5), sV)

            st1 = []
            for i in range(NTT):
                if i >= 2:
                    b.wait("sync", sV, st1[i - 2][0])
                    b.wait("sync", sA, st1[i - 2][1])
                sl = xst[:, (i % 2) * D:(i % 2 + 1) * D]
                vld = b.dma("sync", sX[i % 2], sl, dr["x"][i * P:(i + 1) * P, :])
                b.wait("vector", sX[i % 2], vld)
                vr = b.op("vector", lambda e, o=stats[:, i:i + 1], s=sl:
                          e.reduce_sum(out=o, in_=s, axis=AX), sV)
                b.wait("scalar", sX[i % 2], vld)
                if i >= 1:
                    b.wait("scalar", sA, st1[i - 1][1])
                va = b.op("scalar", lambda e, o=sqst[:], s=sl,
                          a=stats[:, NTT + i:NTT + i + 1]:
                          e.activation(out=o, in_=s, func=AF.Square, accum_out=a), sA)
                st1.append((vr, va))

            sum_c = stats[:, 0:NTT]
            m2_c = stats[:, NTT:2 * NTT]
            mean_c = stats[:, 2 * NTT:3 * NTT]
            q_c = stats[:, 3 * NTT:4 * NTT]
            var_c = stats[:, 4 * NTT:5 * NTT]
            std_c = stats[:, 5 * NTT:6 * NTT]
            rstd_c = stats[:, 6 * NTT:7 * NTT]
            nmr_c = stats[:, 7 * NTT:8 * NTT]
            b.wait("vector", sA, sA.n)
            b.wait("vector", sV, st1[-1][0])
            v_mean = b.op("vector", lambda e: e.tensor_scalar(out=mean_c, in0=sum_c,
                          scalar1=1.0 / D, scalar2=None, op0=ALU.mult), sV)
            b.wait("vector", sV, v_mean)
            v_q = b.op("vector", lambda e: e.tensor_mul(out=q_c, in0=mean_c,
                       in1=mean_c), sV)
            b.wait("vector", sV, v_q)
            b.op("vector", lambda e: e.scalar_tensor_tensor(out=var_c, in0=m2_c,
                 scalar=1.0 / D, in1=q_c, op0=ALU.mult, op1=ALU.subtract), sV)
            b.wait("scalar", sV, sV.n)
            va_std = b.op("scalar", lambda e: e.activation(out=std_c, in_=var_c,
                          func=AF.Sqrt, bias=gfin[:, 15:16]), sA)
            b.wait("vector", sA, va_std)
            v_rec = b.op("vector", lambda e: e.reciprocal(out=rstd_c, in_=std_c), sV)
            b.op("vector", lambda e: e.tensor_copy(out=sqst[:, 0:NTT], in_=std_c))
            b.wait("vector", sV, v_rec)
            v_nmr = b.op("vector", lambda e: e.scalar_tensor_tensor(out=nmr_c,
                         in0=mean_c, scalar=-1.0, in1=rstd_c, op0=ALU.mult,
                         op1=ALU.mult), sV)

            ev_tr = []
            ap2 = []
            b.wait("sync", sV, sV.n)
            b.wait("sync", sA, sA.n)
            b.wait("tensor", sBulkH, v_id)
            b.wait("tensor", sBulkG, 16)
            for i in range(NTT):
                if i >= 2:
                    b.wait("sync", sV, ap2[i - 2][0])
                sl = xst[:, (i % 2) * D:(i % 2 + 1) * D]
                xnsl = xnt[:, (i % 2) * D:(i % 2 + 1) * D]
                vld = b.dma("sync", sX[i % 2], sl, dr["x"][i * P:(i + 1) * P, :])
                b.wait("vector", sX[i % 2], vld)
                b.wait("vector", sV, v_nmr)
                if i >= 2:
                    b.wait("vector", sPE, ap2[i - 2][1])
                va = b.op("vector", lambda e, o=xnsl, s=sl, r=rstd_c[:, i:i + 1],
                          nm=nmr_c[:, i:i + 1]:
                          e.tensor_scalar(out=o, in0=s, scalar1=r, scalar2=nm,
                                          op0=ALU.mult, op1=ALU.add), sV)
                b.wait("tensor", sV, va)
                vpe_last = 0
                for k in range(NKD):
                    idx = i * NKD + k
                    bank = px[idx % 2]
                    if idx >= 2:
                        b.wait("tensor", sA, ev_tr[idx - 2])
                    vpe = b.op("tensor", lambda e, o=bank[:, 0:P],
                               inp=xnsl[:, k * P:(k + 1) * P]:
                               e.transpose(out=o, in_=inp, identity=identf[:]), sPE)
                    b.wait("scalar", sPE, vpe)
                    ev = b.op("scalar", lambda e, o=xn_sl(k, i * P, P), pb=bank:
                              e.activation(out=o, in_=pb[:, 0:P], func=AF.Copy), sA)
                    ev_tr.append(ev)
                    vpe_last = vpe
                ap2.append((va, vpe_last))


            gld = []
            for k in range(NKD):
                if k >= 2:
                    b.wait("gpsimd", sV, gld[k - 2])
                sl = wstg[:, (k % 2) * FF:(k % 2 + 1) * FF]
                vld = b.dma("gpsimd", sGam[k % 2], sl, dr["w1t_sl"][k])
                b.wait("vector", sGam[k % 2], vld)
                vr = b.op("vector", lambda e, o=gacc[:, k:k + 1], s=sl:
                          e.tensor_reduce(out=o, in_=s, axis=AX, op=ALU.add,
                                          apply_absolute_value=True), sV)
                gld.append(vr)
            g2 = []
            b.wait("gpsimd", sV, gld[-1])
            for j in range(NFT):
                if j >= 2:
                    b.wait("gpsimd", sV, g2[j - 2])
                sl = wstg[:, (j % 2) * D:(j % 2 + 1) * D]
                vld = b.dma("gpsimd", sGam[j % 2], sl, dr["w2t_sl"][j])
                b.wait("vector", sGam[j % 2], vld)
                vr = b.op("vector", lambda e, o=gacc[:, 8 + j:9 + j], s=sl:
                          e.tensor_reduce(out=o, in_=s, axis=AX, op=ALU.add,
                                          apply_absolute_value=True), sV)
                g2.append(vr)
            b.wait("vector", sV, gld[-1])
            b.wait("vector", sV, g2[-1])
            v_r1 = b.op("vector", lambda e: e.reduce_sum(out=gacc[:, 40:41],
                        in_=gacc[:, 0:NKD], axis=AX), sV)
            v_ga = b.op("vector", lambda e: e.reduce_sum(out=gacc[:, 41:42],
                        in_=gacc[:, 8:8 + NFT], axis=AX), sV)
            b.wait("tensor", sV, v_ga)
            vmm = b.op("tensor", lambda e: e.matmul(out=pw[:, 0:2], lhsT=onesf[:],
                       rhs=gacc[:, 40:42], start=True, stop=True), sPE)
            b.wait("vector", sPE, vmm)
            v_gc = b.op("vector", lambda e: e.tensor_copy(out=gfin[:, 10:12],
                        in_=pw[:, 0:2]), sV)
            b.wait("vector", sV, v_gc)
            v_t1 = b.op("vector", lambda e: e.tensor_scalar(out=gfin[:, 0:1],
                 in0=gfin[:, 10:11], scalar1=0.5 / float(D * FF), scalar2=0.5 * EPSQ,
                 op0=ALU.mult, op1=ALU.add), sV)
            v_t2 = b.op("vector", lambda e: e.tensor_scalar(out=gfin[:, 1:2],
                 in0=gfin[:, 11:12], scalar1=0.5 / float(FF * D), scalar2=0.5 * EPSQ,
                 op0=ALU.mult, op1=ALU.add), sV)
            b.wait("vector", sV, v_t2)
            b.op("vector", lambda e: e.tensor_scalar(out=gfin[:, 2:3], in0=gfin[:, 0:1],
                 scalar1=-1.0, scalar2=None, op0=ALU.mult), sV)
            b.op("vector", lambda e: e.tensor_scalar(out=gfin[:, 3:4], in0=gfin[:, 1:2],
                 scalar1=-1.0, scalar2=None, op0=ALU.mult), sV)
            b.barrier(bar, [("sync", sBulkH), ("sync", sX[0]), ("sync", sX[1]), ("sync", sPreS[0]), ("sync", sPreS[1]), ("sync", sWin[0]), ("sync", sWin[1]), ("sync", sW2s[0]), ("sync", sW2s[1]), ("sync", sCatL[0]), ("sync", sCatL[1]), ("gpsimd", sBulkG), ("gpsimd", sGam[0]), ("gpsimd", sGam[1]), ("gpsimd", sPreL[0]), ("gpsimd", sPreL[1]), ("gpsimd", sBC), ("gpsimd", sBC2), ("gpsimd", sQl), ("gpsimd", sQs), ("gpsimd", sCatG[0]), ("gpsimd", sCatG[1]), ("gpsimd", sXG[0]), ("gpsimd", sXG[1]), ("gpsimd", sWinG[0]), ("gpsimd", sWinG[1]), ("gpsimd", sCat[0]), ("gpsimd", sCat[1])])
            b.flush()

        with ExitStack() as p12:
            wslot = p12.enter_context(nc.sbuf_tensor("wslot", [P, 2 * D], BF16))
            wout_sb = p12.enter_context(nc.sbuf_tensor("wout_sb", [P, NET * D], BF16))
            wx_sb = p12.enter_context(nc.sbuf_tensor("wx_sb", [P, 2 * NET * 96], BF16))
            wdt_sb = p12.enter_context(nc.sbuf_tensor("wdt_sb", [RK, E], BF16))
            cvec = p12.enter_context(nc.sbuf_tensor("cvec", [P, 2 * 112], F32))
            atmp = p12.enter_context(nc.sbuf_tensor("atmp", [P, NET * NS], F32))
            A_sb = p12.enter_context(nc.sbuf_tensor("A_sb", [P, 2 * NET * NS], F32))
            u_ext = p12.enter_context(nc.sbuf_tensor("u_ext", [P, 2 * (TC + 16)], BF16))
            utail = p12.enter_context(nc.sbuf_tensor("utail", [P, 4 * NET], BF16))
            cacc = p12.enter_context(nc.sbuf_tensor("cacc", [P, 2 * TC], BF16))
            u_sil = p12.enter_context(nc.sbuf_tensor("u_sil", [P, NET * TC], BF16))
            z_sb = p12.enter_context(nc.sbuf_tensor("z_sb", [P, NET * TC], BF16))
            dbc_sb = p12.enter_context(nc.sbuf_tensor("dbc_sb", [P, TC], BF16))
            B_bc = p12.enter_context(nc.sbuf_tensor("B_bc", [P, NS * TC], BF16))
            C_bc = p12.enter_context(nc.sbuf_tensor("C_bc", [P, NS * TC], BF16))
            delta_sb = p12.enter_context(nc.sbuf_tensor("delta_sb", [P, 2 * TC], BF16))
            spe = p12.enter_context(nc.sbuf_tensor("spe", [P, TC], BF16))
            du_sb = p12.enter_context(nc.sbuf_tensor("du_sb", [P, 2 * TC], BF16))
            TCP = TC + 2
            dA_sl = p12.enter_context(nc.sbuf_tensor("dA_sl", [P, 2 * GRP * TCP], BF16))
            h_sl = p12.enter_context(nc.sbuf_tensor("h_sl", [P, 2 * GRP * TCP], BF16))
            dBu_sb = p12.enter_context(nc.sbuf_tensor("dBu_sb", [P, GRP * TCP], BF16))
            hcar = p12.enter_context(nc.sbuf_tensor("hcar", [P, NET * NS], F32))
            ytmp = p12.enter_context(nc.sbuf_tensor("ytmp", [P, TC], BF16))
            Y_sb = p12.enter_context(nc.sbuf_tensor("Y_sb", [P, NET * TC], BF16))
            cstage = p12.enter_context(nc.sbuf_tensor("cstage", [P, 2 * TC], BF16))
            qst = p12.enter_context(nc.sbuf_tensor("qst", [P, TC], F32))
            qsg = p12.enter_context(nc.sbuf_tensor("qsg", [P, 2 * TC], BF16))
            w1f = p12.enter_context(nc.sbuf_tensor("w1f", [P, 2 * NKD * P], BF16))
            h1st = p12.enter_context(nc.sbuf_tensor("h1st", [P, 2 * TC], BF16))
            b1c12 = p12.enter_context(nc.sbuf_tensor("b1c12", [P, NFT], F32))

            jobs = []
            for half, s in enumerate("fb"):
                crange = range(NCH) if s == "f" else range(NCH - 1, -1, -1)
                for c in crange:
                    jobs.append((s, c, half))

            qunits = [("w1", k, q) for k in range(NKD) for q in range(8)] + \
                     [("w2", jj, q) for jj in range(NFT) for q in range(2)]
            qstate = {"ld": 0, "cp": 0, "sign": 0, "store": 0, "vld": {}, "stv": {}}

            def q_src_dst(i):
                kind, a, q = qunits[i]
                if kind == "w1":
                    return (dr["w1t_sl"][a][:, q * TC:(q + 1) * TC],
                            w1q_scr[a][:, q * TC:(q + 1) * TC],
                            gfin[:, 2:3], gfin[:, 0:1])
                return (dr["w2t_sl"][a][:, q * TC:(q + 1) * TC],
                        w2q_scr[a][:, q * TC:(q + 1) * TC],
                        gfin[:, 3:4], gfin[:, 1:2])

            def pump_quant(nu):
                for _ in range(nu):
                    if qstate["cp"] < qstate["ld"]:
                        i = qstate["cp"]
                        qstate["cp"] = i + 1
                        srcp, dst, thn, thp = q_src_dst(i)
                        qsl = qst[:, 0:TC]
                        s1 = qsg[:, 0:TC]
                        s2 = qsg[:, TC:2 * TC]
                        b.wait("scalar", sQl, qstate["vld"][i])
                        b.wait("scalar", sQs, qstate["store"])
                        b.op("scalar", lambda e, o=s1, s=qsl, t=thn:
                             e.activation(out=o, in_=s, func=AF.Sign, bias=t))
                        qstate["sign"] = b.op("scalar", lambda e, o=s2, s=qsl, t=thp:
                                              e.activation(out=o, in_=s, func=AF.Sign,
                                                           bias=t), sA)
                        b.wait("vector", sA, qstate["sign"])
                        v_ad = b.op("vector", lambda e, o=s1, a1=s1, a2=s2:
                                    e.tensor_add(out=o, in0=a1, in1=a2), sV)
                        b.wait("vector", sV, v_ad)
                        v_ts = b.op("vector", lambda e, o=s1, a1=s1:
                                    e.tensor_scalar(out=o, in0=a1, scalar1=0.5,
                                                    scalar2=None, op0=ALU.mult), sV)
                        b.wait("gpsimd", sV, v_ts)
                        qstate["store"] = b.dma("gpsimd", sQs, dst, s1)
                        qstate["stv"][i] = qstate["store"]
                    if qstate["ld"] < len(qunits):
                        i = qstate["ld"]
                        qstate["ld"] = i + 1
                        srcp, dst, thn, thp = q_src_dst(i)
                        qsl = qst[:, 0:TC]
                        if i >= 1:
                            b.wait("gpsimd", sA, qstate["sign"])
                        qstate["vld"][i] = b.dma("gpsimd", sQl, qsl, srcp)

            h1s = {"qm": 0, "qe": 0, "vld": {}, "mmv": {}, "mmft": {},
                   "relu": {}, "stv": {}}

            def h1_evac():
                    if h1s["qe"] < h1s["qm"]:
                        q = h1s["qe"]
                        h1s["qe"] = q + 1
                        ft, c = q // NCH, q % NCH
                        b.wait("scalar", sPE, h1s["mmv"][q])
                        pst = h1s["stv"].get(q - 2)
                        if pst:
                            b.wait("scalar", sH1p[q % 2], pst)
                        h1s["relu"][q] = b.op("scalar", lambda e,
                            o=h1st[:, (q % 2) * TC:(q % 2 + 1) * TC], pb=pw,
                            bc=b1c12[:, ft:ft + 1]:
                            e.activation(out=o, in_=pb[:], func=AF.Relu, bias=bc),
                            sA)
                        b.wait("sync", sA, h1s["relu"][q])
                        h1s["stv"][q] = b.dma("sync", sH1p[q % 2],
                                              h1_scr[ft][:, c * TC:(c + 1) * TC],
                                              h1st[:, (q % 2) * TC:(q % 2 + 1) * TC])

            def pump_h1(nu):
                for _ in range(nu):
                    h1_evac()
                    q = h1s["qm"]
                    if q >= NFT * NCH:
                        continue
                    h1s["qm"] = q + 1
                    ft, c = q // NCH, q % NCH
                    fpar = ft % 2
                    if c == 0:
                        prev_mm = h1s["mmft"].get(ft - 2)
                        if prev_mm:
                            b.wait("sync", sPE, prev_mm)
                        b.wait("sync", sQs, qstate["stv"][7 * NKD + ft // 4])
                        h1s["vld"][ft] = b.dma("sync", sW1f[fpar],
                            w1f[:, fpar * NKD * P:(fpar + 1) * NKD * P],
                            w1q_scr[0:NKD, :, ft * P:(ft + 1) * P]
                            .transpose([1, 0, 2]))
                        b.wait("tensor", sW1f[fpar], h1s["vld"][ft])
                    pr = h1s["relu"].get(q - 1)
                    if pr:
                        b.wait("tensor", sA, pr)
                    wv = mk["wout_ev"].get(NKD - 1)
                    if wv:
                        b.wait("tensor", sV, wv)
                    for k in range(NKD):
                        lastk = (k == NKD - 1)
                        v = b.op("tensor", lambda e, o=pw[:],
                                 w=w1f[:, fpar * NKD * P + k * P:
                                       fpar * NKD * P + (k + 1) * P],
                                 r=xn_sl(k, c * TC, TC), st=(k == 0), sp=lastk:
                                 e.matmul(out=o, lhsT=w, rhs=r, start=st, stop=sp),
                                 sPE if lastk else None)
                    h1s["mmv"][q] = v
                    h1s["mmft"][ft] = v

            mk = {
                "xz_mm": {}, "xz_ev": {}, "conv": {}, "usil": {},
                "dbc_ev": {}, "bc_dma": {}, "minev": {}, "lnsp": {}, "dA": {},
                "scan": {}, "yacc": {}, "gate": {}, "wout_last": {}, "wout_ev": {},
                "cat_dma": {}, "delta_mm": {}, "zsil": {},
                "bc_users": {}, "prep": {}, "b_dma": {},
            }
            GG = {"n": 0}
            WINLD = {"cnt": [0, 0]}

            def dirpar(j):
                return jobs[j][2]

            def emit_scan_memsets(fwd):
                bk = 0 if fwd else (TC + 1)
                fl = 1 if fwd else TC
                v = None
                for gp in range(2):
                    base = gp * GRP * TCP
                    v = b.op("vector", lambda e, o=dA_sl[:, base + bk:base + bk
                             + (GRP - 1) * TCP + 1:TCP]: e.memset(o, 0.0), sV)
                    v = b.op("vector", lambda e, o=dA_sl[:, base + fl:base + fl
                             + (GRP - 1) * TCP + 1:TCP]: e.memset(o, 1.0), sV)
                v = b.op("vector", lambda e, o=dBu_sb[:, fl:fl
                         + (GRP - 1) * TCP + 1:TCP]: e.memset(o, 0.0), sV)
                return v

            def emit_dir_consts(s, half):
                b.dma("sync", sBulkH, cvec[:, half * 112:(half + 1) * 112],
                      dr[f"cvec_{s}"])
                v_al = b.dma("sync", sBulkH, atmp[:], dr[f"alog_{s}"])
                b.wait("scalar", sBulkH, v_al)
                asl = A_sb[:, half * NET * NS:(half + 1) * NET * NS]
                v_ex = b.op("scalar", lambda e, o=asl: e.activation(
                    out=o, in_=atmp[:], func=AF.Exp), sA)
                b.wait("vector", sA, v_ex)
                v_neg = b.op("vector", lambda e, o=asl: e.tensor_scalar(
                    out=o, in0=o, scalar1=-1.0, scalar2=None, op0=ALU.mult), sV)
                for jj in range(2):
                    b.dma("gpsimd", sBulkG,
                          wx_sb[:, (half * NET + jj * 8) * 96:(half * NET + (jj + 1) * 8) * 96],
                          dr[f"wx_sl_{s}"][8 * jj:8 * (jj + 1)].transpose([1, 0, 2]))
                return v_neg, sBulkH.n, sBulkG.n

            def emit_wdt_load(s, gate_pe):
                if gate_pe:
                    b.wait("gpsimd", sPE, gate_pe)
                b.dma("gpsimd", sBulkG, wdt_sb[:], dr[f"wdt_{s}"])
                return sBulkG.n

            def emit_wout_load(s, gate_pe):
                b.wait("gpsimd", sPE, gate_pe)
                for jj in range(NKD):
                    b.dma("gpsimd", sBulkG, wout_sb[:, jj * 2 * D:(jj + 1) * 2 * D],
                          dr[f"wout_sl_{s}"][2 * jj:2 * (jj + 1)].transpose([1, 0, 2]))
                return sBulkG.n

            def emit_A_mm(j, et):
                s, c, half = jobs[j]
                fwd = (s == "f")
                t0 = c * TC
                par = et % 2
                for xt, bank in ((et, px[par]), (NET + et, pdb)):
                    pk = WINLD["cnt"][par]
                    prev_use = mk["xz_mm"].get(("ld", par, pk - 1))
                    if prev_use:
                        b.wait("gpsimd", sPE, prev_use)
                    vld = b.dma("gpsimd", sWinG[par], wslot[:, par * D:(par + 1) * D],
                                dr[f"win_sl_{s}"][xt])
                    WINLD["cnt"][par] = pk + 1
                    b.wait("tensor", sWinG[par], vld)
                    if xt < NET:
                        prev_ev = mk["xz_ev"].get((j, et - 2)) or \
                            (mk["xz_ev"].get((j - 1, NET - 1)) if et < 2 else None)
                    else:
                        prev_ev = mk["zsil"].get((j, et - 1)) or \
                            (mk["zsil"].get((j - 1, NET - 1)) if et == 0 else None)
                    if prev_ev:
                        b.wait("tensor", sA, prev_ev)
                    for k in range(NKD):
                        last = (k == NKD - 1)
                        v = b.op("tensor", lambda e, o=bank[:],
                                 w=wslot[:, par * D + k * P:par * D + (k + 1) * P],
                                 r=xn_sl(k, t0, TC), st=(k == 0), sp=last:
                                 e.matmul(out=o, lhsT=w, rhs=r, start=st, stop=sp),
                                 sPE if last else None)
                    mk["xz_mm"][("ld", par, pk)] = v
                    mk["xz_mm"][("mm", j, xt)] = v

            def emit_A_evac_u(j, et):
                s, c, half = jobs[j]
                fwd = (s == "f")
                par = et % 2
                uo = u_ext[:, par * (TC + 16) + (3 if fwd else 0):
                           par * (TC + 16) + (3 if fwd else 0) + TC]
                b.wait("scalar", sPE, mk["xz_mm"][("mm", j, et)])
                prev_conv = mk["conv"].get((j, et - 2)) or \
                    (mk["conv"].get((j - 1, NET - 2 + et)) if et < 2 else None)
                if prev_conv:
                    b.wait("scalar", sV, prev_conv)
                if j > 0:
                    g8 = mk["gate"].get((j - 1, et))
                    if g8:
                        b.wait("scalar", sV, g8)
                mk["xz_ev"][(j, et)] = b.op("scalar", lambda e, o=uo,
                    pb=px[par]: e.activation(out=o, in_=pb[:], func=AF.Copy), sA)

            def emit_A_evac_z(j, et):
                b.wait("scalar", sPE, mk["xz_mm"][("mm", j, NET + et)])
                if j > 0:
                    g8 = mk["gate"].get((j - 1, et))
                    if g8:
                        b.wait("scalar", sV, g8)
                mk["zsil"][(j, et)] = b.op("scalar", lambda e,
                    o=z_sb[:, et * TC:(et + 1) * TC], pb=pdb:
                    e.activation(out=o, in_=pb[:], func=AF.Copy), sA)

            def emit_A_conv(j, et):
                s, c, half = jobs[j]
                fwd = (s == "f")
                par = et % 2
                b.wait("vector", sA, mk["xz_ev"][(j, et)])
                prev_sil = mk["usil"].get((j, et - 2)) or \
                    (mk["usil"].get((j - 1, NET - 2 + et)) if et < 2 else None)
                if prev_sil:
                    b.wait("vector", sA, prev_sil)
                ue0 = par * (TC + 16)
                tl = utail[:, et * 4:et * 4 + 3]
                if fwd:
                    v_h = b.op("vector", lambda e, o=u_ext[:, ue0:ue0 + 3], i=tl:
                               e.tensor_copy(out=o, in_=i), sV)
                else:
                    v_h = b.op("vector", lambda e, o=u_ext[:, ue0 + TC:ue0 + TC + 3],
                               i=tl: e.tensor_copy(out=o, in_=i), sV)
                ca = cacc[:, par * TC:(par + 1) * TC]
                wks = [half * 112 + 16 + et * 4 + k for k in range(4)]
                order = [0, 1, 2, 3] if fwd else [3, 2, 1, 0]
                b.wait("vector", sV, v_h)
                v_c = b.op("vector", lambda e, o=ca, i=u_ext[:, ue0:ue0 + TC],
                           w=cvec[:, wks[order[0]]:wks[order[0]] + 1]:
                           e.tensor_scalar(out=o, in0=i, scalar1=w, scalar2=None,
                                           op0=ALU.mult), sV)
                for jj in (1, 2, 3):
                    b.wait("vector", sV, v_c)
                    dst = ca if jj < 3 else u_sil[:, et * TC:(et + 1) * TC]
                    v_c = b.op("vector", lambda e, o=dst,
                               i=u_ext[:, ue0 + jj:ue0 + jj + TC],
                               w=cvec[:, wks[order[jj]]:wks[order[jj]] + 1], acc=ca:
                               e.scalar_tensor_tensor(out=o, in0=i, scalar=w,
                                                      in1=acc, op0=ALU.mult,
                                                      op1=ALU.add), sV)
                b.wait("vector", sV, v_c)
                if fwd:
                    mk["conv"][(j, et)] = b.op("vector", lambda e, o=tl,
                        i=u_ext[:, ue0 + TC:ue0 + TC + 3]:
                        e.tensor_copy(out=o, in_=i), sV)
                else:
                    mk["conv"][(j, et)] = b.op("vector", lambda e, o=tl,
                        i=u_ext[:, ue0:ue0 + 3]:
                        e.tensor_copy(out=o, in_=i), sV)

            def emit_silu_batch(j, ets, zets):
                s, c, half = jobs[j]
                if ets:
                    b.wait("scalar", sV, mk["conv"][(j, ets[-1])])
                if zets:
                    b.wait("scalar", sA, mk["zsil"][(j, zets[-1])])
                for et in ets:
                    usl = u_sil[:, et * TC:(et + 1) * TC]
                    mk["usil"][(j, et)] = b.op("scalar", lambda e, o=usl,
                        bc=cvec[:, half * 112 + et:half * 112 + et + 1]:
                        e.activation(out=o, in_=o, func=AF.Silu, bias=bc), sA)
                for et in zets:
                    zsl = z_sb[:, et * TC:(et + 1) * TC]
                    mk["usil"][(j, NET + et)] = b.op("scalar", lambda e, o=zsl:
                        e.activation(out=o, in_=o, func=AF.Silu), sA)


            def emit_dbc(j):
                s, c, half = jobs[j]
                b.wait("tensor", sA, mk["usil"][(j, NET - 1)])
                b.wait("tensor", sA, mk["zsil"][(j, NET - 1)])
                if j > 0:
                    b.wait("tensor", sA, mk["dbc_ev"][j - 1])
                for et in range(NET):
                    last = (et == NET - 1)
                    v = b.op("tensor", lambda e, o=pdb[0:96, :],
                             w=wx_sb[:, (half * NET + et) * 96:(half * NET + et + 1) * 96],
                             r=u_sil[:, et * TC:(et + 1) * TC],
                             st=(et == 0), sp=last:
                             e.matmul(out=o, lhsT=w, rhs=r, start=st, stop=sp),
                             sPE if last else None)
                b.wait("scalar", sPE, v)
                if j > 0:
                    b.wait("scalar", sBC2, mk["bc_dma"][j - 1])
                mk["dbc_ev"][j] = b.op("scalar", lambda e:
                                       e.activation(out=dbc_sb[0:96, 0:TC],
                                                    in_=pdb[0:96, :], func=AF.Copy), sA)
                b.wait("gpsimd", sA, mk["dbc_ev"][j])
                if j > 0:
                    b.wait("gpsimd", sV, mk["bc_users"][j - 1])
                vf1 = b.dma("gpsimd", sBC, bc_scr[0:1, :], dbc_sb[RK:RK + NS, 0:TC])
                vf2 = b.dma("gpsimd", sBC, bc_scr[1:2, :],
                            dbc_sb[RK + NS:RK + 2 * NS, 0:TC])
                b.wait("gpsimd", sBC, vf2)
                mk["b_dma"][j] = b.dma("gpsimd", sBC, B_bc[:],
                                       bc_scr[0:1, :]
                                       .unsqueeze(1).broadcast_to([1, P, NS * TC]))
                mk["bc_dma"][j] = b.dma("gpsimd", sBC2, C_bc[:],
                                        bc_scr[1:2, :]
                                        .unsqueeze(1).broadcast_to([1, P, NS * TC]))

            def emit_delta(j, et):
                s, c, half = jobs[j]
                par = et % 2
                b.wait("tensor", sA, mk["dbc_ev"][j])
                wl = mk.get("wdt_pend")
                if wl is not None and j >= 4:
                    b.wait("tensor", sBulkG, wl)
                    mk["wdt_pend"] = None
                prev = mk["minev"].get((j, et - 2)) or \
                    (mk["minev"].get((j - 1, NET - 2 + et)) if et < 2 else None)
                if prev:
                    b.wait("tensor", sV, prev)
                mk["delta_mm"][(j, et)] = b.op("tensor", lambda e, o=pd[par][:],
                    w=wdt_sb[:, et * P:(et + 1) * P],
                    r=dbc_sb[0:RK, 0:TC]:
                    e.matmul(out=o, lhsT=w, rhs=r, start=True, stop=True), sPE)

            def emit_prep_v(j, et):
                s, c, half = jobs[j]
                par = et % 2
                dsl = delta_sb[:, par * TC:(par + 1) * TC]
                b.wait("vector", sPE, mk["delta_mm"][(j, et)])
                prev_dA = mk["dA"].get((j, et - 2, NGR - 1)) or \
                    (mk["dA"].get((j - 1, NET - 2 + et, NGR - 1)) if et < 2 else None)
                if prev_dA:
                    b.wait("vector", sA, prev_dA)
                mk["minev"][(j, et)] = b.op("vector", lambda e, o=dsl, pb=pd[par],
                    bd=cvec[:, half * 112 + 80 + et:half * 112 + 81 + et]:
                    e.tensor_scalar(out=o, in0=pb[:], scalar1=bd, scalar2=25.0,
                                    op0=ALU.add, op1=ALU.min), sV)

            def emit_prep_a(j, et):
                par = et % 2
                dsl = delta_sb[:, par * TC:(par + 1) * TC]
                b.wait("scalar", sV, mk["minev"][(j, et)])
                v_se = b.op("scalar", lambda e, o=spe[:], i=dsl:
                            e.activation(out=o, in_=i, func=AF.Exp), sA)
                b.wait("scalar", sA, v_se)
                mk["lnsp"][(j, et)] = b.op("scalar", lambda e, o=dsl, i=spe[:]:
                                           e.activation(out=o, in_=i, func=AF.Ln,
                                                        bias=1.0), sA)

            def emit_B_et(j, et):
                s, c, half = jobs[j]
                fwd = (s == "f")
                par = et % 2
                dsl = delta_sb[:, par * TC:(par + 1) * TC]
                dusl = du_sb[:, par * TC:(par + 1) * TC]
                b.wait("vector", sA, mk["lnsp"][(j, et)])
                b.wait("scalar", sA, mk["lnsp"][(j, et)])
                v_du = b.op("vector", lambda e, o=dusl, a1=dsl,
                            a2=u_sil[:, et * TC:(et + 1) * TC]:
                            e.tensor_mul(out=o, in0=a1, in1=a2), sV)
                b.wait("vector", sV, v_du)
                for g in range(NGR):
                    gg = GG["n"]
                    GG["n"] += 1
                    gp = gg % 2
                    if gg >= 2:
                        b.wait("scalar", sV, mk["scan"][gg - 2])
                    doff = 2 if fwd else 0
                    for jj in range(GRP):
                        n = g * GRP + jj
                        last = (jj == GRP - 1)
                        sl0 = (gp * GRP + jj) * TCP + doff
                        v = b.op("scalar", lambda e,
                                 o=dA_sl[:, sl0:sl0 + TC],
                                 i=dsl, sc=A_sb[:, half * NET * NS + et * NS + n:
                                                half * NET * NS + et * NS + n + 1]:
                                 e.activation(out=o, in_=i, func=AF.Exp, scale=sc),
                                 sA if last else None)
                    mk["dA"][(j, et, g)] = v
                    b.wait("vector", sA, v)
                    if gg >= 2:
                        b.wait("vector", sPE, mk["yacc"][gg - 2])
                    if et == 0 and g == 0:
                        b.wait("vector", sBC, mk["b_dma"][j])
                    base = gp * GRP * TCP
                    v_db = b.op("vector", lambda e,
                                o=dBu_sb[:].rearrange("p (s t) -> p s t", s=GRP)
                                [:, :, doff:doff + TC],
                                a1=dusl.unsqueeze(1).broadcast_to([P, GRP, TC]),
                                a2=B_bc[:, g * GRP * TC:(g + 1) * GRP * TC]
                                .rearrange("p (s t) -> p s t", s=GRP):
                                e.tensor_mul(out=o, in0=a1, in1=a2), sV)
                    b.wait("vector", sV, v_db)
                    for jj in range(GRP):
                        n = g * GRP + jj
                        sl0 = (gp * GRP + jj) * TCP + doff
                        db0 = jj * TCP + doff
                        ic = hcar[:, et * NS + n:et * NS + n + 1]
                        if fwd:
                            v_sc = b.op("vector", lambda e, o=h_sl[:, sl0:sl0 + TC],
                                        d0=dA_sl[:, sl0:sl0 + TC],
                                        d1=dBu_sb[:, db0:db0 + TC], i=ic:
                                        e.tensor_tensor_scan(out=o, data0=d0, data1=d1,
                                        initial=i, op0=ALU.mult, op1=ALU.add),
                                        sV if jj == GRP - 1 else None)
                        else:
                            v_sc = b.op("vector", lambda e, o=h_sl[:, sl0:sl0 + TC],
                                        d0=dA_sl[:, sl0:sl0 + TC],
                                        d1=dBu_sb[:, db0:db0 + TC], i=ic:
                                        e.tensor_tensor_scan(out=o[:, ::-1],
                                        data0=d0[:, ::-1], data1=d1[:, ::-1],
                                        initial=i, op0=ALU.mult, op1=ALU.add),
                                        sV if jj == GRP - 1 else None)
                    b.wait("vector", sV, v_sc)
                    off = (doff + TC - 1) if fwd else doff
                    v_tl = b.op("vector", lambda e,
                                o=hcar[:, et * NS + g * GRP:et * NS + (g + 1) * GRP],
                                i=h_sl[:, base + off:base + off
                                       + (GRP - 1) * TCP + 1:TCP]:
                                e.tensor_copy(out=o, in_=i), sV)
                    b.wait("vector", sV, v_tl)
                    if et == 0 and g == 0:
                        b.wait("vector", sBC2, mk["bc_dma"][j])
                    v = b.op("vector", lambda e,
                             o=h_sl[:, base:base + GRP * TCP]
                             .rearrange("p (s t) -> p s t", s=GRP)[:, :, doff:doff + TC],
                             a2=C_bc[:, g * GRP * TC:(g + 1) * GRP * TC]
                             .rearrange("p (s t) -> p s t", s=GRP):
                             e.tensor_mul(out=o, in0=o, in1=a2), sV)
                    mk["scan"][gg] = v
                    b.wait("tensor", sV, mk["scan"][gg])
                    if g == 0:
                        prev_gate = mk["gate"].get((j, et - 2)) or \
                            (mk["gate"].get((j - 1, NET - 2 + et)) if et < 2 else None)
                        if prev_gate:
                            b.wait("tensor", sV, prev_gate)
                    for jj in range(GRP):
                        n = g * GRP + jj
                        last = (jj == GRP - 1)
                        sl0 = (gp * GRP + jj) * TCP + doff
                        v = b.op("tensor", lambda e, o=py[par][:],
                                 r=h_sl[:, sl0:sl0 + TC],
                                 st=(n == 0), sp=(n == NS - 1):
                                 e.matmul(out=o, lhsT=ident[:], rhs=r, start=st,
                                          stop=sp), sPE if last else None)
                    mk["yacc"][gg] = v
                    if g == 0 and et + 1 < NET:
                        emit_prep_v(j, et + 1)
                    if g == 1:
                        if j < 7 and et >= 1:
                            emit_A_evac_u(j + 1, et - 1)
                        pump_quant(1)
                        if j >= 4:
                            pump_h1(1)
                    if g == 2:
                        if j < 7 and et >= 1:
                            emit_A_conv(j + 1, et - 1)
                        if et + 1 < NET:
                            emit_prep_a(j, et + 1)
                    if g == 3:
                        if j < 7 and et >= 1:
                            emit_A_evac_z(j + 1, et - 1)
                        if j >= 4:
                            pump_h1(1)
                        if j < 7 and et == NET - 1:
                            emit_silu_batch(j + 1, list(range(NET - 1)),
                                            list(range(NET - 1)))
                b.wait("vector", sPE, mk["yacc"][GG["n"] - 1])
                v_g1 = b.op("vector", lambda e, o=ytmp[:],
                            i=u_sil[:, et * TC:(et + 1) * TC],
                            dp=cvec[:, half * 112 + 96 + et:half * 112 + 97 + et],
                            pb=py[par]:
                            e.scalar_tensor_tensor(out=o, in0=i, scalar=dp, in1=pb[:],
                                                   op0=ALU.mult, op1=ALU.add), sV)
                b.wait("vector", sV, v_g1)
                mk["gate"][(j, et)] = b.op("vector", lambda e,
                    o=Y_sb[:, et * TC:(et + 1) * TC], a1=ytmp[:],
                    a2=z_sb[:, et * TC:(et + 1) * TC]:
                    e.tensor_mul(out=o, in0=a1, in1=a2), sV)
                mk["bc_users"][j] = mk["scan"][GG["n"] - 1]


            def emit_C(j):
                s, c, half = jobs[j]
                fwd = (s == "f")
                t0 = c * TC
                row0 = 0 if fwd else D
                b.wait("tensor", sV, mk["gate"][(j, NET - 1)])
                hv = h1s["relu"].get(h1s["qe"] - 1)
                if hv:
                    b.wait("tensor", sA, hv)
                if j > 0:
                    b.wait("tensor", sBulkG, mk["wout_ld"])
                for dt in range(NKD):
                    if dt > 0:
                        b.wait("tensor", sV, mk["wout_ev"][dt - 1])
                    elif j > 0:
                        b.wait("tensor", sV, mk["wout_ev"][NKD - 1])
                    for et in range(NET):
                        last = (et == NET - 1)
                        v = b.op("tensor", lambda e, o=pw[:],
                                 w=wout_sb[:, et * D + dt * P:et * D + (dt + 1) * P],
                                 r=Y_sb[:, et * TC:(et + 1) * TC],
                                 st=(et == 0), sp=last:
                                 e.matmul(out=o, lhsT=w, rhs=r, start=st, stop=sp),
                                 sPE if last else None)
                    b.wait("vector", sPE, v)
                    mk["wout_last"][j] = v
                    if dt >= 2:
                        b.wait("vector", sCat[dt % 2], mk["cat_dma"][dt - 2])
                    elif j > 0:
                        b.wait("vector", sCat[dt % 2], mk["cat_dma"][NKD - 2 + dt])
                    mk["wout_ev"][dt] = b.op("vector", lambda e,
                        o=cstage[:, (dt % 2) * TC:(dt % 2 + 1) * TC], pb=pw:
                        e.tensor_copy(out=o, in_=pb[:]), sV)
                    b.wait("gpsimd", sV, mk["wout_ev"][dt])
                    mk["cat_dma"][dt] = b.dma("gpsimd", sCat[dt % 2],
                        cat_scr[row0 + dt * P:row0 + (dt + 1) * P, t0:t0 + TC],
                        cstage[:, (dt % 2) * TC:(dt % 2 + 1) * TC])

            b.dma("gpsimd", sBulkG, b1c12[:], dr["b1col"])
            v_neg, vbh, vbg = emit_dir_consts("f", 0)
            emit_wdt_load("f", 0)
            mk["wout_ld"] = emit_wout_load("f", 0)
            vbg = sBulkG.n
            for eng in ENGS:
                b.wait(eng, sBulkH, vbh)
                b.wait(eng, sBulkG, vbg)
                b.wait(eng, sV, v_neg)
            v_ms = b.op("vector", lambda e: e.memset(hcar[:], 0.0), sV)
            v_ms = b.op("vector", lambda e: e.memset(utail[:], 0.0), sV)
            v_ms = emit_scan_memsets(True)
            b.wait("scalar", sV, v_ms)
            b.wait("tensor", sV, v_ms)

            for et in range(NET):
                emit_A_mm(0, et)
                emit_A_evac_u(0, et)
                emit_A_evac_z(0, et)
                emit_A_conv(0, et)
            emit_silu_batch(0, list(range(NET)), list(range(NET)))
            emit_dbc(0)
            emit_delta(0, 0)
            emit_delta(0, 1)
            emit_prep_v(0, 0)
            emit_prep_a(0, 0)

            for j in range(8):
                b.wait("vector", sV, sV.n)
                b.wait("scalar", sA, sA.n)
                if j == 3:
                    v_neg, vbh, vbg = emit_dir_consts("b", 1)
                    for eng in ("vector", "scalar", "tensor", "sync", "gpsimd"):
                        b.wait(eng, sBulkH, vbh)
                        b.wait(eng, sBulkG, vbg)
                        b.wait(eng, sV, v_neg)
                    v_ms = b.op("vector", lambda e: e.memset(utail[:], 0.0), sV)
                    b.wait("scalar", sV, v_ms)
                    mk["bwd_ms"] = None
                if j >= 5:
                    h1_evac()
                if j >= 1:
                    emit_C(j - 1)
                if j == 4:
                    v_ms = b.op("vector", lambda e: e.memset(hcar[:], 0.0), sV)
                    emit_scan_memsets(False)
                    mk["wout_ld"] = emit_wout_load("b", mk["wout_last"][3])
                for et in range(NET):
                    emit_B_et(j, et)
                    if et + 2 < NET:
                        emit_delta(j, et + 2)
                    if j == 3 and et == 14:
                        mk["wdt_pend"] = emit_wdt_load("b",
                                                       mk["delta_mm"][(3, 15)])
                    if j < 7:
                        emit_A_mm(j + 1, et)
                if j < 7:
                    emit_A_evac_u(j + 1, NET - 1)
                    emit_A_evac_z(j + 1, NET - 1)
                    emit_A_conv(j + 1, NET - 1)
                    emit_silu_batch(j + 1, [NET - 1], [NET - 1])
                    emit_dbc(j + 1)
                    emit_delta(j + 1, 0)
                    emit_delta(j + 1, 1)
                    emit_prep_v(j + 1, 0)
                    emit_prep_a(j + 1, 0)
            pump_quant(4)
            pump_h1(1)
            emit_C(7)
            b.barrier(bar, [("sync", sBulkH), ("sync", sX[0]), ("sync", sX[1]), ("sync", sPreS[0]), ("sync", sPreS[1]), ("sync", sWin[0]), ("sync", sWin[1]), ("sync", sW2s[0]), ("sync", sW2s[1]), ("sync", sCatL[0]), ("sync", sCatL[1]), ("sync", sW1f[0]), ("sync", sW1f[1]), ("sync", sH1p[0]), ("sync", sH1p[1]), ("gpsimd", sBulkG), ("gpsimd", sGam[0]), ("gpsimd", sGam[1]), ("gpsimd", sPreL[0]), ("gpsimd", sPreL[1]), ("gpsimd", sBC), ("gpsimd", sBC2), ("gpsimd", sQl), ("gpsimd", sQs), ("gpsimd", sCatG[0]), ("gpsimd", sCatG[1]), ("gpsimd", sXG[0]), ("gpsimd", sXG[1]), ("gpsimd", sWinG[0]), ("gpsimd", sWinG[1]), ("gpsimd", sCat[0]), ("gpsimd", sCat[1])])
            b.flush()
        with ExitStack() as p34:
            b2bc = p34.enter_context(nc.sbuf_tensor("b2bc", [P, D], F32))
            bpbc = p34.enter_context(nc.sbuf_tensor("bpbc", [P, D], F32))
            gstage = p34.enter_context(nc.sbuf_tensor("gstage", [P, 2 * TC], BF16))
            w2q_sb = p34.enter_context(nc.sbuf_tensor("w2q_sb", [P, NFT * D], BF16))
            h1sl = p34.enter_context(nc.sbuf_tensor("h1sl", [P, 8 * TC], BF16))
            g_sb = p34.enter_context(nc.sbuf_tensor("g_sb", [P, 4 * D], BF16))
            wproj_sb = p34.enter_context(nc.sbuf_tensor("wproj_sb", [P, NET * D], BF16))
            catsl = p34.enter_context(nc.sbuf_tensor("catsl", [P, 2 * NET * P], BF16))
            xsl = p34.enter_context(nc.sbuf_tensor("xsl", [P, 2 * D], F32))
            sstage = p34.enter_context(nc.sbuf_tensor("sstage", [P, 2 * TC], BF16))

            b.dma("gpsimd", sBulkG, b2bc[:],
                  dr["b2row"].unsqueeze(1).broadcast_to([1, P, D]))
            b.dma("gpsimd", sBulkG, bpbc[:],
                  dr["bprojrow"].unsqueeze(1).broadcast_to([1, P, D]))
            for jj in range(NKD):
                b.dma("gpsimd", sBulkG, wproj_sb[:, jj * 2 * D:(jj + 1) * 2 * D],
                      dr["wproj_sl"][2 * jj:2 * (jj + 1)].transpose([1, 0, 2]))
            v_wp = sBulkG.n
            w2half = []
            for hl in range(2):
                for fp in range(NFT // 4):
                    fpp = hl * (NFT // 4) + fp
                    v = b.dma("sync", sW2s[hl],
                              w2q_sb[:, 2 * fpp * D:(2 * fpp + 2) * D],
                              w2q_scr[2 * fpp:2 * fpp + 2].transpose([1, 0, 2]))
                w2half.append(v)

            HB = TC // 2
            h1mm = {}
            gsil = {}
            pj_mm = {}
            pj_out = {}
            pj_st = {}
            PASSES = [([pdb, py[0], py[1], pw], [(0, 0), (0, 1), (1, 0), (1, 1)],
                       sH1G),
                      ([px[0], px[1], pd[0], pd[1]],
                       [(2, 0), (2, 1), (3, 0), (3, 1)], sH1G2)]
            NG4 = NFT // 4
            for c in range(NCH):
                t0 = c * TC
                for ps, (banks, tiles, sems) in enumerate(PASSES):
                    if ps == 0:
                        for i4, (tt, hf) in enumerate(tiles):
                            pv = gsil.get((c - 1, tt, hf))
                            if pv:
                                b.wait("tensor", sA, pv)
                    else:
                        if c > 0:
                            b.wait("tensor", sV, pj_out[4 * (c - 1) + 2])
                            b.wait("tensor", sV, pj_out[4 * (c - 1) + 3])
                    for g4 in range(NG4):
                        par = g4 % 2
                        prev = h1mm.get((c, ps, g4 - 2)) or \
                               (h1mm.get((c - 1, ps, NG4 - 2 + g4))
                                if g4 < 2 and c > 0 else None)
                        if prev:
                            b.wait("gpsimd", sPE, prev)
                        vh = b.dma("gpsimd", sems[par],
                                   h1sl[:, (ps * 2 + par) * 4 * HB:
                                        (ps * 2 + par + 1) * 4 * HB],
                                   h1_scr[4 * g4:4 * (g4 + 1),
                                          :, t0 + ps * 2 * P:t0 + ps * 2 * P + 2 * P]
                                   .transpose([1, 0, 2]))
                        b.wait("tensor", sems[par], vh)
                        if c == 0 and ps == 0:
                            b.wait("tensor", sW2s[g4 // (NG4 // 2)],
                                   w2half[g4 // (NG4 // 2)])
                        for sub in range(4):
                            ft = 4 * g4 + sub
                            for i4, (tt, hf) in enumerate(tiles):
                                last = (i4 == 3 and sub == 3)
                                v = b.op("tensor", lambda e, o=banks[i4][:],
                                         w=h1sl[:, (ps * 2 + par) * 4 * HB
                                                + sub * HB + (tt - 2 * ps) * P:
                                                (ps * 2 + par) * 4 * HB
                                                + sub * HB + (tt - 2 * ps + 1) * P],
                                         r=w2q_sb[:, ft * D + hf * TC:
                                                  ft * D + (hf + 1) * TC],
                                         st=(ft == 0), sp=(ft == NFT - 1):
                                         e.matmul(out=o, lhsT=w, rhs=r, start=st,
                                                  stop=sp), sPE if last else None)
                        h1mm[(c, ps, g4)] = v
                    b.wait("vector", sPE, v)
                    if c == 0 and ps == 0:
                        b.wait("vector", sBulkG, v_wp)
                    for i4, (tt, hf) in enumerate(tiles):
                        gsl = gstage[:, hf * TC:(hf + 1) * TC]
                        ps_prev = gsil.get((c, tt - 1, hf)) if i4 >= 2 else \
                            (gsil.get((c - 1, 3, hf)) if ps == 0 and c > 0 else
                             gsil.get((c, 1, hf)) if ps == 1 else None)
                        if ps_prev:
                            b.wait("vector", sA, ps_prev)
                        vv = b.op("vector", lambda e, o=gsl, pb=banks[i4],
                                  bb=b2bc[:, hf * TC:(hf + 1) * TC]:
                                  e.tensor_add(out=o, in0=pb[:], in1=bb), sV)
                        b.wait("scalar", sV, vv)
                        if c > 0:
                            b.wait("scalar", sV, pj_out[4 * (c - 1) + tt])
                        gsil[(c, tt, hf)] = b.op("scalar", lambda e,
                            o=g_sb[:, tt * D + hf * TC:tt * D + (hf + 1) * TC],
                            i=gsl:
                            e.activation(out=o, in_=i, func=AF.Silu), sA)
                b.wait("tensor", sBulkG, v_wp)
                for tl in range(4):
                    ti = 4 * c + tl
                    par = ti % 2
                    prev = pj_mm.get(ti - 2)
                    if prev:
                        b.wait("gpsimd", sPE, prev)
                    if ti >= 2:
                        b.wait("gpsimd", sV, pj_out[ti - 2])
                        b.wait("gpsimd", sSt[par], pj_st[ti - 2])
                    vc = b.dma("gpsimd", sCatG[par],
                               catsl[:, par * NET * P:(par + 1) * NET * P],
                               cat_scr[:, ti * P:(ti + 1) * P]
                               .rearrange("(k p) t -> p k t", p=P))
                    vx = b.dma("gpsimd", sXG[par], xsl[:, par * D:(par + 1) * D],
                               dr["x"][ti * P:(ti + 1) * P, :])
                    b.wait("tensor", sCatG[par], vc)
                    if ti >= 2:
                        b.wait("tensor", sV, pj_out[ti - 2])
                    pjbanks = [px[0], px[1]] if tl % 2 == 0 else [pd[0], pd[1]]
                    pjtt = 2 if tl % 2 == 0 else 3
                    for hf in range(2):
                        bank = pjbanks[hf]
                        b.wait("tensor", sA, gsil[(c, pjtt, hf)])
                        for k in range(16):
                            last = (k == 15)
                            v = b.op("tensor", lambda e, o=bank[:],
                                     w=catsl[:, par * NET * P + k * P:
                                             par * NET * P + (k + 1) * P],
                                     r=wproj_sb[:, k * D + hf * TC:
                                                k * D + (hf + 1) * TC],
                                     st=(k == 0), sp=last:
                                     e.matmul(out=o, lhsT=w, rhs=r, start=st,
                                              stop=sp),
                                     sPE if last else None)
                    pj_mm[ti] = v
                    b.wait("vector", sPE, v)
                    b.wait("vector", sXG[par], vx)
                    b.wait("vector", sA, gsil[(c, tl, 1)])
                    if ti >= 2:
                        b.wait("vector", sSt[par], pj_st[ti - 2])
                    for hf in range(2):
                        bank = pjbanks[hf]
                        ssl = sstage[:, hf * TC:(hf + 1) * TC]
                        v_a = b.op("vector", lambda e, o=ssl, pb=bank,
                                   bb=bpbc[:, hf * TC:(hf + 1) * TC]:
                                   e.tensor_add(out=o, in0=pb[:], in1=bb), sV)
                        b.wait("vector", sV, v_a)
                        v_m = b.op("vector", lambda e, o=ssl, a1=ssl,
                                   a2=g_sb[:, tl * D + hf * TC:
                                           tl * D + (hf + 1) * TC]:
                                   e.tensor_mul(out=o, in0=a1, in1=a2), sV)
                        b.wait("vector", sV, v_m)
                        v2 = b.op("vector", lambda e,
                                  o=xsl[:, par * D + hf * TC:par * D + (hf + 1) * TC],
                                  a1=ssl, a2=xsl[:, par * D + hf * TC:
                                                 par * D + (hf + 1) * TC]:
                                  e.tensor_add(out=o, in0=a1, in1=a2), sV)
                    pj_out[ti] = v2
                    b.wait("sync", sV, v2)
                    pj_st[ti] = b.dma("sync", sSt[par], y_out[ti * P:(ti + 1) * P, :],
                                      xsl[:, par * D:(par + 1) * D])
            b.wait("sync", sSt[0], sSt[0].n)
            b.wait("sync", sSt[1], sSt[1].n)
            b.flush()
    return nc


_PROG = None


def _get_prog():
    global _PROG
    if _PROG is None:
        _PROG = build_program()
    return _PROG


def kernel(**inputs):
    f32 = np.float32

    def cc(a):
        return np.ascontiguousarray(a, dtype=f32)

    ins = {k[len("L0_attn_"):]: v for k, v in inputs.items() if k.startswith("L0_attn_")}
    ins.update({k[len("L0_ffn_"):]: v for k, v in inputs.items() if k.startswith("L0_ffn_")})

    base = {}
    for s in "fb":
        win = np.asarray(ins[f"win_{s}"], dtype=f32)
        base[f"win_sl_{s}"] = cc(win.reshape(NXT, P, NKD, P).transpose(0, 3, 2, 1)
                                 .reshape(NXT, P, D))
        base[f"wout_sl_{s}"] = cc(np.asarray(ins[f"wout_{s}"], dtype=f32).T
                                  .reshape(NET, P, D))
        base[f"wx_sl_{s}"] = cc(np.asarray(ins[f"wx_{s}"], dtype=f32).T
                                .reshape(NET, P, 96))
        base[f"wdt_{s}"] = cc(np.asarray(ins[f"wdt_{s}"], dtype=f32).T)
        bconv = np.asarray(ins[f"bconv_{s}"], dtype=f32).reshape(NET, P).T
        wconv = np.asarray(ins[f"wconv_{s}"], dtype=f32)[:, 0, :] \
            .reshape(NET, P, 4).transpose(1, 0, 2).reshape(P, NET * 4)
        bdt = np.asarray(ins[f"bdt_{s}"], dtype=f32).reshape(NET, P).T
        Dp = np.asarray(ins[f"Dp_{s}"], dtype=f32).reshape(NET, P).T
        base[f"cvec_{s}"] = cc(np.concatenate([bconv, wconv, bdt, Dp], axis=1))
        base[f"alog_{s}"] = cc(np.asarray(ins[f"Alog_{s}"], dtype=f32)
                               .reshape(NET, P, NS).transpose(1, 0, 2).reshape(P, NET * NS))
    base["w1t_sl"] = cc(np.asarray(ins["W1"], dtype=f32).T.reshape(NKD, P, FF))
    base["w2t_sl"] = cc(np.asarray(ins["W2"], dtype=f32).T.reshape(NFT, P, D))
    base["wproj_sl"] = cc(np.asarray(ins["wproj"], dtype=f32).T.reshape(NET, P, D))
    base["b1col"] = cc(np.asarray(ins["b1"], dtype=f32).reshape(NFT, P).T)
    base["b2row"] = cc(np.asarray(ins["b2"], dtype=f32).reshape(1, D))
    base["bprojrow"] = cc(np.asarray(ins["bproj"], dtype=f32).reshape(1, D))
    base["identity"] = np.eye(P, dtype=f32)

    x_all = np.asarray(ins["x"], dtype=f32)
    n_cores = x_all.shape[0]
    in_maps = []
    for c in range(n_cores):
        m = dict(base)
        m["x"] = cc(x_all[c])
        in_maps.append(m)

    nc = _get_prog()
    res = run_bass_kernel_spmd(nc, in_maps, core_ids=list(range(n_cores)))
    out = np.stack([np.asarray(r["y"], dtype=np.float32) for r in res.results], axis=0)
    return out


# version 20
# speedup vs baseline: 1.0281x; 1.0003x over previous
import numpy as np
import ml_dtypes
from contextlib import ExitStack
import concourse.bass as bass
import concourse.mybir as mybir
from concourse.bass_utils import run_bass_kernel_spmd

F32 = mybir.dt.float32
BF16 = mybir.dt.bfloat16
AF = mybir.ActivationFunctionType
ALU = mybir.AluOpType
AX = mybir.AxisListType.X

P = 128
T, D, E, NS, RK, FF = 2048, 1024, 2048, 16, 64, 4096
TC = 512
NCH = T // TC
NET = E // P
NXT = 2 * E // P
NKD = D // P
NTT = T // P
NFT = FF // P
GRP = 4
NGR = NS // GRP
EPSQ = 1e-5
ENGS = ("sync", "gpsimd", "vector", "scalar", "tensor")


class Sem:
    def __init__(self, nc, ctx, name):
        self.h = ctx.enter_context(nc.semaphore(name))
        self.n = 0


class Bld:
    def __init__(self, nc):
        self.nc = nc
        self.st = {e: [] for e in ENGS}

    def em(self, eng, fn):
        self.st[eng].append(fn)

    def wait(self, eng, sem, v):
        if v and v > 0:
            self.em(eng, lambda e, h=sem.h, vv=v: e.wait_ge(h, vv))

    def dma(self, eng, sem, out, in_):
        sem.n += 16
        self.em(eng, lambda e, o=out, i=in_, h=sem.h:
                e.dma_start(out=o, in_=i).then_inc(h, 16))
        return sem.n

    def op(self, eng, fn, sem=None):
        if sem is None:
            self.em(eng, fn)
            return None
        sem.n += 1
        self.em(eng, lambda e, f=fn, h=sem.h: f(e).then_inc(h, 1))
        return sem.n

    def barrier(self, bar, dma_sems=()):
        for eng, sem in dma_sems:
            self.wait(eng, sem, sem.n)
        for eng in ENGS:
            self.em(eng, lambda e, h=bar.h: e.sem_inc(h, 1))
        bar.n += 5
        tgt = bar.n
        for eng in ENGS:
            self.wait(eng, bar, tgt)

    def flush(self):
        st = self.st
        self.st = {e: [] for e in ENGS}
        with self.nc.Block() as blk:
            for eng in ENGS:
                def runner(e, fns=tuple(st[eng])):
                    for f in fns:
                        f(e)
                getattr(blk, eng)(runner)


def build_program():
    nc = bass.Bass("TRN2", target_bir_lowering=False)
    dr = {}

    def din(name, shape):
        dr[name] = nc.dram_tensor(name, list(shape), F32, kind="ExternalInput").ap()

    din("x", (T, D))
    for s in "fb":
        din(f"win_sl_{s}", (NXT, P, D))
        din(f"wout_sl_{s}", (NET, P, D))
        din(f"wx_sl_{s}", (NET, P, 96))
        din(f"wdt_{s}", (RK, E))
        din(f"cvec_{s}", (P, 112))
        din(f"alog_{s}", (P, NET * NS))
    din("w1t_sl", (NKD, P, FF))
    din("w2t_sl", (NFT, P, D))
    din("wproj_sl", (NET, P, D))
    din("b1col", (P, NFT))
    din("b2row", (1, D))
    din("bprojrow", (1, D))
    din("identity", (P, P))

    cat_scr = nc.dram_tensor("cat_scr", [2 * D, T], BF16, kind="Internal").ap()
    bc_scr = nc.dram_tensor("bc_scr", [2, NS * TC], BF16, kind="Internal").ap()
    w1q_scr = nc.dram_tensor("w1q_scr", [NKD, P, FF], BF16, kind="Internal").ap()
    w2q_scr = nc.dram_tensor("w2q_scr", [NFT, P, D], BF16, kind="Internal").ap()
    h1_scr = nc.dram_tensor("h1_scr", [NFT, P, T], BF16, kind="Internal").ap()
    y_out = nc.dram_tensor("y", [T, D], F32, kind="ExternalOutput").ap()

    with ExitStack() as ctx:
        sPE = Sem(nc, ctx, "sPE")
        sV = Sem(nc, ctx, "sV")
        sA = Sem(nc, ctx, "sA")
        bar = Sem(nc, ctx, "bar")
        sBulkH = Sem(nc, ctx, "sBulkH")
        sBulkG = Sem(nc, ctx, "sBulkG")
        sX = [Sem(nc, ctx, f"sX{i}") for i in range(2)]
        sGam = [Sem(nc, ctx, f"sGam{i}") for i in range(2)]
        sPreL = [Sem(nc, ctx, f"sPreL{i}") for i in range(2)]
        sPreS = [Sem(nc, ctx, f"sPreS{i}") for i in range(2)]
        sWin = [Sem(nc, ctx, f"sWin{i}") for i in range(2)]
        sBC = Sem(nc, ctx, "sBC")
        sQl = Sem(nc, ctx, "sQl")
        sQs = Sem(nc, ctx, "sQs")
        sCat = [Sem(nc, ctx, f"sCat{i}") for i in range(2)]
        sW2s = [Sem(nc, ctx, f"sW2s{i}") for i in range(2)]
        sCatL = [Sem(nc, ctx, f"sCatL{i}") for i in range(2)]
        sSt = [Sem(nc, ctx, f"sSt{i}") for i in range(2)]
        sGp = Sem(nc, ctx, "sGp")
        sBC2 = Sem(nc, ctx, "sBC2")
        sBCr = Sem(nc, ctx, "sBCr")
        sBC2r = Sem(nc, ctx, "sBC2r")
        sCatG = [Sem(nc, ctx, f"sCatG{i}") for i in range(2)]
        sXG = [Sem(nc, ctx, f"sXG{i}") for i in range(2)]
        sWinG = [Sem(nc, ctx, f"sWinG{i}") for i in range(2)]
        sW1f = [Sem(nc, ctx, f"sW1f{i}") for i in range(2)]
        sH1p = [Sem(nc, ctx, f"sH1p{i}") for i in range(2)]
        sH1G = [Sem(nc, ctx, f"sH1G{i}") for i in range(2)]
        sH1G2 = [Sem(nc, ctx, f"sH1G2{i}") for i in range(2)]

        xn = ctx.enter_context(nc.sbuf_tensor("xn_cm", [P, NKD * T], BF16))
        ident = ctx.enter_context(nc.sbuf_tensor("ident", [P, P], BF16))
        gfin = ctx.enter_context(nc.sbuf_tensor("gfin", [P, 16], F32))

        px = [ctx.enter_context(nc.psum_tensor(f"px{i}", [P, TC], F32)) for i in range(2)]
        pdb = ctx.enter_context(nc.psum_tensor("pdb", [P, TC], F32))
        pd = [ctx.enter_context(nc.psum_tensor(f"pd{i}", [P, TC], F32)) for i in range(2)]
        py = [ctx.enter_context(nc.psum_tensor(f"py{i}", [P, TC], F32)) for i in range(2)]
        pw = ctx.enter_context(nc.psum_tensor("pw", [P, TC], F32))
        pall = [px[0], px[1], pdb, pd[0], pd[1], py[0], py[1], pw]

        b = Bld(nc)

        def xn_sl(k, t0, tl):
            return xn[:, k * T + t0: k * T + t0 + tl]

        with ExitStack() as p0:
            xst = p0.enter_context(nc.sbuf_tensor("xst", [P, 2 * D], F32))
            sqst = p0.enter_context(nc.sbuf_tensor("sqst", [P, D], F32))
            stats = p0.enter_context(nc.sbuf_tensor("stats", [P, 8 * NTT], F32))
            xnt = p0.enter_context(nc.sbuf_tensor("xnt", [P, 2 * D], F32))
            wstg = p0.enter_context(nc.sbuf_tensor("wstg", [P, 2 * FF], F32))
            gacc = p0.enter_context(nc.sbuf_tensor("gacc", [P, 48], F32))
            onesf = p0.enter_context(nc.sbuf_tensor("onesf", [P, P], F32))

            identf = p0.enter_context(nc.sbuf_tensor("identf", [P, P], F32))
            b.dma("gpsimd", sBulkG, ident[:], dr["identity"])
            v_id = b.dma("sync", sBulkH, identf[:], dr["identity"])
            b.op("vector", lambda e: e.memset(onesf[:], 1.0))
            b.op("vector", lambda e: e.memset(gfin[:, 15:16], 1e-5), sV)

            st1 = []
            for i in range(NTT):
                if i >= 2:
                    b.wait("sync", sV, st1[i - 2][0])
                    b.wait("sync", sA, st1[i - 2][1])
                sl = xst[:, (i % 2) * D:(i % 2 + 1) * D]
                vld = b.dma("sync", sX[i % 2], sl, dr["x"][i * P:(i + 1) * P, :])
                b.wait("vector", sX[i % 2], vld)
                vr = b.op("vector", lambda e, o=stats[:, i:i + 1], s=sl:
                          e.reduce_sum(out=o, in_=s, axis=AX), sV)
                b.wait("scalar", sX[i % 2], vld)
                if i >= 1:
                    b.wait("scalar", sA, st1[i - 1][1])
                va = b.op("scalar", lambda e, o=sqst[:], s=sl,
                          a=stats[:, NTT + i:NTT + i + 1]:
                          e.activation(out=o, in_=s, func=AF.Square, accum_out=a), sA)
                st1.append((vr, va))

            sum_c = stats[:, 0:NTT]
            m2_c = stats[:, NTT:2 * NTT]
            mean_c = stats[:, 2 * NTT:3 * NTT]
            q_c = stats[:, 3 * NTT:4 * NTT]
            var_c = stats[:, 4 * NTT:5 * NTT]
            std_c = stats[:, 5 * NTT:6 * NTT]
            rstd_c = stats[:, 6 * NTT:7 * NTT]
            nmr_c = stats[:, 7 * NTT:8 * NTT]
            b.wait("vector", sA, sA.n)
            b.wait("vector", sV, st1[-1][0])
            v_mean = b.op("vector", lambda e: e.tensor_scalar(out=mean_c, in0=sum_c,
                          scalar1=1.0 / D, scalar2=None, op0=ALU.mult), sV)
            b.wait("vector", sV, v_mean)
            v_q = b.op("vector", lambda e: e.tensor_mul(out=q_c, in0=mean_c,
                       in1=mean_c), sV)
            b.wait("vector", sV, v_q)
            b.op("vector", lambda e: e.scalar_tensor_tensor(out=var_c, in0=m2_c,
                 scalar=1.0 / D, in1=q_c, op0=ALU.mult, op1=ALU.subtract), sV)
            b.wait("scalar", sV, sV.n)
            va_std = b.op("scalar", lambda e: e.activation(out=std_c, in_=var_c,
                          func=AF.Sqrt, bias=gfin[:, 15:16]), sA)
            b.wait("vector", sA, va_std)
            v_rec = b.op("vector", lambda e: e.reciprocal(out=rstd_c, in_=std_c), sV)
            b.op("vector", lambda e: e.tensor_copy(out=sqst[:, 0:NTT], in_=std_c))
            b.wait("vector", sV, v_rec)
            v_nmr = b.op("vector", lambda e: e.scalar_tensor_tensor(out=nmr_c,
                         in0=mean_c, scalar=-1.0, in1=rstd_c, op0=ALU.mult,
                         op1=ALU.mult), sV)

            ev_tr = []
            ap2 = []
            b.wait("sync", sV, sV.n)
            b.wait("sync", sA, sA.n)
            b.wait("tensor", sBulkH, v_id)
            b.wait("tensor", sBulkG, 16)
            for i in range(NTT):
                if i >= 2:
                    b.wait("sync", sV, ap2[i - 2][0])
                sl = xst[:, (i % 2) * D:(i % 2 + 1) * D]
                xnsl = xnt[:, (i % 2) * D:(i % 2 + 1) * D]
                vld = b.dma("sync", sX[i % 2], sl, dr["x"][i * P:(i + 1) * P, :])
                b.wait("vector", sX[i % 2], vld)
                b.wait("vector", sV, v_nmr)
                if i >= 2:
                    b.wait("vector", sPE, ap2[i - 2][1])
                va = b.op("vector", lambda e, o=xnsl, s=sl, r=rstd_c[:, i:i + 1],
                          nm=nmr_c[:, i:i + 1]:
                          e.tensor_scalar(out=o, in0=s, scalar1=r, scalar2=nm,
                                          op0=ALU.mult, op1=ALU.add), sV)
                b.wait("tensor", sV, va)
                vpe_last = 0
                for k in range(NKD):
                    idx = i * NKD + k
                    bank = px[idx % 2]
                    if idx >= 2:
                        b.wait("tensor", sA, ev_tr[idx - 2])
                    vpe = b.op("tensor", lambda e, o=bank[:, 0:P],
                               inp=xnsl[:, k * P:(k + 1) * P]:
                               e.transpose(out=o, in_=inp, identity=identf[:]), sPE)
                    b.wait("scalar", sPE, vpe)
                    ev = b.op("scalar", lambda e, o=xn_sl(k, i * P, P), pb=bank:
                              e.activation(out=o, in_=pb[:, 0:P], func=AF.Copy), sA)
                    ev_tr.append(ev)
                    vpe_last = vpe
                ap2.append((va, vpe_last))


            gld = []
            for k in range(NKD):
                if k >= 2:
                    b.wait("gpsimd", sV, gld[k - 2])
                sl = wstg[:, (k % 2) * FF:(k % 2 + 1) * FF]
                vld = b.dma("gpsimd", sGam[k % 2], sl, dr["w1t_sl"][k])
                b.wait("vector", sGam[k % 2], vld)
                vr = b.op("vector", lambda e, o=gacc[:, k:k + 1], s=sl:
                          e.tensor_reduce(out=o, in_=s, axis=AX, op=ALU.add,
                                          apply_absolute_value=True), sV)
                gld.append(vr)
            g2 = []
            b.wait("gpsimd", sV, gld[-1])
            for j in range(NFT):
                if j >= 2:
                    b.wait("gpsimd", sV, g2[j - 2])
                sl = wstg[:, (j % 2) * D:(j % 2 + 1) * D]
                vld = b.dma("gpsimd", sGam[j % 2], sl, dr["w2t_sl"][j])
                b.wait("vector", sGam[j % 2], vld)
                vr = b.op("vector", lambda e, o=gacc[:, 8 + j:9 + j], s=sl:
                          e.tensor_reduce(out=o, in_=s, axis=AX, op=ALU.add,
                                          apply_absolute_value=True), sV)
                g2.append(vr)
            b.wait("vector", sV, gld[-1])
            b.wait("vector", sV, g2[-1])
            v_r1 = b.op("vector", lambda e: e.reduce_sum(out=gacc[:, 40:41],
                        in_=gacc[:, 0:NKD], axis=AX), sV)
            v_ga = b.op("vector", lambda e: e.reduce_sum(out=gacc[:, 41:42],
                        in_=gacc[:, 8:8 + NFT], axis=AX), sV)
            b.wait("tensor", sV, v_ga)
            vmm = b.op("tensor", lambda e: e.matmul(out=pw[:, 0:2], lhsT=onesf[:],
                       rhs=gacc[:, 40:42], start=True, stop=True), sPE)
            b.wait("vector", sPE, vmm)
            v_gc = b.op("vector", lambda e: e.tensor_copy(out=gfin[:, 10:12],
                        in_=pw[:, 0:2]), sV)
            b.wait("vector", sV, v_gc)
            v_t1 = b.op("vector", lambda e: e.tensor_scalar(out=gfin[:, 0:1],
                 in0=gfin[:, 10:11], scalar1=0.5 / float(D * FF), scalar2=0.5 * EPSQ,
                 op0=ALU.mult, op1=ALU.add), sV)
            v_t2 = b.op("vector", lambda e: e.tensor_scalar(out=gfin[:, 1:2],
                 in0=gfin[:, 11:12], scalar1=0.5 / float(FF * D), scalar2=0.5 * EPSQ,
                 op0=ALU.mult, op1=ALU.add), sV)
            b.wait("vector", sV, v_t2)
            b.op("vector", lambda e: e.tensor_scalar(out=gfin[:, 2:3], in0=gfin[:, 0:1],
                 scalar1=-1.0, scalar2=None, op0=ALU.mult), sV)
            b.op("vector", lambda e: e.tensor_scalar(out=gfin[:, 3:4], in0=gfin[:, 1:2],
                 scalar1=-1.0, scalar2=None, op0=ALU.mult), sV)
            b.barrier(bar, [("sync", sBulkH), ("sync", sX[0]), ("sync", sX[1]), ("sync", sPreS[0]), ("sync", sPreS[1]), ("sync", sWin[0]), ("sync", sWin[1]), ("sync", sW2s[0]), ("sync", sW2s[1]), ("sync", sCatL[0]), ("sync", sCatL[1]), ("gpsimd", sBulkG), ("gpsimd", sGam[0]), ("gpsimd", sGam[1]), ("gpsimd", sPreL[0]), ("gpsimd", sPreL[1]), ("gpsimd", sBC), ("gpsimd", sBC2), ("gpsimd", sBCr), ("gpsimd", sBC2r), ("gpsimd", sQl), ("gpsimd", sQs), ("gpsimd", sCatG[0]), ("gpsimd", sCatG[1]), ("gpsimd", sXG[0]), ("gpsimd", sXG[1]), ("gpsimd", sWinG[0]), ("gpsimd", sWinG[1]), ("gpsimd", sCat[0]), ("gpsimd", sCat[1])])
            b.flush()

        with ExitStack() as p12:
            wslot = p12.enter_context(nc.sbuf_tensor("wslot", [P, 2 * D], BF16))
            wout_sb = p12.enter_context(nc.sbuf_tensor("wout_sb", [P, NET * D], BF16))
            wx_sb = p12.enter_context(nc.sbuf_tensor("wx_sb", [P, 2 * NET * 96], BF16))
            wdt_sb = p12.enter_context(nc.sbuf_tensor("wdt_sb", [RK, E], BF16))
            cvec = p12.enter_context(nc.sbuf_tensor("cvec", [P, 2 * 112], F32))
            atmp = p12.enter_context(nc.sbuf_tensor("atmp", [P, NET * NS], F32))
            A_sb = p12.enter_context(nc.sbuf_tensor("A_sb", [P, 2 * NET * NS], F32))
            u_ext = p12.enter_context(nc.sbuf_tensor("u_ext", [P, 2 * (TC + 16)], BF16))
            utail = p12.enter_context(nc.sbuf_tensor("utail", [P, 4 * NET], BF16))
            cacc = p12.enter_context(nc.sbuf_tensor("cacc", [P, 2 * TC], BF16))
            u_sil = p12.enter_context(nc.sbuf_tensor("u_sil", [P, NET * TC], BF16))
            z_sb = p12.enter_context(nc.sbuf_tensor("z_sb", [P, NET * TC], BF16))
            dbc_sb = p12.enter_context(nc.sbuf_tensor("dbc_sb", [P, TC], BF16))
            B_bc = p12.enter_context(nc.sbuf_tensor("B_bc", [P, NS * TC], BF16))
            C_bc = p12.enter_context(nc.sbuf_tensor("C_bc", [P, NS * TC], BF16))
            delta_sb = p12.enter_context(nc.sbuf_tensor("delta_sb", [P, 2 * TC], BF16))
            spe = p12.enter_context(nc.sbuf_tensor("spe", [P, TC], BF16))
            du_sb = p12.enter_context(nc.sbuf_tensor("du_sb", [P, 2 * TC], BF16))
            TCP = TC + 2
            dA_sl = p12.enter_context(nc.sbuf_tensor("dA_sl", [P, 2 * GRP * TCP], BF16))
            h_sl = p12.enter_context(nc.sbuf_tensor("h_sl", [P, 2 * GRP * TCP], BF16))
            dBu_sb = p12.enter_context(nc.sbuf_tensor("dBu_sb", [P, GRP * TCP], BF16))
            hcar = p12.enter_context(nc.sbuf_tensor("hcar", [P, NET * NS], F32))
            ytmp = p12.enter_context(nc.sbuf_tensor("ytmp", [P, TC], BF16))
            Y_sb = p12.enter_context(nc.sbuf_tensor("Y_sb", [P, NET * TC], BF16))
            cstage = p12.enter_context(nc.sbuf_tensor("cstage", [P, 2 * TC], BF16))
            qst = p12.enter_context(nc.sbuf_tensor("qst", [P, TC], F32))
            qsg = p12.enter_context(nc.sbuf_tensor("qsg", [P, 2 * TC], BF16))
            w1f = p12.enter_context(nc.sbuf_tensor("w1f", [P, 2 * NKD * P], BF16))
            h1st = p12.enter_context(nc.sbuf_tensor("h1st", [P, 2 * TC], BF16))
            b1c12 = p12.enter_context(nc.sbuf_tensor("b1c12", [P, NFT], F32))

            jobs = []
            for half, s in enumerate("fb"):
                crange = range(NCH) if s == "f" else range(NCH - 1, -1, -1)
                for c in crange:
                    jobs.append((s, c, half))

            qunits = [("w1", k, q) for k in range(NKD) for q in range(8)] + \
                     [("w2", jj, q) for jj in range(NFT) for q in range(2)]
            qstate = {"ld": 0, "cp": 0, "sign": 0, "store": 0, "vld": {}, "stv": {}}

            def q_src_dst(i):
                kind, a, q = qunits[i]
                if kind == "w1":
                    return (dr["w1t_sl"][a][:, q * TC:(q + 1) * TC],
                            w1q_scr[a][:, q * TC:(q + 1) * TC],
                            gfin[:, 2:3], gfin[:, 0:1])
                return (dr["w2t_sl"][a][:, q * TC:(q + 1) * TC],
                        w2q_scr[a][:, q * TC:(q + 1) * TC],
                        gfin[:, 3:4], gfin[:, 1:2])

            def pump_quant(nu):
                for _ in range(nu):
                    if qstate["cp"] < qstate["ld"]:
                        i = qstate["cp"]
                        qstate["cp"] = i + 1
                        srcp, dst, thn, thp = q_src_dst(i)
                        qsl = qst[:, 0:TC]
                        s1 = qsg[:, 0:TC]
                        s2 = qsg[:, TC:2 * TC]
                        b.wait("scalar", sQl, qstate["vld"][i])
                        b.wait("scalar", sQs, qstate["store"])
                        b.op("scalar", lambda e, o=s1, s=qsl, t=thn:
                             e.activation(out=o, in_=s, func=AF.Sign, bias=t))
                        qstate["sign"] = b.op("scalar", lambda e, o=s2, s=qsl, t=thp:
                                              e.activation(out=o, in_=s, func=AF.Sign,
                                                           bias=t), sA)
                        b.wait("vector", sA, qstate["sign"])
                        v_ad = b.op("vector", lambda e, o=s1, a1=s1, a2=s2:
                                    e.tensor_add(out=o, in0=a1, in1=a2), sV)
                        b.wait("vector", sV, v_ad)
                        v_ts = b.op("vector", lambda e, o=s1, a1=s1:
                                    e.tensor_scalar(out=o, in0=a1, scalar1=0.5,
                                                    scalar2=None, op0=ALU.mult), sV)
                        b.wait("gpsimd", sV, v_ts)
                        qstate["store"] = b.dma("gpsimd", sQs, dst, s1)
                        qstate["stv"][i] = qstate["store"]
                    if qstate["ld"] < len(qunits):
                        i = qstate["ld"]
                        qstate["ld"] = i + 1
                        srcp, dst, thn, thp = q_src_dst(i)
                        qsl = qst[:, 0:TC]
                        if i >= 1:
                            b.wait("gpsimd", sA, qstate["sign"])
                        qstate["vld"][i] = b.dma("gpsimd", sQl, qsl, srcp)

            h1s = {"qm": 0, "qe": 0, "vld": {}, "mmv": {}, "mmft": {},
                   "relu": {}, "stv": {}}

            def h1_evac():
                    if h1s["qe"] < h1s["qm"]:
                        q = h1s["qe"]
                        h1s["qe"] = q + 1
                        ft, c = q // NCH, q % NCH
                        b.wait("scalar", sPE, h1s["mmv"][q])
                        pst = h1s["stv"].get(q - 2)
                        if pst:
                            b.wait("scalar", sH1p[q % 2], pst)
                        h1s["relu"][q] = b.op("scalar", lambda e,
                            o=h1st[:, (q % 2) * TC:(q % 2 + 1) * TC], pb=pw,
                            bc=b1c12[:, ft:ft + 1]:
                            e.activation(out=o, in_=pb[:], func=AF.Relu, bias=bc),
                            sA)
                        b.wait("sync", sA, h1s["relu"][q])
                        h1s["stv"][q] = b.dma("sync", sH1p[q % 2],
                                              h1_scr[ft][:, c * TC:(c + 1) * TC],
                                              h1st[:, (q % 2) * TC:(q % 2 + 1) * TC])

            def pump_h1(nu):
                for _ in range(nu):
                    h1_evac()
                    q = h1s["qm"]
                    if q >= NFT * NCH:
                        continue
                    h1s["qm"] = q + 1
                    ft, c = q // NCH, q % NCH
                    fpar = ft % 2
                    if c == 0:
                        prev_mm = h1s["mmft"].get(ft - 2)
                        if prev_mm:
                            b.wait("sync", sPE, prev_mm)
                        b.wait("sync", sQs, qstate["stv"][7 * NKD + ft // 4])
                        h1s["vld"][ft] = b.dma("sync", sW1f[fpar],
                            w1f[:, fpar * NKD * P:(fpar + 1) * NKD * P],
                            w1q_scr[0:NKD, :, ft * P:(ft + 1) * P]
                            .transpose([1, 0, 2]))
                        b.wait("tensor", sW1f[fpar], h1s["vld"][ft])
                    pr = h1s["relu"].get(q - 1)
                    wv = mk["wout_ev"].get(NKD - 1)
                    tgt = max(pr or 0, wv or 0)
                    if tgt:
                        b.wait("tensor", sA, tgt)
                    for k in range(NKD):
                        lastk = (k == NKD - 1)
                        v = b.op("tensor", lambda e, o=pw[:],
                                 w=w1f[:, fpar * NKD * P + k * P:
                                       fpar * NKD * P + (k + 1) * P],
                                 r=xn_sl(k, c * TC, TC), st=(k == 0), sp=lastk:
                                 e.matmul(out=o, lhsT=w, rhs=r, start=st, stop=sp),
                                 sPE if lastk else None)
                    h1s["mmv"][q] = v
                    h1s["mmft"][ft] = v

            mk = {
                "xz_mm": {}, "xz_ev": {}, "conv": {}, "usil": {},
                "dbc_ev": {}, "bc_dma": {}, "minev": {}, "lnsp": {}, "dA": {},
                "scan": {}, "yacc": {}, "gate": {}, "wout_last": {}, "wout_ev": {},
                "cat_dma": {}, "delta_mm": {}, "zsil": {},
                "bc_users": {}, "prep": {}, "b_dma": {},
                "b_rest": {}, "bc_rest": {},
            }
            GG = {"n": 0}
            WINLD = {"cnt": [0, 0]}

            def dirpar(j):
                return jobs[j][2]

            def emit_scan_memsets(fwd):
                bk = 0 if fwd else (TC + 1)
                fl = 1 if fwd else TC
                v = None
                for gp in range(2):
                    base = gp * GRP * TCP
                    v = b.op("vector", lambda e, o=dA_sl[:, base + bk:base + bk
                             + (GRP - 1) * TCP + 1:TCP]: e.memset(o, 0.0), sV)
                    v = b.op("vector", lambda e, o=dA_sl[:, base + fl:base + fl
                             + (GRP - 1) * TCP + 1:TCP]: e.memset(o, 1.0), sV)
                v = b.op("vector", lambda e, o=dBu_sb[:, fl:fl
                         + (GRP - 1) * TCP + 1:TCP]: e.memset(o, 0.0), sV)
                return v

            def emit_dir_consts(s, half):
                b.dma("sync", sBulkH, cvec[:, half * 112:(half + 1) * 112],
                      dr[f"cvec_{s}"])
                v_al = b.dma("sync", sBulkH, atmp[:], dr[f"alog_{s}"])
                b.wait("scalar", sBulkH, v_al)
                asl = A_sb[:, half * NET * NS:(half + 1) * NET * NS]
                v_ex = b.op("scalar", lambda e, o=asl: e.activation(
                    out=o, in_=atmp[:], func=AF.Exp), sA)
                b.wait("vector", sA, v_ex)
                v_neg = b.op("vector", lambda e, o=asl: e.tensor_scalar(
                    out=o, in0=o, scalar1=-1.0, scalar2=None, op0=ALU.mult), sV)
                for jj in range(2):
                    b.dma("gpsimd", sBulkG,
                          wx_sb[:, (half * NET + jj * 8) * 96:(half * NET + (jj + 1) * 8) * 96],
                          dr[f"wx_sl_{s}"][8 * jj:8 * (jj + 1)].transpose([1, 0, 2]))
                return v_neg, sBulkH.n, sBulkG.n

            def emit_wdt_load(s, gate_pe):
                if gate_pe:
                    b.wait("gpsimd", sPE, gate_pe)
                b.dma("gpsimd", sBulkG, wdt_sb[:], dr[f"wdt_{s}"])
                return sBulkG.n

            def emit_wout_load(s, gate_pe):
                b.wait("gpsimd", sPE, gate_pe)
                for jj in range(NKD):
                    b.dma("gpsimd", sBulkG, wout_sb[:, jj * 2 * D:(jj + 1) * 2 * D],
                          dr[f"wout_sl_{s}"][2 * jj:2 * (jj + 1)].transpose([1, 0, 2]))
                return sBulkG.n

            def emit_A_mm(j, et):
                s, c, half = jobs[j]
                fwd = (s == "f")
                t0 = c * TC
                par = et % 2
                for xt, bank in ((et, px[par]), (NET + et, pdb)):
                    pk = WINLD["cnt"][par]
                    prev_use = mk["xz_mm"].get(("ld", par, pk - 1))
                    if prev_use:
                        b.wait("gpsimd", sPE, prev_use)
                    vld = b.dma("gpsimd", sWinG[par], wslot[:, par * D:(par + 1) * D],
                                dr[f"win_sl_{s}"][xt])
                    WINLD["cnt"][par] = pk + 1
                    b.wait("tensor", sWinG[par], vld)
                    if xt < NET:
                        prev_ev = mk["xz_ev"].get((j, et - 2)) or \
                            (mk["xz_ev"].get((j - 1, NET - 1)) if et < 2 else None)
                    else:
                        prev_ev = mk["zsil"].get((j, et - 1)) or \
                            (mk["zsil"].get((j - 1, NET - 1)) if et == 0 else None)
                    if prev_ev:
                        b.wait("tensor", sA, prev_ev)
                    for k in range(NKD):
                        last = (k == NKD - 1)
                        v = b.op("tensor", lambda e, o=bank[:],
                                 w=wslot[:, par * D + k * P:par * D + (k + 1) * P],
                                 r=xn_sl(k, t0, TC), st=(k == 0), sp=last:
                                 e.matmul(out=o, lhsT=w, rhs=r, start=st, stop=sp),
                                 sPE if last else None)
                    mk["xz_mm"][("ld", par, pk)] = v
                    mk["xz_mm"][("mm", j, xt)] = v

            def emit_A_evac_u(j, et):
                s, c, half = jobs[j]
                fwd = (s == "f")
                par = et % 2
                uo = u_ext[:, par * (TC + 16) + (3 if fwd else 0):
                           par * (TC + 16) + (3 if fwd else 0) + TC]
                b.wait("scalar", sPE, mk["xz_mm"][("mm", j, et)])
                prev_conv = mk["conv"].get((j, et - 2)) or \
                    (mk["conv"].get((j - 1, NET - 2 + et)) if et < 2 else None)
                if prev_conv:
                    b.wait("scalar", sV, prev_conv)
                if j > 0:
                    g8 = mk["gate"].get((j - 1, et))
                    if g8:
                        b.wait("scalar", sV, g8)
                mk["xz_ev"][(j, et)] = b.op("scalar", lambda e, o=uo,
                    pb=px[par]: e.activation(out=o, in_=pb[:], func=AF.Copy), sA)

            def emit_A_evac_z(j, et):
                b.wait("scalar", sPE, mk["xz_mm"][("mm", j, NET + et)])
                if j > 0:
                    g8 = mk["gate"].get((j - 1, et))
                    if g8:
                        b.wait("scalar", sV, g8)
                mk["zsil"][(j, et)] = b.op("scalar", lambda e,
                    o=z_sb[:, et * TC:(et + 1) * TC], pb=pdb:
                    e.activation(out=o, in_=pb[:], func=AF.Copy), sA)

            def emit_A_conv(j, et):
                s, c, half = jobs[j]
                fwd = (s == "f")
                par = et % 2
                b.wait("vector", sA, mk["xz_ev"][(j, et)])
                prev_sil = mk["usil"].get((j, et - 2)) or \
                    (mk["usil"].get((j - 1, NET - 2 + et)) if et < 2 else None)
                if prev_sil:
                    b.wait("vector", sA, prev_sil)
                ue0 = par * (TC + 16)
                tl = utail[:, et * 4:et * 4 + 3]
                if fwd:
                    v_h = b.op("vector", lambda e, o=u_ext[:, ue0:ue0 + 3], i=tl:
                               e.tensor_copy(out=o, in_=i), sV)
                else:
                    v_h = b.op("vector", lambda e, o=u_ext[:, ue0 + TC:ue0 + TC + 3],
                               i=tl: e.tensor_copy(out=o, in_=i), sV)
                ca = cacc[:, par * TC:(par + 1) * TC]
                wks = [half * 112 + 16 + et * 4 + k for k in range(4)]
                order = [0, 1, 2, 3] if fwd else [3, 2, 1, 0]
                b.wait("vector", sV, v_h)
                v_c = b.op("vector", lambda e, o=ca, i=u_ext[:, ue0:ue0 + TC],
                           w=cvec[:, wks[order[0]]:wks[order[0]] + 1],
                           bc=cvec[:, half * 112 + et:half * 112 + et + 1]:
                           e.tensor_scalar(out=o, in0=i, scalar1=w, scalar2=bc,
                                           op0=ALU.mult, op1=ALU.add), sV)
                for jj in (1, 2, 3):
                    b.wait("vector", sV, v_c)
                    dst = ca if jj < 3 else u_sil[:, et * TC:(et + 1) * TC]
                    v_c = b.op("vector", lambda e, o=dst,
                               i=u_ext[:, ue0 + jj:ue0 + jj + TC],
                               w=cvec[:, wks[order[jj]]:wks[order[jj]] + 1], acc=ca:
                               e.scalar_tensor_tensor(out=o, in0=i, scalar=w,
                                                      in1=acc, op0=ALU.mult,
                                                      op1=ALU.add), sV)
                b.wait("vector", sV, v_c)
                if fwd:
                    mk["conv"][(j, et)] = b.op("vector", lambda e, o=tl,
                        i=u_ext[:, ue0 + TC:ue0 + TC + 3]:
                        e.tensor_copy(out=o, in_=i), sV)
                else:
                    mk["conv"][(j, et)] = b.op("vector", lambda e, o=tl,
                        i=u_ext[:, ue0:ue0 + 3]:
                        e.tensor_copy(out=o, in_=i), sV)

            def emit_silu_batch(j, ets, zets):
                s, c, half = jobs[j]
                if ets:
                    b.wait("scalar", sV, mk["conv"][(j, ets[-1])])
                if zets:
                    b.wait("scalar", sA, mk["zsil"][(j, zets[-1])])
                if ets:
                    e0, e1 = ets[0], ets[-1]
                    assert ets == list(range(e0, e1 + 1))
                    vu = b.op("scalar", lambda e,
                              o=u_sil[:, e0 * TC:(e1 + 1) * TC]:
                              e.activation(out=o, in_=o, func=AF.Silu), sA)
                    for et in ets:
                        mk["usil"][(j, et)] = vu
                if zets:
                    e0, e1 = zets[0], zets[-1]
                    assert zets == list(range(e0, e1 + 1))
                    vz = b.op("scalar", lambda e,
                              o=z_sb[:, e0 * TC:(e1 + 1) * TC]:
                              e.activation(out=o, in_=o, func=AF.Silu), sA)
                    for et in zets:
                        mk["usil"][(j, NET + et)] = vz


            def emit_dbc(j):
                s, c, half = jobs[j]
                b.wait("tensor", sA, mk["usil"][(j, NET - 1)])
                b.wait("tensor", sA, mk["zsil"][(j, NET - 1)])
                if j > 0:
                    b.wait("tensor", sA, mk["dbc_ev"][j - 1])
                for et in range(NET):
                    last = (et == NET - 1)
                    v = b.op("tensor", lambda e, o=pdb[0:96, :],
                             w=wx_sb[:, (half * NET + et) * 96:(half * NET + et + 1) * 96],
                             r=u_sil[:, et * TC:(et + 1) * TC],
                             st=(et == 0), sp=last:
                             e.matmul(out=o, lhsT=w, rhs=r, start=st, stop=sp),
                             sPE if last else None)
                b.wait("scalar", sPE, v)
                if j > 0:
                    b.wait("scalar", sBC2r, mk["bc_rest"][j - 1])
                mk["dbc_ev"][j] = b.op("scalar", lambda e:
                                       e.activation(out=dbc_sb[0:96, 0:TC],
                                                    in_=pdb[0:96, :], func=AF.Copy), sA)
                b.wait("gpsimd", sA, mk["dbc_ev"][j])
                if j > 0:
                    b.wait("gpsimd", sV, mk["bc_users"][j - 1])
                vf1 = b.dma("gpsimd", sBC, bc_scr[0:1, :], dbc_sb[RK:RK + NS, 0:TC])
                vf2 = b.dma("gpsimd", sBC, bc_scr[1:2, :],
                            dbc_sb[RK + NS:RK + 2 * NS, 0:TC])
                b.wait("gpsimd", sBC, vf2)
                Q1 = GRP * TC
                mk["b_dma"][j] = b.dma("gpsimd", sBC, B_bc[:, 0:Q1],
                                       bc_scr[0:1, 0:Q1]
                                       .unsqueeze(1).broadcast_to([1, P, Q1]))
                mk["bc_dma"][j] = b.dma("gpsimd", sBC2, C_bc[:, 0:Q1],
                                        bc_scr[1:2, 0:Q1]
                                        .unsqueeze(1).broadcast_to([1, P, Q1]))
                mk["b_rest"][j] = b.dma("gpsimd", sBCr, B_bc[:, Q1:NS * TC],
                                        bc_scr[0:1, Q1:NS * TC]
                                        .unsqueeze(1)
                                        .broadcast_to([1, P, NS * TC - Q1]))
                mk["bc_rest"][j] = b.dma("gpsimd", sBC2r, C_bc[:, Q1:NS * TC],
                                         bc_scr[1:2, Q1:NS * TC]
                                         .unsqueeze(1)
                                         .broadcast_to([1, P, NS * TC - Q1]))

            def emit_delta(j, et):
                s, c, half = jobs[j]
                par = et % 2
                b.wait("tensor", sA, mk["dbc_ev"][j])
                wl = mk.get("wdt_pend")
                if wl is not None and j >= 4:
                    b.wait("tensor", sBulkG, wl)
                    mk["wdt_pend"] = None
                prev = mk["minev"].get((j, et - 2)) or \
                    (mk["minev"].get((j - 1, NET - 2 + et)) if et < 2 else None)
                if prev:
                    b.wait("tensor", sV, prev)
                mk["delta_mm"][(j, et)] = b.op("tensor", lambda e, o=pd[par][:],
                    w=wdt_sb[:, et * P:(et + 1) * P],
                    r=dbc_sb[0:RK, 0:TC]:
                    e.matmul(out=o, lhsT=w, rhs=r, start=True, stop=True), sPE)

            def emit_prep_v(j, et):
                s, c, half = jobs[j]
                par = et % 2
                dsl = delta_sb[:, par * TC:(par + 1) * TC]
                b.wait("vector", sPE, mk["delta_mm"][(j, et)])
                prev_dA = mk["dA"].get((j, et - 2, NGR - 1)) or \
                    (mk["dA"].get((j - 1, NET - 2 + et, NGR - 1)) if et < 2 else None)
                if prev_dA:
                    b.wait("vector", sA, prev_dA)
                mk["minev"][(j, et)] = b.op("vector", lambda e, o=dsl, pb=pd[par],
                    bd=cvec[:, half * 112 + 80 + et:half * 112 + 81 + et]:
                    e.tensor_scalar(out=o, in0=pb[:], scalar1=bd, scalar2=25.0,
                                    op0=ALU.add, op1=ALU.min), sV)

            def emit_prep_a(j, et):
                par = et % 2
                dsl = delta_sb[:, par * TC:(par + 1) * TC]
                b.wait("scalar", sV, mk["minev"][(j, et)])
                v_se = b.op("scalar", lambda e, o=spe[:], i=dsl:
                            e.activation(out=o, in_=i, func=AF.Exp), sA)
                b.wait("scalar", sA, v_se)
                mk["lnsp"][(j, et)] = b.op("scalar", lambda e, o=dsl, i=spe[:]:
                                           e.activation(out=o, in_=i, func=AF.Ln,
                                                        bias=1.0), sA)

            def emit_B_et(j, et):
                s, c, half = jobs[j]
                fwd = (s == "f")
                par = et % 2
                dsl = delta_sb[:, par * TC:(par + 1) * TC]
                dusl = du_sb[:, par * TC:(par + 1) * TC]
                b.wait("vector", sA, mk["lnsp"][(j, et)])
                b.wait("scalar", sA, mk["lnsp"][(j, et)])
                v_du = b.op("vector", lambda e, o=dusl, a1=dsl,
                            a2=u_sil[:, et * TC:(et + 1) * TC]:
                            e.tensor_mul(out=o, in0=a1, in1=a2), sV)
                b.wait("vector", sV, v_du)
                for g in range(NGR):
                    gg = GG["n"]
                    GG["n"] += 1
                    gp = gg % 2
                    if gg >= 2:
                        b.wait("scalar", sV, mk["scan"][gg - 2])
                    doff = 2 if fwd else 0
                    for jj in range(GRP):
                        n = g * GRP + jj
                        last = (jj == GRP - 1)
                        sl0 = (gp * GRP + jj) * TCP + doff
                        v = b.op("scalar", lambda e,
                                 o=dA_sl[:, sl0:sl0 + TC],
                                 i=dsl, sc=A_sb[:, half * NET * NS + et * NS + n:
                                                half * NET * NS + et * NS + n + 1]:
                                 e.activation(out=o, in_=i, func=AF.Exp, scale=sc),
                                 sA if last else None)
                    mk["dA"][(j, et, g)] = v
                    b.wait("vector", sA, v)
                    if gg >= 2:
                        b.wait("vector", sPE, mk["yacc"][gg - 2])
                    if et == 0 and g == 0:
                        b.wait("vector", sBC, mk["b_dma"][j])
                    if et == 0 and g == 1:
                        b.wait("vector", sBCr, mk["b_rest"][j])
                    base = gp * GRP * TCP
                    v_db = b.op("vector", lambda e,
                                o=dBu_sb[:].rearrange("p (s t) -> p s t", s=GRP)
                                [:, :, doff:doff + TC],
                                a1=dusl.unsqueeze(1).broadcast_to([P, GRP, TC]),
                                a2=B_bc[:, g * GRP * TC:(g + 1) * GRP * TC]
                                .rearrange("p (s t) -> p s t", s=GRP):
                                e.tensor_mul(out=o, in0=a1, in1=a2), sV)
                    b.wait("vector", sV, v_db)
                    for jj in range(GRP):
                        n = g * GRP + jj
                        sl0 = (gp * GRP + jj) * TCP + doff
                        db0 = jj * TCP + doff
                        ic = hcar[:, et * NS + n:et * NS + n + 1]
                        if fwd:
                            v_sc = b.op("vector", lambda e, o=h_sl[:, sl0:sl0 + TC],
                                        d0=dA_sl[:, sl0:sl0 + TC],
                                        d1=dBu_sb[:, db0:db0 + TC], i=ic:
                                        e.tensor_tensor_scan(out=o, data0=d0, data1=d1,
                                        initial=i, op0=ALU.mult, op1=ALU.add),
                                        sV if jj == GRP - 1 else None)
                        else:
                            v_sc = b.op("vector", lambda e, o=h_sl[:, sl0:sl0 + TC],
                                        d0=dA_sl[:, sl0:sl0 + TC],
                                        d1=dBu_sb[:, db0:db0 + TC], i=ic:
                                        e.tensor_tensor_scan(out=o[:, ::-1],
                                        data0=d0[:, ::-1], data1=d1[:, ::-1],
                                        initial=i, op0=ALU.mult, op1=ALU.add),
                                        sV if jj == GRP - 1 else None)
                    b.wait("vector", sV, v_sc)
                    off = (doff + TC - 1) if fwd else doff
                    v_tl = b.op("vector", lambda e,
                                o=hcar[:, et * NS + g * GRP:et * NS + (g + 1) * GRP],
                                i=h_sl[:, base + off:base + off
                                       + (GRP - 1) * TCP + 1:TCP]:
                                e.tensor_copy(out=o, in_=i), sV)
                    b.wait("vector", sV, v_tl)
                    if et == 0 and g == 0:
                        b.wait("vector", sBC2, mk["bc_dma"][j])
                    if et == 0 and g == 1:
                        b.wait("vector", sBC2r, mk["bc_rest"][j])
                    v = b.op("vector", lambda e,
                             o=h_sl[:, base:base + GRP * TCP]
                             .rearrange("p (s t) -> p s t", s=GRP)[:, :, doff:doff + TC],
                             a2=C_bc[:, g * GRP * TC:(g + 1) * GRP * TC]
                             .rearrange("p (s t) -> p s t", s=GRP):
                             e.tensor_mul(out=o, in0=o, in1=a2), sV)
                    mk["scan"][gg] = v
                    b.wait("tensor", sV, mk["scan"][gg])
                    if g == 0:
                        prev_gate = mk["gate"].get((j, et - 2)) or \
                            (mk["gate"].get((j - 1, NET - 2 + et)) if et < 2 else None)
                        if prev_gate:
                            b.wait("tensor", sV, prev_gate)
                    for jj in range(GRP):
                        n = g * GRP + jj
                        last = (jj == GRP - 1)
                        sl0 = (gp * GRP + jj) * TCP + doff
                        v = b.op("tensor", lambda e, o=py[par][:],
                                 r=h_sl[:, sl0:sl0 + TC],
                                 st=(n == 0), sp=(n == NS - 1):
                                 e.matmul(out=o, lhsT=ident[:], rhs=r, start=st,
                                          stop=sp), sPE if last else None)
                    mk["yacc"][gg] = v
                    if g == 0 and et + 1 < NET:
                        emit_prep_v(j, et + 1)
                    if g == 1:
                        if j < 7 and et >= 1:
                            emit_A_evac_u(j + 1, et - 1)
                        pump_quant(1)
                        if j >= 4:
                            pump_h1(1)
                    if g == 2:
                        if j < 7 and et >= 1:
                            emit_A_conv(j + 1, et - 1)
                        if et + 1 < NET:
                            emit_prep_a(j, et + 1)
                    if g == 3:
                        if j < 7 and et >= 1:
                            emit_A_evac_z(j + 1, et - 1)
                        if j >= 4:
                            pump_h1(1)
                        if j < 7 and et == NET - 1:
                            emit_silu_batch(j + 1, list(range(NET - 1)),
                                            list(range(NET - 1)))
                b.wait("vector", sPE, mk["yacc"][GG["n"] - 1])
                v_g1 = b.op("vector", lambda e, o=ytmp[:],
                            i=u_sil[:, et * TC:(et + 1) * TC],
                            dp=cvec[:, half * 112 + 96 + et:half * 112 + 97 + et],
                            pb=py[par]:
                            e.scalar_tensor_tensor(out=o, in0=i, scalar=dp, in1=pb[:],
                                                   op0=ALU.mult, op1=ALU.add), sV)
                b.wait("vector", sV, v_g1)
                mk["gate"][(j, et)] = b.op("vector", lambda e,
                    o=Y_sb[:, et * TC:(et + 1) * TC], a1=ytmp[:],
                    a2=z_sb[:, et * TC:(et + 1) * TC]:
                    e.tensor_mul(out=o, in0=a1, in1=a2), sV)
                mk["bc_users"][j] = mk["scan"][GG["n"] - 1]


            def emit_C(j):
                s, c, half = jobs[j]
                fwd = (s == "f")
                t0 = c * TC
                row0 = 0 if fwd else D
                b.wait("tensor", sV, mk["gate"][(j, NET - 1)])
                hv = h1s["relu"].get(h1s["qe"] - 1)
                if hv:
                    b.wait("tensor", sA, hv)
                if j > 0:
                    b.wait("tensor", sBulkG, mk["wout_ld"])
                for dt in range(NKD):
                    if dt > 0:
                        b.wait("tensor", sA, mk["wout_ev"][dt - 1])
                    elif j > 0:
                        b.wait("tensor", sA, mk["wout_ev"][NKD - 1])
                    for et in range(NET):
                        last = (et == NET - 1)
                        v = b.op("tensor", lambda e, o=pw[:],
                                 w=wout_sb[:, et * D + dt * P:et * D + (dt + 1) * P],
                                 r=Y_sb[:, et * TC:(et + 1) * TC],
                                 st=(et == 0), sp=last:
                                 e.matmul(out=o, lhsT=w, rhs=r, start=st, stop=sp),
                                 sPE if last else None)
                    b.wait("scalar", sPE, v)
                    mk["wout_last"][j] = v
                    if dt >= 2:
                        b.wait("scalar", sCat[dt % 2], mk["cat_dma"][dt - 2])
                    elif j > 0:
                        b.wait("scalar", sCat[dt % 2], mk["cat_dma"][NKD - 2 + dt])
                    mk["wout_ev"][dt] = b.op("scalar", lambda e,
                        o=cstage[:, (dt % 2) * TC:(dt % 2 + 1) * TC], pb=pw:
                        e.activation(out=o, in_=pb[:], func=AF.Copy), sA)
                    b.wait("gpsimd", sA, mk["wout_ev"][dt])
                    mk["cat_dma"][dt] = b.dma("gpsimd", sCat[dt % 2],
                        cat_scr[row0 + dt * P:row0 + (dt + 1) * P, t0:t0 + TC],
                        cstage[:, (dt % 2) * TC:(dt % 2 + 1) * TC])

            b.dma("gpsimd", sBulkG, b1c12[:], dr["b1col"])
            v_neg, vbh, vbg = emit_dir_consts("f", 0)
            emit_wdt_load("f", 0)
            mk["wout_ld"] = emit_wout_load("f", 0)
            vbg = sBulkG.n
            for eng in ENGS:
                b.wait(eng, sBulkH, vbh)
                b.wait(eng, sBulkG, vbg)
                b.wait(eng, sV, v_neg)
            v_ms = b.op("vector", lambda e: e.memset(hcar[:], 0.0), sV)
            v_ms = b.op("vector", lambda e: e.memset(utail[:], 0.0), sV)
            v_ms = emit_scan_memsets(True)
            b.wait("scalar", sV, v_ms)
            b.wait("tensor", sV, v_ms)

            for et in range(NET):
                emit_A_mm(0, et)
                emit_A_evac_u(0, et)
                emit_A_evac_z(0, et)
                emit_A_conv(0, et)
            emit_silu_batch(0, list(range(NET)), list(range(NET)))
            emit_dbc(0)
            emit_delta(0, 0)
            emit_delta(0, 1)
            emit_prep_v(0, 0)
            emit_prep_a(0, 0)

            for j in range(8):
                b.wait("vector", sV, sV.n)
                b.wait("scalar", sA, sA.n)
                if j == 3:
                    v_neg, vbh, vbg = emit_dir_consts("b", 1)
                    for eng in ("vector", "scalar", "tensor", "sync", "gpsimd"):
                        b.wait(eng, sBulkH, vbh)
                        b.wait(eng, sBulkG, vbg)
                        b.wait(eng, sV, v_neg)
                    v_ms = b.op("vector", lambda e: e.memset(utail[:], 0.0), sV)
                    b.wait("scalar", sV, v_ms)
                    mk["bwd_ms"] = None
                if j >= 5:
                    h1_evac()
                if j >= 1:
                    emit_C(j - 1)
                if j == 4:
                    v_ms = b.op("vector", lambda e: e.memset(hcar[:], 0.0), sV)
                    emit_scan_memsets(False)
                    mk["wout_ld"] = emit_wout_load("b", mk["wout_last"][3])
                for et in range(NET):
                    emit_B_et(j, et)
                    if et + 2 < NET:
                        emit_delta(j, et + 2)
                    if j == 3 and et == 14:
                        mk["wdt_pend"] = emit_wdt_load("b",
                                                       mk["delta_mm"][(3, 15)])
                    if j < 7:
                        emit_A_mm(j + 1, et)
                if j < 7:
                    emit_A_evac_u(j + 1, NET - 1)
                    emit_A_evac_z(j + 1, NET - 1)
                    emit_A_conv(j + 1, NET - 1)
                    emit_silu_batch(j + 1, [NET - 1], [NET - 1])
                    emit_dbc(j + 1)
                    emit_delta(j + 1, 0)
                    emit_delta(j + 1, 1)
                    emit_prep_v(j + 1, 0)
                    emit_prep_a(j + 1, 0)
            pump_quant(4)
            pump_h1(1)
            emit_C(7)
            b.barrier(bar, [("sync", sBulkH), ("sync", sX[0]), ("sync", sX[1]), ("sync", sPreS[0]), ("sync", sPreS[1]), ("sync", sWin[0]), ("sync", sWin[1]), ("sync", sW2s[0]), ("sync", sW2s[1]), ("sync", sCatL[0]), ("sync", sCatL[1]), ("sync", sW1f[0]), ("sync", sW1f[1]), ("sync", sH1p[0]), ("sync", sH1p[1]), ("gpsimd", sBulkG), ("gpsimd", sGam[0]), ("gpsimd", sGam[1]), ("gpsimd", sPreL[0]), ("gpsimd", sPreL[1]), ("gpsimd", sBC), ("gpsimd", sBC2), ("gpsimd", sBCr), ("gpsimd", sBC2r), ("gpsimd", sQl), ("gpsimd", sQs), ("gpsimd", sCatG[0]), ("gpsimd", sCatG[1]), ("gpsimd", sXG[0]), ("gpsimd", sXG[1]), ("gpsimd", sWinG[0]), ("gpsimd", sWinG[1]), ("gpsimd", sCat[0]), ("gpsimd", sCat[1])])
            b.flush()
        with ExitStack() as p34:
            b2bc = p34.enter_context(nc.sbuf_tensor("b2bc", [P, D], F32))
            bpbc = p34.enter_context(nc.sbuf_tensor("bpbc", [P, D], F32))
            gstage = p34.enter_context(nc.sbuf_tensor("gstage", [P, 2 * TC], BF16))
            w2q_sb = p34.enter_context(nc.sbuf_tensor("w2q_sb", [P, NFT * D], BF16))
            h1sl = p34.enter_context(nc.sbuf_tensor("h1sl", [P, 8 * TC], BF16))
            g_sb = p34.enter_context(nc.sbuf_tensor("g_sb", [P, 4 * D], BF16))
            wproj_sb = p34.enter_context(nc.sbuf_tensor("wproj_sb", [P, NET * D], BF16))
            catsl = p34.enter_context(nc.sbuf_tensor("catsl", [P, 2 * NET * P], BF16))
            xsl = p34.enter_context(nc.sbuf_tensor("xsl", [P, 2 * D], F32))
            sstage = p34.enter_context(nc.sbuf_tensor("sstage", [P, 2 * TC], BF16))

            b.dma("gpsimd", sBulkG, b2bc[:],
                  dr["b2row"].unsqueeze(1).broadcast_to([1, P, D]))
            b.dma("gpsimd", sBulkG, bpbc[:],
                  dr["bprojrow"].unsqueeze(1).broadcast_to([1, P, D]))
            for jj in range(NKD):
                b.dma("gpsimd", sBulkG, wproj_sb[:, jj * 2 * D:(jj + 1) * 2 * D],
                      dr["wproj_sl"][2 * jj:2 * (jj + 1)].transpose([1, 0, 2]))
            v_wp = sBulkG.n
            w2half = []
            for hl in range(2):
                for fp in range(NFT // 4):
                    fpp = hl * (NFT // 4) + fp
                    v = b.dma("sync", sW2s[hl],
                              w2q_sb[:, 2 * fpp * D:(2 * fpp + 2) * D],
                              w2q_scr[2 * fpp:2 * fpp + 2].transpose([1, 0, 2]))
                w2half.append(v)

            HB = TC // 2
            h1mm = {}
            gsil = {}
            pj_mm = {}
            pj_out = {}
            pj_st = {}
            PASSES = [([pdb, py[0], py[1], pw], [(0, 0), (0, 1), (1, 0), (1, 1)],
                       sH1G),
                      ([px[0], px[1], pd[0], pd[1]],
                       [(2, 0), (2, 1), (3, 0), (3, 1)], sH1G2)]
            NG4 = NFT // 4
            for c in range(NCH):
                t0 = c * TC
                for ps, (banks, tiles, sems) in enumerate(PASSES):
                    if ps == 0:
                        for i4, (tt, hf) in enumerate(tiles):
                            pv = gsil.get((c - 1, tt, hf))
                            if pv:
                                b.wait("tensor", sA, pv)
                    else:
                        if c > 0:
                            b.wait("tensor", sV, pj_out[4 * (c - 1) + 2])
                            b.wait("tensor", sV, pj_out[4 * (c - 1) + 3])
                    for g4 in range(NG4):
                        par = g4 % 2
                        prev = h1mm.get((c, ps, g4 - 2)) or \
                               (h1mm.get((c - 1, ps, NG4 - 2 + g4))
                                if g4 < 2 and c > 0 else None)
                        if prev:
                            b.wait("gpsimd", sPE, prev)
                        vh = b.dma("gpsimd", sems[par],
                                   h1sl[:, (ps * 2 + par) * 4 * HB:
                                        (ps * 2 + par + 1) * 4 * HB],
                                   h1_scr[4 * g4:4 * (g4 + 1),
                                          :, t0 + ps * 2 * P:t0 + ps * 2 * P + 2 * P]
                                   .transpose([1, 0, 2]))
                        b.wait("tensor", sems[par], vh)
                        if c == 0 and ps == 0:
                            b.wait("tensor", sW2s[g4 // (NG4 // 2)],
                                   w2half[g4 // (NG4 // 2)])
                        for sub in range(4):
                            ft = 4 * g4 + sub
                            for i4, (tt, hf) in enumerate(tiles):
                                last = (i4 == 3 and sub == 3)
                                v = b.op("tensor", lambda e, o=banks[i4][:],
                                         w=h1sl[:, (ps * 2 + par) * 4 * HB
                                                + sub * HB + (tt - 2 * ps) * P:
                                                (ps * 2 + par) * 4 * HB
                                                + sub * HB + (tt - 2 * ps + 1) * P],
                                         r=w2q_sb[:, ft * D + hf * TC:
                                                  ft * D + (hf + 1) * TC],
                                         st=(ft == 0), sp=(ft == NFT - 1):
                                         e.matmul(out=o, lhsT=w, rhs=r, start=st,
                                                  stop=sp), sPE if last else None)
                        h1mm[(c, ps, g4)] = v
                    b.wait("vector", sPE, v)
                    if c == 0 and ps == 0:
                        b.wait("vector", sBulkG, v_wp)
                    for i4, (tt, hf) in enumerate(tiles):
                        gsl = gstage[:, hf * TC:(hf + 1) * TC]
                        ps_prev = gsil.get((c, tt - 1, hf)) if i4 >= 2 else \
                            (gsil.get((c - 1, 3, hf)) if ps == 0 and c > 0 else
                             gsil.get((c, 1, hf)) if ps == 1 else None)
                        if ps_prev:
                            b.wait("vector", sA, ps_prev)
                        vv = b.op("vector", lambda e, o=gsl, pb=banks[i4],
                                  bb=b2bc[:, hf * TC:(hf + 1) * TC]:
                                  e.tensor_add(out=o, in0=pb[:], in1=bb), sV)
                        b.wait("scalar", sV, vv)
                        if c > 0:
                            b.wait("scalar", sV, pj_out[4 * (c - 1) + tt])
                        gsil[(c, tt, hf)] = b.op("scalar", lambda e,
                            o=g_sb[:, tt * D + hf * TC:tt * D + (hf + 1) * TC],
                            i=gsl:
                            e.activation(out=o, in_=i, func=AF.Silu), sA)
                b.wait("tensor", sBulkG, v_wp)
                for tl in range(4):
                    ti = 4 * c + tl
                    par = ti % 2
                    prev = pj_mm.get(ti - 2)
                    if prev:
                        b.wait("gpsimd", sPE, prev)
                    if ti >= 2:
                        b.wait("gpsimd", sV, pj_out[ti - 2])
                        b.wait("gpsimd", sSt[par], pj_st[ti - 2])
                    vc = b.dma("gpsimd", sCatG[par],
                               catsl[:, par * NET * P:(par + 1) * NET * P],
                               cat_scr[:, ti * P:(ti + 1) * P]
                               .rearrange("(k p) t -> p k t", p=P))
                    vx = b.dma("gpsimd", sXG[par], xsl[:, par * D:(par + 1) * D],
                               dr["x"][ti * P:(ti + 1) * P, :])
                    b.wait("tensor", sCatG[par], vc)
                    if ti >= 2:
                        b.wait("tensor", sV, pj_out[ti - 2])
                    pjbanks = [px[0], px[1]] if tl % 2 == 0 else [pd[0], pd[1]]
                    pjtt = 2 if tl % 2 == 0 else 3
                    for hf in range(2):
                        bank = pjbanks[hf]
                        b.wait("tensor", sA, gsil[(c, pjtt, hf)])
                        for k in range(16):
                            last = (k == 15)
                            v = b.op("tensor", lambda e, o=bank[:],
                                     w=catsl[:, par * NET * P + k * P:
                                             par * NET * P + (k + 1) * P],
                                     r=wproj_sb[:, k * D + hf * TC:
                                                k * D + (hf + 1) * TC],
                                     st=(k == 0), sp=last:
                                     e.matmul(out=o, lhsT=w, rhs=r, start=st,
                                              stop=sp),
                                     sPE if last else None)
                    pj_mm[ti] = v
                    b.wait("vector", sPE, v)
                    b.wait("vector", sXG[par], vx)
                    b.wait("vector", sA, gsil[(c, tl, 1)])
                    if ti >= 2:
                        b.wait("vector", sSt[par], pj_st[ti - 2])
                    for hf in range(2):
                        bank = pjbanks[hf]
                        ssl = sstage[:, hf * TC:(hf + 1) * TC]
                        v_a = b.op("vector", lambda e, o=ssl, pb=bank,
                                   bb=bpbc[:, hf * TC:(hf + 1) * TC]:
                                   e.tensor_add(out=o, in0=pb[:], in1=bb), sV)
                        b.wait("vector", sV, v_a)
                        v_m = b.op("vector", lambda e, o=ssl, a1=ssl,
                                   a2=g_sb[:, tl * D + hf * TC:
                                           tl * D + (hf + 1) * TC]:
                                   e.tensor_mul(out=o, in0=a1, in1=a2), sV)
                        b.wait("vector", sV, v_m)
                        v2 = b.op("vector", lambda e,
                                  o=xsl[:, par * D + hf * TC:par * D + (hf + 1) * TC],
                                  a1=ssl, a2=xsl[:, par * D + hf * TC:
                                                 par * D + (hf + 1) * TC]:
                                  e.tensor_add(out=o, in0=a1, in1=a2), sV)
                    pj_out[ti] = v2
                    b.wait("sync", sV, v2)
                    pj_st[ti] = b.dma("sync", sSt[par], y_out[ti * P:(ti + 1) * P, :],
                                      xsl[:, par * D:(par + 1) * D])
            b.wait("sync", sSt[0], sSt[0].n)
            b.wait("sync", sSt[1], sSt[1].n)
            b.flush()
    return nc


_PROG = None


def _get_prog():
    global _PROG
    if _PROG is None:
        _PROG = build_program()
    return _PROG


def kernel(**inputs):
    f32 = np.float32

    def cc(a):
        return np.ascontiguousarray(a, dtype=f32)

    ins = {k[len("L0_attn_"):]: v for k, v in inputs.items() if k.startswith("L0_attn_")}
    ins.update({k[len("L0_ffn_"):]: v for k, v in inputs.items() if k.startswith("L0_ffn_")})

    base = {}
    for s in "fb":
        win = np.asarray(ins[f"win_{s}"], dtype=f32)
        base[f"win_sl_{s}"] = cc(win.reshape(NXT, P, NKD, P).transpose(0, 3, 2, 1)
                                 .reshape(NXT, P, D))
        base[f"wout_sl_{s}"] = cc(np.asarray(ins[f"wout_{s}"], dtype=f32).T
                                  .reshape(NET, P, D))
        base[f"wx_sl_{s}"] = cc(np.asarray(ins[f"wx_{s}"], dtype=f32).T
                                .reshape(NET, P, 96))
        base[f"wdt_{s}"] = cc(np.asarray(ins[f"wdt_{s}"], dtype=f32).T)
        bconv = np.asarray(ins[f"bconv_{s}"], dtype=f32).reshape(NET, P).T
        wconv = np.asarray(ins[f"wconv_{s}"], dtype=f32)[:, 0, :] \
            .reshape(NET, P, 4).transpose(1, 0, 2).reshape(P, NET * 4)
        bdt = np.asarray(ins[f"bdt_{s}"], dtype=f32).reshape(NET, P).T
        Dp = np.asarray(ins[f"Dp_{s}"], dtype=f32).reshape(NET, P).T
        base[f"cvec_{s}"] = cc(np.concatenate([bconv, wconv, bdt, Dp], axis=1))
        base[f"alog_{s}"] = cc(np.asarray(ins[f"Alog_{s}"], dtype=f32)
                               .reshape(NET, P, NS).transpose(1, 0, 2).reshape(P, NET * NS))
    base["w1t_sl"] = cc(np.asarray(ins["W1"], dtype=f32).T.reshape(NKD, P, FF))
    base["w2t_sl"] = cc(np.asarray(ins["W2"], dtype=f32).T.reshape(NFT, P, D))
    base["wproj_sl"] = cc(np.asarray(ins["wproj"], dtype=f32).T.reshape(NET, P, D))
    base["b1col"] = cc(np.asarray(ins["b1"], dtype=f32).reshape(NFT, P).T)
    base["b2row"] = cc(np.asarray(ins["b2"], dtype=f32).reshape(1, D))
    base["bprojrow"] = cc(np.asarray(ins["bproj"], dtype=f32).reshape(1, D))
    base["identity"] = np.eye(P, dtype=f32)

    x_all = np.asarray(ins["x"], dtype=f32)
    n_cores = x_all.shape[0]
    in_maps = []
    for c in range(n_cores):
        m = dict(base)
        m["x"] = cc(x_all[c])
        in_maps.append(m)

    nc = _get_prog()
    res = run_bass_kernel_spmd(nc, in_maps, core_ids=list(range(n_cores)))
    out = np.stack([np.asarray(r["y"], dtype=np.float32) for r in res.results], axis=0)
    return out


# version 21
# speedup vs baseline: 1.0580x; 1.0292x over previous
import numpy as np
import ml_dtypes
from contextlib import ExitStack
import concourse.bass as bass
import concourse.mybir as mybir
from concourse.bass_utils import run_bass_kernel_spmd

F32 = mybir.dt.float32
BF16 = mybir.dt.bfloat16
AF = mybir.ActivationFunctionType
ALU = mybir.AluOpType
AX = mybir.AxisListType.X

P = 128
T, D, E, NS, RK, FF = 2048, 1024, 2048, 16, 64, 4096
TC = 512
NCH = T // TC
NET = E // P
NXT = 2 * E // P
NKD = D // P
NTT = T // P
NFT = FF // P
GRP = 4
NGR = NS // GRP
EPSQ = 1e-5
ENGS = ("sync", "gpsimd", "vector", "scalar", "tensor")


class Sem:
    def __init__(self, nc, ctx, name):
        self.h = ctx.enter_context(nc.semaphore(name))
        self.n = 0


class Bld:
    def __init__(self, nc):
        self.nc = nc
        self.st = {e: [] for e in ENGS}

    def em(self, eng, fn):
        self.st[eng].append(fn)

    def wait(self, eng, sem, v):
        if v and v > 0:
            self.em(eng, lambda e, h=sem.h, vv=v: e.wait_ge(h, vv))

    def dma(self, eng, sem, out, in_):
        sem.n += 16
        self.em(eng, lambda e, o=out, i=in_, h=sem.h:
                e.dma_start(out=o, in_=i).then_inc(h, 16))
        return sem.n

    def op(self, eng, fn, sem=None):
        if sem is None:
            self.em(eng, fn)
            return None
        sem.n += 1
        self.em(eng, lambda e, f=fn, h=sem.h: f(e).then_inc(h, 1))
        return sem.n

    def barrier(self, bar, dma_sems=()):
        for eng, sem in dma_sems:
            self.wait(eng, sem, sem.n)
        for eng in ENGS:
            self.em(eng, lambda e, h=bar.h: e.sem_inc(h, 1))
        bar.n += 5
        tgt = bar.n
        for eng in ENGS:
            self.wait(eng, bar, tgt)

    def flush(self):
        st = self.st
        self.st = {e: [] for e in ENGS}
        with self.nc.Block() as blk:
            for eng in ENGS:
                def runner(e, fns=tuple(st[eng])):
                    for f in fns:
                        f(e)
                getattr(blk, eng)(runner)


def build_program():
    nc = bass.Bass("TRN2", target_bir_lowering=False)
    dr = {}

    def din(name, shape):
        dr[name] = nc.dram_tensor(name, list(shape), F32, kind="ExternalInput").ap()

    din("x", (T, D))
    for s in "fb":
        din(f"win_sl_{s}", (NXT, P, D))
        din(f"wout_sl_{s}", (NET, P, D))
        din(f"wx_sl_{s}", (NET, P, 96))
        din(f"wdt_{s}", (RK, E))
        din(f"cvec_{s}", (P, 112))
        din(f"alog_{s}", (P, NET * NS))
    din("w1t_sl", (NKD, P, FF))
    din("w2t_sl", (NFT, P, D))
    din("wproj_sl", (NET, P, D))
    din("b1col", (P, NFT))
    din("b2row", (1, D))
    din("bprojrow", (1, D))
    din("identity", (P, P))

    cat_scr = nc.dram_tensor("cat_scr", [2 * D, T], BF16, kind="Internal").ap()
    bc_scr = nc.dram_tensor("bc_scr", [2, NS * TC], BF16, kind="Internal").ap()
    w1q_scr = nc.dram_tensor("w1q_scr", [NKD, P, FF], BF16, kind="Internal").ap()
    w2q_scr = nc.dram_tensor("w2q_scr", [NFT, P, D], BF16, kind="Internal").ap()
    h1_scr = nc.dram_tensor("h1_scr", [NFT, P, T], BF16, kind="Internal").ap()
    y_out = nc.dram_tensor("y", [T, D], F32, kind="ExternalOutput").ap()

    with ExitStack() as ctx:
        sPE = Sem(nc, ctx, "sPE")
        sV = Sem(nc, ctx, "sV")
        sA = Sem(nc, ctx, "sA")
        bar = Sem(nc, ctx, "bar")
        sBulkH = Sem(nc, ctx, "sBulkH")
        sBulkG = Sem(nc, ctx, "sBulkG")
        sX = [Sem(nc, ctx, f"sX{i}") for i in range(2)]
        sGam = [Sem(nc, ctx, f"sGam{i}") for i in range(2)]
        sPreL = [Sem(nc, ctx, f"sPreL{i}") for i in range(2)]
        sPreS = [Sem(nc, ctx, f"sPreS{i}") for i in range(2)]
        sWin = [Sem(nc, ctx, f"sWin{i}") for i in range(2)]
        sBC = Sem(nc, ctx, "sBC")
        sQl = Sem(nc, ctx, "sQl")
        sQs = Sem(nc, ctx, "sQs")
        sCat = [Sem(nc, ctx, f"sCat{i}") for i in range(2)]
        sW2s = [Sem(nc, ctx, f"sW2s{i}") for i in range(2)]
        sCatL = [Sem(nc, ctx, f"sCatL{i}") for i in range(2)]
        sSt = [Sem(nc, ctx, f"sSt{i}") for i in range(2)]
        sGp = Sem(nc, ctx, "sGp")
        sBC2 = Sem(nc, ctx, "sBC2")
        sBCr = Sem(nc, ctx, "sBCr")
        sBC2r = Sem(nc, ctx, "sBC2r")
        sCatG = [Sem(nc, ctx, f"sCatG{i}") for i in range(2)]
        sXG = [Sem(nc, ctx, f"sXG{i}") for i in range(2)]
        sWinG = [Sem(nc, ctx, f"sWinG{i}") for i in range(2)]
        sW1f = [Sem(nc, ctx, f"sW1f{i}") for i in range(2)]
        sH1p = [Sem(nc, ctx, f"sH1p{i}") for i in range(2)]
        sH1G = [Sem(nc, ctx, f"sH1G{i}") for i in range(2)]
        sH1G2 = [Sem(nc, ctx, f"sH1G2{i}") for i in range(2)]

        xn = ctx.enter_context(nc.sbuf_tensor("xn_cm", [P, NKD * T], BF16))
        ident = ctx.enter_context(nc.sbuf_tensor("ident", [P, P], BF16))
        gfin = ctx.enter_context(nc.sbuf_tensor("gfin", [P, 16], F32))

        px = [ctx.enter_context(nc.psum_tensor(f"px{i}", [P, TC], F32)) for i in range(2)]
        pdb = ctx.enter_context(nc.psum_tensor("pdb", [P, TC], F32))
        pd = [ctx.enter_context(nc.psum_tensor(f"pd{i}", [P, TC], F32)) for i in range(2)]
        py = [ctx.enter_context(nc.psum_tensor(f"py{i}", [P, TC], F32)) for i in range(2)]
        pw = ctx.enter_context(nc.psum_tensor("pw", [P, TC], F32))
        pall = [px[0], px[1], pdb, pd[0], pd[1], py[0], py[1], pw]

        b = Bld(nc)

        def xn_sl(k, t0, tl):
            return xn[:, k * T + t0: k * T + t0 + tl]

        with ExitStack() as p0:
            xst = p0.enter_context(nc.sbuf_tensor("xst", [P, 2 * D], F32))
            sqst = p0.enter_context(nc.sbuf_tensor("sqst", [P, D], F32))
            stats = p0.enter_context(nc.sbuf_tensor("stats", [P, 8 * NTT], F32))
            xnt = p0.enter_context(nc.sbuf_tensor("xnt", [P, 2 * D], F32))
            wstg = p0.enter_context(nc.sbuf_tensor("wstg", [P, 2 * FF], F32))
            gacc = p0.enter_context(nc.sbuf_tensor("gacc", [P, 48], F32))
            onesf = p0.enter_context(nc.sbuf_tensor("onesf", [P, P], F32))

            identf = p0.enter_context(nc.sbuf_tensor("identf", [P, P], F32))
            b.dma("gpsimd", sBulkG, ident[:], dr["identity"])
            v_id = b.dma("sync", sBulkH, identf[:], dr["identity"])
            b.op("vector", lambda e: e.memset(onesf[:], 1.0))
            b.op("vector", lambda e: e.memset(gfin[:, 15:16], 1e-5), sV)

            st1 = []
            for i in range(NTT):
                if i >= 2:
                    b.wait("sync", sV, st1[i - 2][0])
                    b.wait("sync", sA, st1[i - 2][1])
                sl = xst[:, (i % 2) * D:(i % 2 + 1) * D]
                vld = b.dma("sync", sX[i % 2], sl, dr["x"][i * P:(i + 1) * P, :])
                b.wait("vector", sX[i % 2], vld)
                vr = b.op("vector", lambda e, o=stats[:, i:i + 1], s=sl:
                          e.reduce_sum(out=o, in_=s, axis=AX), sV)
                b.wait("scalar", sX[i % 2], vld)
                if i >= 1:
                    b.wait("scalar", sA, st1[i - 1][1])
                va = b.op("scalar", lambda e, o=sqst[:], s=sl,
                          a=stats[:, NTT + i:NTT + i + 1]:
                          e.activation(out=o, in_=s, func=AF.Square, accum_out=a), sA)
                st1.append((vr, va))

            sum_c = stats[:, 0:NTT]
            m2_c = stats[:, NTT:2 * NTT]
            mean_c = stats[:, 2 * NTT:3 * NTT]
            q_c = stats[:, 3 * NTT:4 * NTT]
            var_c = stats[:, 4 * NTT:5 * NTT]
            std_c = stats[:, 5 * NTT:6 * NTT]
            rstd_c = stats[:, 6 * NTT:7 * NTT]
            nmr_c = stats[:, 7 * NTT:8 * NTT]
            b.wait("vector", sA, sA.n)
            b.wait("vector", sV, st1[-1][0])
            v_mean = b.op("vector", lambda e: e.tensor_scalar(out=mean_c, in0=sum_c,
                          scalar1=1.0 / D, scalar2=None, op0=ALU.mult), sV)
            b.wait("vector", sV, v_mean)
            v_q = b.op("vector", lambda e: e.tensor_mul(out=q_c, in0=mean_c,
                       in1=mean_c), sV)
            b.wait("vector", sV, v_q)
            b.op("vector", lambda e: e.scalar_tensor_tensor(out=var_c, in0=m2_c,
                 scalar=1.0 / D, in1=q_c, op0=ALU.mult, op1=ALU.subtract), sV)
            b.wait("scalar", sV, sV.n)
            va_std = b.op("scalar", lambda e: e.activation(out=std_c, in_=var_c,
                          func=AF.Sqrt, bias=gfin[:, 15:16]), sA)
            b.wait("vector", sA, va_std)
            v_rec = b.op("vector", lambda e: e.reciprocal(out=rstd_c, in_=std_c), sV)
            b.op("vector", lambda e: e.tensor_copy(out=sqst[:, 0:NTT], in_=std_c))
            b.wait("vector", sV, v_rec)
            v_nmr = b.op("vector", lambda e: e.scalar_tensor_tensor(out=nmr_c,
                         in0=mean_c, scalar=-1.0, in1=rstd_c, op0=ALU.mult,
                         op1=ALU.mult), sV)

            ev_tr = []
            ap2 = []
            b.wait("sync", sV, sV.n)
            b.wait("sync", sA, sA.n)
            b.wait("tensor", sBulkH, v_id)
            b.wait("tensor", sBulkG, 16)
            for i in range(NTT):
                if i >= 2:
                    b.wait("sync", sV, ap2[i - 2][0])
                sl = xst[:, (i % 2) * D:(i % 2 + 1) * D]
                xnsl = xnt[:, (i % 2) * D:(i % 2 + 1) * D]
                vld = b.dma("sync", sX[i % 2], sl, dr["x"][i * P:(i + 1) * P, :])
                b.wait("vector", sX[i % 2], vld)
                b.wait("vector", sV, v_nmr)
                if i >= 2:
                    b.wait("vector", sPE, ap2[i - 2][1])
                va = b.op("vector", lambda e, o=xnsl, s=sl, r=rstd_c[:, i:i + 1],
                          nm=nmr_c[:, i:i + 1]:
                          e.tensor_scalar(out=o, in0=s, scalar1=r, scalar2=nm,
                                          op0=ALU.mult, op1=ALU.add), sV)
                b.wait("tensor", sV, va)
                vpe_last = 0
                for k in range(NKD):
                    idx = i * NKD + k
                    bank = px[idx % 2]
                    if idx >= 2:
                        b.wait("tensor", sA, ev_tr[idx - 2])
                    vpe = b.op("tensor", lambda e, o=bank[:, 0:P],
                               inp=xnsl[:, k * P:(k + 1) * P]:
                               e.transpose(out=o, in_=inp, identity=identf[:]), sPE)
                    b.wait("scalar", sPE, vpe)
                    ev = b.op("scalar", lambda e, o=xn_sl(k, i * P, P), pb=bank:
                              e.activation(out=o, in_=pb[:, 0:P], func=AF.Copy), sA)
                    ev_tr.append(ev)
                    vpe_last = vpe
                ap2.append((va, vpe_last))


            gld = []
            for k in range(NKD):
                if k >= 2:
                    b.wait("gpsimd", sV, gld[k - 2])
                sl = wstg[:, (k % 2) * FF:(k % 2 + 1) * FF]
                vld = b.dma("gpsimd", sGam[k % 2], sl, dr["w1t_sl"][k])
                b.wait("vector", sGam[k % 2], vld)
                vr = b.op("vector", lambda e, o=gacc[:, k:k + 1], s=sl:
                          e.tensor_reduce(out=o, in_=s, axis=AX, op=ALU.add,
                                          apply_absolute_value=True), sV)
                gld.append(vr)
            g2 = []
            b.wait("gpsimd", sV, gld[-1])
            for j in range(NFT):
                if j >= 2:
                    b.wait("gpsimd", sV, g2[j - 2])
                sl = wstg[:, (j % 2) * D:(j % 2 + 1) * D]
                vld = b.dma("gpsimd", sGam[j % 2], sl, dr["w2t_sl"][j])
                b.wait("vector", sGam[j % 2], vld)
                vr = b.op("vector", lambda e, o=gacc[:, 8 + j:9 + j], s=sl:
                          e.tensor_reduce(out=o, in_=s, axis=AX, op=ALU.add,
                                          apply_absolute_value=True), sV)
                g2.append(vr)
            b.wait("vector", sV, gld[-1])
            b.wait("vector", sV, g2[-1])
            v_r1 = b.op("vector", lambda e: e.reduce_sum(out=gacc[:, 40:41],
                        in_=gacc[:, 0:NKD], axis=AX), sV)
            v_ga = b.op("vector", lambda e: e.reduce_sum(out=gacc[:, 41:42],
                        in_=gacc[:, 8:8 + NFT], axis=AX), sV)
            b.wait("tensor", sV, v_ga)
            vmm = b.op("tensor", lambda e: e.matmul(out=pw[:, 0:2], lhsT=onesf[:],
                       rhs=gacc[:, 40:42], start=True, stop=True), sPE)
            b.wait("vector", sPE, vmm)
            v_gc = b.op("vector", lambda e: e.tensor_copy(out=gfin[:, 10:12],
                        in_=pw[:, 0:2]), sV)
            b.wait("vector", sV, v_gc)
            v_t1 = b.op("vector", lambda e: e.tensor_scalar(out=gfin[:, 0:1],
                 in0=gfin[:, 10:11], scalar1=0.5 / float(D * FF), scalar2=0.5 * EPSQ,
                 op0=ALU.mult, op1=ALU.add), sV)
            v_t2 = b.op("vector", lambda e: e.tensor_scalar(out=gfin[:, 1:2],
                 in0=gfin[:, 11:12], scalar1=0.5 / float(FF * D), scalar2=0.5 * EPSQ,
                 op0=ALU.mult, op1=ALU.add), sV)
            b.wait("vector", sV, v_t2)
            b.op("vector", lambda e: e.tensor_scalar(out=gfin[:, 2:3], in0=gfin[:, 0:1],
                 scalar1=-1.0, scalar2=None, op0=ALU.mult), sV)
            b.op("vector", lambda e: e.tensor_scalar(out=gfin[:, 3:4], in0=gfin[:, 1:2],
                 scalar1=-1.0, scalar2=None, op0=ALU.mult), sV)
            b.barrier(bar, [("sync", sBulkH), ("sync", sX[0]), ("sync", sX[1]), ("sync", sPreS[0]), ("sync", sPreS[1]), ("sync", sWin[0]), ("sync", sWin[1]), ("sync", sW2s[0]), ("sync", sW2s[1]), ("sync", sCatL[0]), ("sync", sCatL[1]), ("gpsimd", sBulkG), ("gpsimd", sGam[0]), ("gpsimd", sGam[1]), ("gpsimd", sPreL[0]), ("gpsimd", sPreL[1]), ("gpsimd", sBC), ("gpsimd", sBC2), ("gpsimd", sBCr), ("gpsimd", sBC2r), ("gpsimd", sQl), ("gpsimd", sQs), ("gpsimd", sCatG[0]), ("gpsimd", sCatG[1]), ("gpsimd", sXG[0]), ("gpsimd", sXG[1]), ("gpsimd", sWinG[0]), ("gpsimd", sWinG[1]), ("gpsimd", sCat[0]), ("gpsimd", sCat[1])])
            b.flush()

        with ExitStack() as p12:
            wslot = p12.enter_context(nc.sbuf_tensor("wslot", [P, 2 * D], BF16))
            wout_sb = p12.enter_context(nc.sbuf_tensor("wout_sb", [P, NET * D], BF16))
            wx_sb = p12.enter_context(nc.sbuf_tensor("wx_sb", [P, 2 * NET * 96], BF16))
            wdt_sb = p12.enter_context(nc.sbuf_tensor("wdt_sb", [RK, E], BF16))
            cvec = p12.enter_context(nc.sbuf_tensor("cvec", [P, 2 * 112], F32))
            atmp = p12.enter_context(nc.sbuf_tensor("atmp", [P, NET * NS], F32))
            A_sb = p12.enter_context(nc.sbuf_tensor("A_sb", [P, 2 * NET * NS], F32))
            u_ext = p12.enter_context(nc.sbuf_tensor("u_ext", [P, 2 * (TC + 16)], BF16))
            utail = p12.enter_context(nc.sbuf_tensor("utail", [P, 4 * NET], BF16))
            cacc = p12.enter_context(nc.sbuf_tensor("cacc", [P, 2 * TC], BF16))
            u_sil = p12.enter_context(nc.sbuf_tensor("u_sil", [P, NET * TC], BF16))
            z_sb = p12.enter_context(nc.sbuf_tensor("z_sb", [P, NET * TC], BF16))
            dbc_sb = p12.enter_context(nc.sbuf_tensor("dbc_sb", [P, TC], BF16))
            B_bc = p12.enter_context(nc.sbuf_tensor("B_bc", [P, NS * TC], BF16))
            C_bc = p12.enter_context(nc.sbuf_tensor("C_bc", [P, NS * TC], BF16))
            delta_sb = p12.enter_context(nc.sbuf_tensor("delta_sb", [P, 2 * TC], BF16))
            spe = p12.enter_context(nc.sbuf_tensor("spe", [P, TC], BF16))
            du_sb = p12.enter_context(nc.sbuf_tensor("du_sb", [P, 2 * TC], BF16))
            TCP = TC + 2
            dA_sl = p12.enter_context(nc.sbuf_tensor("dA_sl", [P, 2 * GRP * TCP], BF16))
            h_sl = p12.enter_context(nc.sbuf_tensor("h_sl", [P, 2 * GRP * TCP], BF16))
            dBu_sb = p12.enter_context(nc.sbuf_tensor("dBu_sb", [P, GRP * TCP], BF16))
            hcar = p12.enter_context(nc.sbuf_tensor("hcar", [P, NET * NS], F32))
            ytmp = p12.enter_context(nc.sbuf_tensor("ytmp", [P, TC], BF16))
            Y_sb = p12.enter_context(nc.sbuf_tensor("Y_sb", [P, NET * TC], BF16))
            cstage = p12.enter_context(nc.sbuf_tensor("cstage", [P, 2 * TC], BF16))
            qst = p12.enter_context(nc.sbuf_tensor("qst", [P, TC], F32))
            qsg = p12.enter_context(nc.sbuf_tensor("qsg", [P, 2 * TC], BF16))
            w1f = p12.enter_context(nc.sbuf_tensor("w1f", [P, 2 * NKD * P], BF16))
            h1st = p12.enter_context(nc.sbuf_tensor("h1st", [P, 2 * TC], BF16))
            b1c12 = p12.enter_context(nc.sbuf_tensor("b1c12", [P, NFT], F32))

            jobs = []
            for half, s in enumerate("fb"):
                crange = range(NCH) if s == "f" else range(NCH - 1, -1, -1)
                for c in crange:
                    jobs.append((s, c, half))

            qunits = [("w1", k, q) for k in range(NKD) for q in range(8)] + \
                     [("w2", jj, q) for jj in range(NFT) for q in range(2)]
            qstate = {"ld": 0, "cp": 0, "sign": 0, "store": 0, "vld": {}, "stv": {}}

            def q_src_dst(i):
                kind, a, q = qunits[i]
                if kind == "w1":
                    return (dr["w1t_sl"][a][:, q * TC:(q + 1) * TC],
                            w1q_scr[a][:, q * TC:(q + 1) * TC],
                            gfin[:, 2:3], gfin[:, 0:1])
                return (dr["w2t_sl"][a][:, q * TC:(q + 1) * TC],
                        w2q_scr[a][:, q * TC:(q + 1) * TC],
                        gfin[:, 3:4], gfin[:, 1:2])

            def pump_quant(nu):
                for _ in range(nu):
                    if qstate["cp"] < qstate["ld"]:
                        i = qstate["cp"]
                        qstate["cp"] = i + 1
                        srcp, dst, thn, thp = q_src_dst(i)
                        qsl = qst[:, 0:TC]
                        s1 = qsg[:, 0:TC]
                        s2 = qsg[:, TC:2 * TC]
                        b.wait("scalar", sQl, qstate["vld"][i])
                        b.wait("scalar", sQs, qstate["store"])
                        b.op("scalar", lambda e, o=s1, s=qsl, t=thn:
                             e.activation(out=o, in_=s, func=AF.Sign, bias=t))
                        qstate["sign"] = b.op("scalar", lambda e, o=s2, s=qsl, t=thp:
                                              e.activation(out=o, in_=s, func=AF.Sign,
                                                           bias=t), sA)
                        b.wait("vector", sA, qstate["sign"])
                        v_ad = b.op("vector", lambda e, o=s1, a1=s1, a2=s2:
                                    e.tensor_add(out=o, in0=a1, in1=a2), sV)
                        b.wait("vector", sV, v_ad)
                        v_ts = b.op("vector", lambda e, o=s1, a1=s1:
                                    e.tensor_scalar(out=o, in0=a1, scalar1=0.5,
                                                    scalar2=None, op0=ALU.mult), sV)
                        b.wait("gpsimd", sV, v_ts)
                        qstate["store"] = b.dma("gpsimd", sQs, dst, s1)
                        qstate["stv"][i] = qstate["store"]
                    if qstate["ld"] < len(qunits):
                        i = qstate["ld"]
                        qstate["ld"] = i + 1
                        srcp, dst, thn, thp = q_src_dst(i)
                        qsl = qst[:, 0:TC]
                        if i >= 1:
                            b.wait("gpsimd", sA, qstate["sign"])
                        qstate["vld"][i] = b.dma("gpsimd", sQl, qsl, srcp)

            h1s = {"qm": 0, "qe": 0, "vld": {}, "mmv": {}, "mmft": {},
                   "relu": {}, "stv": {}}

            def h1_evac():
                    if h1s["qe"] < h1s["qm"]:
                        q = h1s["qe"]
                        h1s["qe"] = q + 1
                        ft, c = q // NCH, q % NCH
                        b.wait("scalar", sPE, h1s["mmv"][q])
                        pst = h1s["stv"].get(q - 2)
                        if pst:
                            b.wait("scalar", sH1p[q % 2], pst)
                        h1s["relu"][q] = b.op("scalar", lambda e,
                            o=h1st[:, (q % 2) * TC:(q % 2 + 1) * TC], pb=pw,
                            bc=b1c12[:, ft:ft + 1]:
                            e.activation(out=o, in_=pb[:], func=AF.Relu, bias=bc),
                            sA)
                        b.wait("sync", sA, h1s["relu"][q])
                        h1s["stv"][q] = b.dma("sync", sH1p[q % 2],
                                              h1_scr[ft][:, c * TC:(c + 1) * TC],
                                              h1st[:, (q % 2) * TC:(q % 2 + 1) * TC])

            def pump_h1(nu):
                for _ in range(nu):
                    h1_evac()
                    q = h1s["qm"]
                    if q >= NFT * NCH:
                        continue
                    h1s["qm"] = q + 1
                    ft, c = q // NCH, q % NCH
                    fpar = ft % 2
                    if c == 0:
                        prev_mm = h1s["mmft"].get(ft - 2)
                        if prev_mm:
                            b.wait("sync", sPE, prev_mm)
                        b.wait("sync", sQs, qstate["stv"][7 * NKD + ft // 4])
                        h1s["vld"][ft] = b.dma("sync", sW1f[fpar],
                            w1f[:, fpar * NKD * P:(fpar + 1) * NKD * P],
                            w1q_scr[0:NKD, :, ft * P:(ft + 1) * P]
                            .transpose([1, 0, 2]))
                        b.wait("tensor", sW1f[fpar], h1s["vld"][ft])
                    pr = h1s["relu"].get(q - 1)
                    if pr:
                        b.wait("tensor", sA, pr)
                    wv = mk.get("pw_ev")
                    if wv:
                        b.wait("tensor", sV, wv)
                    for k in range(NKD):
                        lastk = (k == NKD - 1)
                        v = b.op("tensor", lambda e, o=pw[:],
                                 w=w1f[:, fpar * NKD * P + k * P:
                                       fpar * NKD * P + (k + 1) * P],
                                 r=xn_sl(k, c * TC, TC), st=(k == 0), sp=lastk:
                                 e.matmul(out=o, lhsT=w, rhs=r, start=st, stop=sp),
                                 sPE if lastk else None)
                    h1s["mmv"][q] = v
                    h1s["mmft"][ft] = v

            mk = {
                "xz_mm": {}, "xz_ev": {}, "conv": {}, "usil": {},
                "dbc_ev": {}, "bc_dma": {}, "minev": {}, "lnsp": {}, "dA": {},
                "scan": {}, "yacc": {}, "gate": {}, "wout_last": {}, "wout_ev": {},
                "cat_dma": {}, "delta_mm": {}, "zsil": {},
                "bc_users": {}, "prep": {}, "b_dma": {},
                "b_rest": {}, "bc_rest": {}, "wout_mm": {},
            }
            GG = {"n": 0}
            WINLD = {"cnt": [0, 0]}

            def dirpar(j):
                return jobs[j][2]

            def emit_scan_memsets(fwd):
                bk = 0 if fwd else (TC + 1)
                fl = 1 if fwd else TC
                v = None
                for gp in range(2):
                    base = gp * GRP * TCP
                    v = b.op("vector", lambda e, o=dA_sl[:, base + bk:base + bk
                             + (GRP - 1) * TCP + 1:TCP]: e.memset(o, 0.0), sV)
                    v = b.op("vector", lambda e, o=dA_sl[:, base + fl:base + fl
                             + (GRP - 1) * TCP + 1:TCP]: e.memset(o, 1.0), sV)
                v = b.op("vector", lambda e, o=dBu_sb[:, fl:fl
                         + (GRP - 1) * TCP + 1:TCP]: e.memset(o, 0.0), sV)
                return v

            def emit_dir_consts(s, half):
                b.dma("sync", sBulkH, cvec[:, half * 112:(half + 1) * 112],
                      dr[f"cvec_{s}"])
                v_al = b.dma("sync", sBulkH, atmp[:], dr[f"alog_{s}"])
                b.wait("scalar", sBulkH, v_al)
                asl = A_sb[:, half * NET * NS:(half + 1) * NET * NS]
                v_ex = b.op("scalar", lambda e, o=asl: e.activation(
                    out=o, in_=atmp[:], func=AF.Exp), sA)
                b.wait("vector", sA, v_ex)
                v_neg = b.op("vector", lambda e, o=asl: e.tensor_scalar(
                    out=o, in0=o, scalar1=-1.0, scalar2=None, op0=ALU.mult), sV)
                for jj in range(2):
                    b.dma("gpsimd", sBulkG,
                          wx_sb[:, (half * NET + jj * 8) * 96:(half * NET + (jj + 1) * 8) * 96],
                          dr[f"wx_sl_{s}"][8 * jj:8 * (jj + 1)].transpose([1, 0, 2]))
                return v_neg, sBulkH.n, sBulkG.n

            def emit_wdt_load(s, gate_pe):
                if gate_pe:
                    b.wait("gpsimd", sPE, gate_pe)
                b.dma("gpsimd", sBulkG, wdt_sb[:], dr[f"wdt_{s}"])
                return sBulkG.n

            def emit_wout_load(s, gate_pe):
                b.wait("gpsimd", sPE, gate_pe)
                for jj in range(NKD):
                    b.dma("gpsimd", sBulkG, wout_sb[:, jj * 2 * D:(jj + 1) * 2 * D],
                          dr[f"wout_sl_{s}"][2 * jj:2 * (jj + 1)].transpose([1, 0, 2]))
                return sBulkG.n

            def emit_A_mm(j, et):
                s, c, half = jobs[j]
                fwd = (s == "f")
                t0 = c * TC
                par = et % 2
                for xt, bank in ((et, px[par]), (NET + et, pdb)):
                    pk = WINLD["cnt"][par]
                    prev_use = mk["xz_mm"].get(("ld", par, pk - 1))
                    if prev_use:
                        b.wait("gpsimd", sPE, prev_use)
                    vld = b.dma("gpsimd", sWinG[par], wslot[:, par * D:(par + 1) * D],
                                dr[f"win_sl_{s}"][xt])
                    WINLD["cnt"][par] = pk + 1
                    b.wait("tensor", sWinG[par], vld)
                    if xt < NET:
                        prev_ev = mk["xz_ev"].get((j, et - 2)) or \
                            (mk["xz_ev"].get((j - 1, NET - 1)) if et < 2 else None)
                    else:
                        prev_ev = mk["zsil"].get((j, et - 1)) or \
                            (mk["zsil"].get((j - 1, NET - 1)) if et == 0 else None)
                        pz = mk.get("pdb_ev")
                        if pz and et == 0:
                            b.wait("tensor", sV, pz)
                    if prev_ev:
                        b.wait("tensor", sA, prev_ev)
                    for k in range(NKD):
                        last = (k == NKD - 1)
                        v = b.op("tensor", lambda e, o=bank[:],
                                 w=wslot[:, par * D + k * P:par * D + (k + 1) * P],
                                 r=xn_sl(k, t0, TC), st=(k == 0), sp=last:
                                 e.matmul(out=o, lhsT=w, rhs=r, start=st, stop=sp),
                                 sPE if last else None)
                    mk["xz_mm"][("ld", par, pk)] = v
                    mk["xz_mm"][("mm", j, xt)] = v

            def emit_A_evac_u(j, et):
                s, c, half = jobs[j]
                fwd = (s == "f")
                par = et % 2
                uo = u_ext[:, par * (TC + 16) + (3 if fwd else 0):
                           par * (TC + 16) + (3 if fwd else 0) + TC]
                b.wait("scalar", sPE, mk["xz_mm"][("mm", j, et)])
                prev_conv = mk["conv"].get((j, et - 2)) or \
                    (mk["conv"].get((j - 1, NET - 2 + et)) if et < 2 else None)
                if prev_conv:
                    b.wait("scalar", sV, prev_conv)
                if j > 0:
                    g8 = mk["gate"].get((j - 1, et))
                    if g8:
                        b.wait("scalar", sV, g8)
                mk["xz_ev"][(j, et)] = b.op("scalar", lambda e, o=uo,
                    pb=px[par]: e.activation(out=o, in_=pb[:], func=AF.Copy), sA)

            def emit_A_evac_z(j, et):
                b.wait("scalar", sPE, mk["xz_mm"][("mm", j, NET + et)])
                if j > 0:
                    g8 = mk["gate"].get((j - 1, et))
                    if g8:
                        b.wait("scalar", sV, g8)
                mk["zsil"][(j, et)] = b.op("scalar", lambda e,
                    o=z_sb[:, et * TC:(et + 1) * TC], pb=pdb:
                    e.activation(out=o, in_=pb[:], func=AF.Copy), sA)

            def emit_A_conv(j, et):
                s, c, half = jobs[j]
                fwd = (s == "f")
                par = et % 2
                b.wait("vector", sA, mk["xz_ev"][(j, et)])
                prev_sil = mk["usil"].get((j, et - 2)) or \
                    (mk["usil"].get((j - 1, NET - 2 + et)) if et < 2 else None)
                if prev_sil:
                    b.wait("vector", sA, prev_sil)
                ue0 = par * (TC + 16)
                tl = utail[:, et * 4:et * 4 + 3]
                if fwd:
                    v_h = b.op("vector", lambda e, o=u_ext[:, ue0:ue0 + 3], i=tl:
                               e.tensor_copy(out=o, in_=i), sV)
                else:
                    v_h = b.op("vector", lambda e, o=u_ext[:, ue0 + TC:ue0 + TC + 3],
                               i=tl: e.tensor_copy(out=o, in_=i), sV)
                ca = cacc[:, par * TC:(par + 1) * TC]
                wks = [half * 112 + 16 + et * 4 + k for k in range(4)]
                order = [0, 1, 2, 3] if fwd else [3, 2, 1, 0]
                b.wait("vector", sV, v_h)
                v_c = b.op("vector", lambda e, o=ca, i=u_ext[:, ue0:ue0 + TC],
                           w=cvec[:, wks[order[0]]:wks[order[0]] + 1],
                           bc=cvec[:, half * 112 + et:half * 112 + et + 1]:
                           e.tensor_scalar(out=o, in0=i, scalar1=w, scalar2=bc,
                                           op0=ALU.mult, op1=ALU.add), sV)
                for jj in (1, 2, 3):
                    b.wait("vector", sV, v_c)
                    dst = ca if jj < 3 else u_sil[:, et * TC:(et + 1) * TC]
                    v_c = b.op("vector", lambda e, o=dst,
                               i=u_ext[:, ue0 + jj:ue0 + jj + TC],
                               w=cvec[:, wks[order[jj]]:wks[order[jj]] + 1], acc=ca:
                               e.scalar_tensor_tensor(out=o, in0=i, scalar=w,
                                                      in1=acc, op0=ALU.mult,
                                                      op1=ALU.add), sV)
                b.wait("vector", sV, v_c)
                if fwd:
                    mk["conv"][(j, et)] = b.op("vector", lambda e, o=tl,
                        i=u_ext[:, ue0 + TC:ue0 + TC + 3]:
                        e.tensor_copy(out=o, in_=i), sV)
                else:
                    mk["conv"][(j, et)] = b.op("vector", lambda e, o=tl,
                        i=u_ext[:, ue0:ue0 + 3]:
                        e.tensor_copy(out=o, in_=i), sV)

            def emit_silu_batch(j, ets, zets):
                s, c, half = jobs[j]
                if ets:
                    b.wait("scalar", sV, mk["conv"][(j, ets[-1])])
                if zets:
                    b.wait("scalar", sA, mk["zsil"][(j, zets[-1])])
                if ets:
                    e0, e1 = ets[0], ets[-1]
                    assert ets == list(range(e0, e1 + 1))
                    vu = b.op("scalar", lambda e,
                              o=u_sil[:, e0 * TC:(e1 + 1) * TC]:
                              e.activation(out=o, in_=o, func=AF.Silu), sA)
                    for et in ets:
                        mk["usil"][(j, et)] = vu
                if zets:
                    e0, e1 = zets[0], zets[-1]
                    assert zets == list(range(e0, e1 + 1))
                    vz = b.op("scalar", lambda e,
                              o=z_sb[:, e0 * TC:(e1 + 1) * TC]:
                              e.activation(out=o, in_=o, func=AF.Silu), sA)
                    for et in zets:
                        mk["usil"][(j, NET + et)] = vz


            def emit_dbc(j):
                s, c, half = jobs[j]
                b.wait("tensor", sA, mk["usil"][(j, NET - 1)])
                b.wait("tensor", sA, mk["zsil"][(j, NET - 1)])
                pz = mk.get("pdb_ev")
                if pz:
                    b.wait("tensor", sV, pz)
                if j > 0:
                    b.wait("tensor", sA, mk["dbc_ev"][j - 1])
                for et in range(NET):
                    last = (et == NET - 1)
                    v = b.op("tensor", lambda e, o=pdb[0:96, :],
                             w=wx_sb[:, (half * NET + et) * 96:(half * NET + et + 1) * 96],
                             r=u_sil[:, et * TC:(et + 1) * TC],
                             st=(et == 0), sp=last:
                             e.matmul(out=o, lhsT=w, rhs=r, start=st, stop=sp),
                             sPE if last else None)
                b.wait("scalar", sPE, v)
                if j > 0:
                    b.wait("scalar", sBC2r, mk["bc_rest"][j - 1])
                mk["dbc_ev"][j] = b.op("scalar", lambda e:
                                       e.activation(out=dbc_sb[0:96, 0:TC],
                                                    in_=pdb[0:96, :], func=AF.Copy), sA)
                b.wait("gpsimd", sA, mk["dbc_ev"][j])
                if j > 0:
                    b.wait("gpsimd", sV, mk["bc_users"][j - 1])
                vf1 = b.dma("gpsimd", sBC, bc_scr[0:1, :], dbc_sb[RK:RK + NS, 0:TC])
                vf2 = b.dma("gpsimd", sBC, bc_scr[1:2, :],
                            dbc_sb[RK + NS:RK + 2 * NS, 0:TC])
                b.wait("gpsimd", sBC, vf2)
                Q1 = GRP * TC
                mk["b_dma"][j] = b.dma("gpsimd", sBC, B_bc[:, 0:Q1],
                                       bc_scr[0:1, 0:Q1]
                                       .unsqueeze(1).broadcast_to([1, P, Q1]))
                mk["bc_dma"][j] = b.dma("gpsimd", sBC2, C_bc[:, 0:Q1],
                                        bc_scr[1:2, 0:Q1]
                                        .unsqueeze(1).broadcast_to([1, P, Q1]))
                mk["b_rest"][j] = b.dma("gpsimd", sBCr, B_bc[:, Q1:NS * TC],
                                        bc_scr[0:1, Q1:NS * TC]
                                        .unsqueeze(1)
                                        .broadcast_to([1, P, NS * TC - Q1]))
                mk["bc_rest"][j] = b.dma("gpsimd", sBC2r, C_bc[:, Q1:NS * TC],
                                         bc_scr[1:2, Q1:NS * TC]
                                         .unsqueeze(1)
                                         .broadcast_to([1, P, NS * TC - Q1]))

            def emit_delta(j, et):
                s, c, half = jobs[j]
                par = et % 2
                b.wait("tensor", sA, mk["dbc_ev"][j])
                wl = mk.get("wdt_pend")
                if wl is not None and j >= 4:
                    b.wait("tensor", sBulkG, wl)
                    mk["wdt_pend"] = None
                prev = mk["minev"].get((j, et - 2)) or \
                    (mk["minev"].get((j - 1, NET - 2 + et)) if et < 2 else None)
                if prev:
                    b.wait("tensor", sV, prev)
                mk["delta_mm"][(j, et)] = b.op("tensor", lambda e, o=pd[par][:],
                    w=wdt_sb[:, et * P:(et + 1) * P],
                    r=dbc_sb[0:RK, 0:TC]:
                    e.matmul(out=o, lhsT=w, rhs=r, start=True, stop=True), sPE)

            def emit_prep_v(j, et):
                s, c, half = jobs[j]
                par = et % 2
                dsl = delta_sb[:, par * TC:(par + 1) * TC]
                b.wait("vector", sPE, mk["delta_mm"][(j, et)])
                prev_dA = mk["dA"].get((j, et - 2, NGR - 1)) or \
                    (mk["dA"].get((j - 1, NET - 2 + et, NGR - 1)) if et < 2 else None)
                if prev_dA:
                    b.wait("vector", sA, prev_dA)
                mk["minev"][(j, et)] = b.op("vector", lambda e, o=dsl, pb=pd[par],
                    bd=cvec[:, half * 112 + 80 + et:half * 112 + 81 + et]:
                    e.tensor_scalar(out=o, in0=pb[:], scalar1=bd, scalar2=25.0,
                                    op0=ALU.add, op1=ALU.min), sV)

            def emit_prep_a(j, et):
                par = et % 2
                dsl = delta_sb[:, par * TC:(par + 1) * TC]
                b.wait("scalar", sV, mk["minev"][(j, et)])
                v_se = b.op("scalar", lambda e, o=spe[:], i=dsl:
                            e.activation(out=o, in_=i, func=AF.Exp), sA)
                b.wait("scalar", sA, v_se)
                mk["lnsp"][(j, et)] = b.op("scalar", lambda e, o=dsl, i=spe[:]:
                                           e.activation(out=o, in_=i, func=AF.Ln,
                                                        bias=1.0), sA)

            def emit_B_et(j, et):
                s, c, half = jobs[j]
                fwd = (s == "f")
                par = et % 2
                dsl = delta_sb[:, par * TC:(par + 1) * TC]
                dusl = du_sb[:, par * TC:(par + 1) * TC]
                b.wait("vector", sA, mk["lnsp"][(j, et)])
                b.wait("scalar", sA, mk["lnsp"][(j, et)])
                v_du = b.op("vector", lambda e, o=dusl, a1=dsl,
                            a2=u_sil[:, et * TC:(et + 1) * TC]:
                            e.tensor_mul(out=o, in0=a1, in1=a2), sV)
                b.wait("vector", sV, v_du)
                for g in range(NGR):
                    gg = GG["n"]
                    GG["n"] += 1
                    gp = gg % 2
                    if gg >= 2:
                        b.wait("scalar", sV, mk["scan"][gg - 2])
                    doff = 2 if fwd else 0
                    for jj in range(GRP):
                        n = g * GRP + jj
                        last = (jj == GRP - 1)
                        sl0 = (gp * GRP + jj) * TCP + doff
                        v = b.op("scalar", lambda e,
                                 o=dA_sl[:, sl0:sl0 + TC],
                                 i=dsl, sc=A_sb[:, half * NET * NS + et * NS + n:
                                                half * NET * NS + et * NS + n + 1]:
                                 e.activation(out=o, in_=i, func=AF.Exp, scale=sc),
                                 sA if last else None)
                    mk["dA"][(j, et, g)] = v
                    b.wait("vector", sA, v)
                    if gg >= 2:
                        b.wait("vector", sPE, mk["yacc"][gg - 2])
                    if et == 0 and g == 0:
                        b.wait("vector", sBC, mk["b_dma"][j])
                    if et == 0 and g == 1:
                        b.wait("vector", sBCr, mk["b_rest"][j])
                    base = gp * GRP * TCP
                    v_db = b.op("vector", lambda e,
                                o=dBu_sb[:].rearrange("p (s t) -> p s t", s=GRP)
                                [:, :, doff:doff + TC],
                                a1=dusl.unsqueeze(1).broadcast_to([P, GRP, TC]),
                                a2=B_bc[:, g * GRP * TC:(g + 1) * GRP * TC]
                                .rearrange("p (s t) -> p s t", s=GRP):
                                e.tensor_mul(out=o, in0=a1, in1=a2), sV)
                    b.wait("vector", sV, v_db)
                    for jj in range(GRP):
                        n = g * GRP + jj
                        sl0 = (gp * GRP + jj) * TCP + doff
                        db0 = jj * TCP + doff
                        ic = hcar[:, et * NS + n:et * NS + n + 1]
                        if fwd:
                            v_sc = b.op("vector", lambda e, o=h_sl[:, sl0:sl0 + TC],
                                        d0=dA_sl[:, sl0:sl0 + TC],
                                        d1=dBu_sb[:, db0:db0 + TC], i=ic:
                                        e.tensor_tensor_scan(out=o, data0=d0, data1=d1,
                                        initial=i, op0=ALU.mult, op1=ALU.add),
                                        sV if jj == GRP - 1 else None)
                        else:
                            v_sc = b.op("vector", lambda e, o=h_sl[:, sl0:sl0 + TC],
                                        d0=dA_sl[:, sl0:sl0 + TC],
                                        d1=dBu_sb[:, db0:db0 + TC], i=ic:
                                        e.tensor_tensor_scan(out=o[:, ::-1],
                                        data0=d0[:, ::-1], data1=d1[:, ::-1],
                                        initial=i, op0=ALU.mult, op1=ALU.add),
                                        sV if jj == GRP - 1 else None)
                    b.wait("vector", sV, v_sc)
                    off = (doff + TC - 1) if fwd else doff
                    v_tl = b.op("vector", lambda e,
                                o=hcar[:, et * NS + g * GRP:et * NS + (g + 1) * GRP],
                                i=h_sl[:, base + off:base + off
                                       + (GRP - 1) * TCP + 1:TCP]:
                                e.tensor_copy(out=o, in_=i), sV)
                    b.wait("vector", sV, v_tl)
                    if et == 0 and g == 0:
                        b.wait("vector", sBC2, mk["bc_dma"][j])
                    if et == 0 and g == 1:
                        b.wait("vector", sBC2r, mk["bc_rest"][j])
                    v = b.op("vector", lambda e,
                             o=h_sl[:, base:base + GRP * TCP]
                             .rearrange("p (s t) -> p s t", s=GRP)[:, :, doff:doff + TC],
                             a2=C_bc[:, g * GRP * TC:(g + 1) * GRP * TC]
                             .rearrange("p (s t) -> p s t", s=GRP):
                             e.tensor_mul(out=o, in0=o, in1=a2), sV)
                    mk["scan"][gg] = v
                    b.wait("tensor", sV, mk["scan"][gg])
                    if g == 0:
                        prev_gate = mk["gate"].get((j, et - 2)) or \
                            (mk["gate"].get((j - 1, NET - 2 + et)) if et < 2 else None)
                        if prev_gate:
                            b.wait("tensor", sV, prev_gate)
                    for jj in range(GRP):
                        n = g * GRP + jj
                        last = (jj == GRP - 1)
                        sl0 = (gp * GRP + jj) * TCP + doff
                        v = b.op("tensor", lambda e, o=py[par][:],
                                 r=h_sl[:, sl0:sl0 + TC],
                                 st=(n == 0), sp=(n == NS - 1):
                                 e.matmul(out=o, lhsT=ident[:], rhs=r, start=st,
                                          stop=sp), sPE if last else None)
                    mk["yacc"][gg] = v
                    cp = mk.get("cpend")
                    if et == 0 and cp is not None:
                        if g < 3:
                            emit_C_ev(cp, 2 * g)
                            emit_C_ev(cp, 2 * g + 1)
                            emit_C_mm(cp, 2 * g + 2)
                            emit_C_mm(cp, 2 * g + 3)
                        else:
                            emit_C_ev(cp, NKD - 2)
                            emit_C_ev(cp, NKD - 1)
                            mk["cpend"] = None
                    if g == 0 and et + 1 < NET:
                        emit_prep_v(j, et + 1)
                    if g == 0 and et == 1 and j >= 4:
                        pump_h1(1)
                    if g == 1:
                        if j < 7 and et >= 1:
                            emit_A_evac_u(j + 1, et - 1)
                        pump_quant(1)
                        if j >= 4 and not (et == 0
                                           and mk.get("cpend") is not None):
                            pump_h1(1)
                    if g == 2:
                        if j < 7 and et >= 1:
                            emit_A_conv(j + 1, et - 1)
                        if et + 1 < NET:
                            emit_prep_a(j, et + 1)
                    if g == 3:
                        if j < 7 and et >= 1:
                            emit_A_evac_z(j + 1, et - 1)
                        if j >= 4:
                            pump_h1(1)
                        if j < 7 and et == NET - 1:
                            emit_silu_batch(j + 1, list(range(NET - 1)),
                                            list(range(NET - 1)))
                b.wait("vector", sPE, mk["yacc"][GG["n"] - 1])
                v_g1 = b.op("vector", lambda e, o=ytmp[:],
                            i=u_sil[:, et * TC:(et + 1) * TC],
                            dp=cvec[:, half * 112 + 96 + et:half * 112 + 97 + et],
                            pb=py[par]:
                            e.scalar_tensor_tensor(out=o, in0=i, scalar=dp, in1=pb[:],
                                                   op0=ALU.mult, op1=ALU.add), sV)
                b.wait("vector", sV, v_g1)
                mk["gate"][(j, et)] = b.op("vector", lambda e,
                    o=Y_sb[:, et * TC:(et + 1) * TC], a1=ytmp[:],
                    a2=z_sb[:, et * TC:(et + 1) * TC]:
                    e.tensor_mul(out=o, in0=a1, in1=a2), sV)
                mk["bc_users"][j] = mk["scan"][GG["n"] - 1]


            def emit_C_mm(j, dt):
                s, c, half = jobs[j]
                bank = pw if dt % 2 == 0 else pdb
                if dt == 0:
                    b.wait("tensor", sV, mk["gate"][(j, NET - 1)])
                    hv = h1s["relu"].get(h1s["qe"] - 1)
                    if hv:
                        b.wait("tensor", sA, hv)
                    if j > 0:
                        b.wait("tensor", sBulkG, mk["wout_ld"])
                if dt == 1:
                    b.wait("tensor", sA, mk["dbc_ev"][min(j + 1, 7)])
                pv = mk.get("pw_ev") if dt % 2 == 0 else mk.get("pdb_ev")
                if pv:
                    b.wait("tensor", sV, pv)
                for et in range(NET):
                    last = (et == NET - 1)
                    v = b.op("tensor", lambda e, o=bank[:],
                             w=wout_sb[:, et * D + dt * P:et * D + (dt + 1) * P],
                             r=Y_sb[:, et * TC:(et + 1) * TC],
                             st=(et == 0), sp=last:
                             e.matmul(out=o, lhsT=w, rhs=r, start=st, stop=sp),
                             sPE if last else None)
                mk["wout_mm"][dt] = v
                mk["wout_last"][j] = v

            def emit_C_ev(j, dt):
                s, c, half = jobs[j]
                fwd = (s == "f")
                t0 = c * TC
                row0 = 0 if fwd else D
                bank = pw if dt % 2 == 0 else pdb
                b.wait("vector", sPE, mk["wout_mm"][dt])
                if dt >= 2:
                    b.wait("vector", sCat[dt % 2], mk["cat_dma"][dt - 2])
                elif j > 0:
                    b.wait("vector", sCat[dt % 2], mk["cat_dma"][NKD - 2 + dt])
                vv = b.op("vector", lambda e,
                          o=cstage[:, (dt % 2) * TC:(dt % 2 + 1) * TC], pb=bank:
                          e.tensor_copy(out=o, in_=pb[:]), sV)
                mk["wout_ev"][dt] = vv
                if dt % 2 == 0:
                    mk["pw_ev"] = vv
                else:
                    mk["pdb_ev"] = vv
                b.wait("gpsimd", sV, vv)
                mk["cat_dma"][dt] = b.dma("gpsimd", sCat[dt % 2],
                    cat_scr[row0 + dt * P:row0 + (dt + 1) * P, t0:t0 + TC],
                    cstage[:, (dt % 2) * TC:(dt % 2 + 1) * TC])

            def emit_C(j):
                emit_C_mm(j, 0)
                emit_C_mm(j, 1)
                for dt in range(2, NKD):
                    emit_C_ev(j, dt - 2)
                    emit_C_mm(j, dt)
                emit_C_ev(j, NKD - 2)
                emit_C_ev(j, NKD - 1)

            b.dma("gpsimd", sBulkG, b1c12[:], dr["b1col"])
            v_neg, vbh, vbg = emit_dir_consts("f", 0)
            emit_wdt_load("f", 0)
            mk["wout_ld"] = emit_wout_load("f", 0)
            vbg = sBulkG.n
            for eng in ENGS:
                b.wait(eng, sBulkH, vbh)
                b.wait(eng, sBulkG, vbg)
                b.wait(eng, sV, v_neg)
            v_ms = b.op("vector", lambda e: e.memset(hcar[:], 0.0), sV)
            v_ms = b.op("vector", lambda e: e.memset(utail[:], 0.0), sV)
            v_ms = emit_scan_memsets(True)
            b.wait("scalar", sV, v_ms)
            b.wait("tensor", sV, v_ms)

            for et in range(NET):
                emit_A_mm(0, et)
                emit_A_evac_u(0, et)
                emit_A_evac_z(0, et)
                emit_A_conv(0, et)
            emit_silu_batch(0, list(range(NET)), list(range(NET)))
            emit_dbc(0)
            emit_delta(0, 0)
            emit_delta(0, 1)
            emit_prep_v(0, 0)
            emit_prep_a(0, 0)

            for j in range(8):
                b.wait("vector", sV, sV.n)
                b.wait("scalar", sA, sA.n)
                if j == 3:
                    v_neg, vbh, vbg = emit_dir_consts("b", 1)
                    for eng in ("vector", "scalar", "tensor", "sync", "gpsimd"):
                        b.wait(eng, sBulkH, vbh)
                        b.wait(eng, sBulkG, vbg)
                        b.wait(eng, sV, v_neg)
                    v_ms = b.op("vector", lambda e: e.memset(utail[:], 0.0), sV)
                    b.wait("scalar", sV, v_ms)
                    mk["bwd_ms"] = None
                if j >= 5:
                    h1_evac()
                if j >= 1:
                    emit_C_mm(j - 1, 0)
                    emit_C_mm(j - 1, 1)
                    mk["cpend"] = j - 1
                if j == 4:
                    v_ms = b.op("vector", lambda e: e.memset(hcar[:], 0.0), sV)
                    emit_scan_memsets(False)
                for et in range(NET):
                    emit_B_et(j, et)
                    if j == 4 and et == 0:
                        mk["wout_ld"] = emit_wout_load("b", mk["wout_last"][3])
                    if et + 2 < NET:
                        emit_delta(j, et + 2)
                    if j == 3 and et == 14:
                        mk["wdt_pend"] = emit_wdt_load("b",
                                                       mk["delta_mm"][(3, 15)])
                    if j < 7:
                        emit_A_mm(j + 1, et)
                if j < 7:
                    emit_A_evac_u(j + 1, NET - 1)
                    emit_A_evac_z(j + 1, NET - 1)
                    emit_A_conv(j + 1, NET - 1)
                    emit_silu_batch(j + 1, [NET - 1], [NET - 1])
                    emit_dbc(j + 1)
                    emit_delta(j + 1, 0)
                    emit_delta(j + 1, 1)
                    emit_prep_v(j + 1, 0)
                    emit_prep_a(j + 1, 0)
            pump_quant(4)
            pump_h1(1)
            emit_C(7)
            b.barrier(bar, [("sync", sBulkH), ("sync", sX[0]), ("sync", sX[1]), ("sync", sPreS[0]), ("sync", sPreS[1]), ("sync", sWin[0]), ("sync", sWin[1]), ("sync", sW2s[0]), ("sync", sW2s[1]), ("sync", sCatL[0]), ("sync", sCatL[1]), ("sync", sW1f[0]), ("sync", sW1f[1]), ("sync", sH1p[0]), ("sync", sH1p[1]), ("gpsimd", sBulkG), ("gpsimd", sGam[0]), ("gpsimd", sGam[1]), ("gpsimd", sPreL[0]), ("gpsimd", sPreL[1]), ("gpsimd", sBC), ("gpsimd", sBC2), ("gpsimd", sBCr), ("gpsimd", sBC2r), ("gpsimd", sQl), ("gpsimd", sQs), ("gpsimd", sCatG[0]), ("gpsimd", sCatG[1]), ("gpsimd", sXG[0]), ("gpsimd", sXG[1]), ("gpsimd", sWinG[0]), ("gpsimd", sWinG[1]), ("gpsimd", sCat[0]), ("gpsimd", sCat[1])])
            b.flush()
        with ExitStack() as p34:
            b2bc = p34.enter_context(nc.sbuf_tensor("b2bc", [P, D], F32))
            bpbc = p34.enter_context(nc.sbuf_tensor("bpbc", [P, D], F32))
            gstage = p34.enter_context(nc.sbuf_tensor("gstage", [P, 2 * TC], BF16))
            w2q_sb = p34.enter_context(nc.sbuf_tensor("w2q_sb", [P, NFT * D], BF16))
            h1sl = p34.enter_context(nc.sbuf_tensor("h1sl", [P, 8 * TC], BF16))
            g_sb = p34.enter_context(nc.sbuf_tensor("g_sb", [P, 4 * D], BF16))
            wproj_sb = p34.enter_context(nc.sbuf_tensor("wproj_sb", [P, NET * D], BF16))
            catsl = p34.enter_context(nc.sbuf_tensor("catsl", [P, 2 * NET * P], BF16))
            xsl = p34.enter_context(nc.sbuf_tensor("xsl", [P, 2 * D], F32))
            sstage = p34.enter_context(nc.sbuf_tensor("sstage", [P, 2 * TC], BF16))

            b.dma("gpsimd", sBulkG, b2bc[:],
                  dr["b2row"].unsqueeze(1).broadcast_to([1, P, D]))
            b.dma("gpsimd", sBulkG, bpbc[:],
                  dr["bprojrow"].unsqueeze(1).broadcast_to([1, P, D]))
            for jj in range(NKD):
                b.dma("gpsimd", sBulkG, wproj_sb[:, jj * 2 * D:(jj + 1) * 2 * D],
                      dr["wproj_sl"][2 * jj:2 * (jj + 1)].transpose([1, 0, 2]))
            v_wp = sBulkG.n
            w2half = []
            for hl in range(2):
                for fp in range(NFT // 4):
                    fpp = hl * (NFT // 4) + fp
                    v = b.dma("sync", sW2s[hl],
                              w2q_sb[:, 2 * fpp * D:(2 * fpp + 2) * D],
                              w2q_scr[2 * fpp:2 * fpp + 2].transpose([1, 0, 2]))
                w2half.append(v)

            HB = TC // 2
            h1mm = {}
            gsil = {}
            pj_mm = {}
            pj_out = {}
            pj_st = {}
            PASSES = [([pdb, py[0], py[1], pw], [(0, 0), (0, 1), (1, 0), (1, 1)],
                       sH1G),
                      ([px[0], px[1], pd[0], pd[1]],
                       [(2, 0), (2, 1), (3, 0), (3, 1)], sH1G2)]
            NG4 = NFT // 4
            for c in range(NCH):
                t0 = c * TC
                for ps, (banks, tiles, sems) in enumerate(PASSES):
                    if ps == 0:
                        for i4, (tt, hf) in enumerate(tiles):
                            pv = gsil.get((c - 1, tt, hf))
                            if pv:
                                b.wait("tensor", sA, pv)
                    else:
                        if c > 0:
                            b.wait("tensor", sV, pj_out[4 * (c - 1) + 2])
                            b.wait("tensor", sV, pj_out[4 * (c - 1) + 3])
                    for g4 in range(NG4):
                        par = g4 % 2
                        prev = h1mm.get((c, ps, g4 - 2)) or \
                               (h1mm.get((c - 1, ps, NG4 - 2 + g4))
                                if g4 < 2 and c > 0 else None)
                        if prev:
                            b.wait("gpsimd", sPE, prev)
                        vh = b.dma("gpsimd", sems[par],
                                   h1sl[:, (ps * 2 + par) * 4 * HB:
                                        (ps * 2 + par + 1) * 4 * HB],
                                   h1_scr[4 * g4:4 * (g4 + 1),
                                          :, t0 + ps * 2 * P:t0 + ps * 2 * P + 2 * P]
                                   .transpose([1, 0, 2]))
                        b.wait("tensor", sems[par], vh)
                        if c == 0 and ps == 0:
                            b.wait("tensor", sW2s[g4 // (NG4 // 2)],
                                   w2half[g4 // (NG4 // 2)])
                        for sub in range(4):
                            ft = 4 * g4 + sub
                            for i4, (tt, hf) in enumerate(tiles):
                                last = (i4 == 3 and sub == 3)
                                v = b.op("tensor", lambda e, o=banks[i4][:],
                                         w=h1sl[:, (ps * 2 + par) * 4 * HB
                                                + sub * HB + (tt - 2 * ps) * P:
                                                (ps * 2 + par) * 4 * HB
                                                + sub * HB + (tt - 2 * ps + 1) * P],
                                         r=w2q_sb[:, ft * D + hf * TC:
                                                  ft * D + (hf + 1) * TC],
                                         st=(ft == 0), sp=(ft == NFT - 1):
                                         e.matmul(out=o, lhsT=w, rhs=r, start=st,
                                                  stop=sp), sPE if last else None)
                        h1mm[(c, ps, g4)] = v
                    b.wait("vector", sPE, v)
                    if c == 0 and ps == 0:
                        b.wait("vector", sBulkG, v_wp)
                    for i4, (tt, hf) in enumerate(tiles):
                        gsl = gstage[:, hf * TC:(hf + 1) * TC]
                        ps_prev = gsil.get((c, tt - 1, hf)) if i4 >= 2 else \
                            (gsil.get((c - 1, 3, hf)) if ps == 0 and c > 0 else
                             gsil.get((c, 1, hf)) if ps == 1 else None)
                        if ps_prev:
                            b.wait("vector", sA, ps_prev)
                        vv = b.op("vector", lambda e, o=gsl, pb=banks[i4],
                                  bb=b2bc[:, hf * TC:(hf + 1) * TC]:
                                  e.tensor_add(out=o, in0=pb[:], in1=bb), sV)
                        b.wait("scalar", sV, vv)
                        if c > 0:
                            b.wait("scalar", sV, pj_out[4 * (c - 1) + tt])
                        gsil[(c, tt, hf)] = b.op("scalar", lambda e,
                            o=g_sb[:, tt * D + hf * TC:tt * D + (hf + 1) * TC],
                            i=gsl:
                            e.activation(out=o, in_=i, func=AF.Silu), sA)
                b.wait("tensor", sBulkG, v_wp)
                for tl in range(4):
                    ti = 4 * c + tl
                    par = ti % 2
                    prev = pj_mm.get(ti - 2)
                    if prev:
                        b.wait("gpsimd", sPE, prev)
                    if ti >= 2:
                        b.wait("gpsimd", sV, pj_out[ti - 2])
                        b.wait("gpsimd", sSt[par], pj_st[ti - 2])
                    vc = b.dma("gpsimd", sCatG[par],
                               catsl[:, par * NET * P:(par + 1) * NET * P],
                               cat_scr[:, ti * P:(ti + 1) * P]
                               .rearrange("(k p) t -> p k t", p=P))
                    vx = b.dma("gpsimd", sXG[par], xsl[:, par * D:(par + 1) * D],
                               dr["x"][ti * P:(ti + 1) * P, :])
                    b.wait("tensor", sCatG[par], vc)
                    if ti >= 2:
                        b.wait("tensor", sV, pj_out[ti - 2])
                    pjbanks = [px[0], px[1]] if tl % 2 == 0 else [pd[0], pd[1]]
                    pjtt = 2 if tl % 2 == 0 else 3
                    for hf in range(2):
                        bank = pjbanks[hf]
                        b.wait("tensor", sA, gsil[(c, pjtt, hf)])
                        for k in range(16):
                            last = (k == 15)
                            v = b.op("tensor", lambda e, o=bank[:],
                                     w=catsl[:, par * NET * P + k * P:
                                             par * NET * P + (k + 1) * P],
                                     r=wproj_sb[:, k * D + hf * TC:
                                                k * D + (hf + 1) * TC],
                                     st=(k == 0), sp=last:
                                     e.matmul(out=o, lhsT=w, rhs=r, start=st,
                                              stop=sp),
                                     sPE if last else None)
                    pj_mm[ti] = v
                    b.wait("vector", sPE, v)
                    b.wait("vector", sXG[par], vx)
                    b.wait("vector", sA, gsil[(c, tl, 1)])
                    if ti >= 2:
                        b.wait("vector", sSt[par], pj_st[ti - 2])
                    for hf in range(2):
                        bank = pjbanks[hf]
                        ssl = sstage[:, hf * TC:(hf + 1) * TC]
                        v_a = b.op("vector", lambda e, o=ssl, pb=bank,
                                   bb=bpbc[:, hf * TC:(hf + 1) * TC]:
                                   e.tensor_add(out=o, in0=pb[:], in1=bb), sV)
                        b.wait("vector", sV, v_a)
                        v_m = b.op("vector", lambda e, o=ssl, a1=ssl,
                                   a2=g_sb[:, tl * D + hf * TC:
                                           tl * D + (hf + 1) * TC]:
                                   e.tensor_mul(out=o, in0=a1, in1=a2), sV)
                        b.wait("vector", sV, v_m)
                        v2 = b.op("vector", lambda e,
                                  o=xsl[:, par * D + hf * TC:par * D + (hf + 1) * TC],
                                  a1=ssl, a2=xsl[:, par * D + hf * TC:
                                                 par * D + (hf + 1) * TC]:
                                  e.tensor_add(out=o, in0=a1, in1=a2), sV)
                    pj_out[ti] = v2
                    b.wait("sync", sV, v2)
                    pj_st[ti] = b.dma("sync", sSt[par], y_out[ti * P:(ti + 1) * P, :],
                                      xsl[:, par * D:(par + 1) * D])
            b.wait("sync", sSt[0], sSt[0].n)
            b.wait("sync", sSt[1], sSt[1].n)
            b.flush()
    return nc


_PROG = None


def _get_prog():
    global _PROG
    if _PROG is None:
        _PROG = build_program()
    return _PROG


def kernel(**inputs):
    f32 = np.float32

    def cc(a):
        return np.ascontiguousarray(a, dtype=f32)

    ins = {k[len("L0_attn_"):]: v for k, v in inputs.items() if k.startswith("L0_attn_")}
    ins.update({k[len("L0_ffn_"):]: v for k, v in inputs.items() if k.startswith("L0_ffn_")})

    base = {}
    for s in "fb":
        win = np.asarray(ins[f"win_{s}"], dtype=f32)
        base[f"win_sl_{s}"] = cc(win.reshape(NXT, P, NKD, P).transpose(0, 3, 2, 1)
                                 .reshape(NXT, P, D))
        base[f"wout_sl_{s}"] = cc(np.asarray(ins[f"wout_{s}"], dtype=f32).T
                                  .reshape(NET, P, D))
        base[f"wx_sl_{s}"] = cc(np.asarray(ins[f"wx_{s}"], dtype=f32).T
                                .reshape(NET, P, 96))
        base[f"wdt_{s}"] = cc(np.asarray(ins[f"wdt_{s}"], dtype=f32).T)
        bconv = np.asarray(ins[f"bconv_{s}"], dtype=f32).reshape(NET, P).T
        wconv = np.asarray(ins[f"wconv_{s}"], dtype=f32)[:, 0, :] \
            .reshape(NET, P, 4).transpose(1, 0, 2).reshape(P, NET * 4)
        bdt = np.asarray(ins[f"bdt_{s}"], dtype=f32).reshape(NET, P).T
        Dp = np.asarray(ins[f"Dp_{s}"], dtype=f32).reshape(NET, P).T
        base[f"cvec_{s}"] = cc(np.concatenate([bconv, wconv, bdt, Dp], axis=1))
        base[f"alog_{s}"] = cc(np.asarray(ins[f"Alog_{s}"], dtype=f32)
                               .reshape(NET, P, NS).transpose(1, 0, 2).reshape(P, NET * NS))
    base["w1t_sl"] = cc(np.asarray(ins["W1"], dtype=f32).T.reshape(NKD, P, FF))
    base["w2t_sl"] = cc(np.asarray(ins["W2"], dtype=f32).T.reshape(NFT, P, D))
    base["wproj_sl"] = cc(np.asarray(ins["wproj"], dtype=f32).T.reshape(NET, P, D))
    base["b1col"] = cc(np.asarray(ins["b1"], dtype=f32).reshape(NFT, P).T)
    base["b2row"] = cc(np.asarray(ins["b2"], dtype=f32).reshape(1, D))
    base["bprojrow"] = cc(np.asarray(ins["bproj"], dtype=f32).reshape(1, D))
    base["identity"] = np.eye(P, dtype=f32)

    x_all = np.asarray(ins["x"], dtype=f32)
    n_cores = x_all.shape[0]
    in_maps = []
    for c in range(n_cores):
        m = dict(base)
        m["x"] = cc(x_all[c])
        in_maps.append(m)

    nc = _get_prog()
    res = run_bass_kernel_spmd(nc, in_maps, core_ids=list(range(n_cores)))
    out = np.stack([np.asarray(r["y"], dtype=np.float32) for r in res.results], axis=0)
    return out


# version 22
# speedup vs baseline: 1.0760x; 1.0170x over previous
import numpy as np
import ml_dtypes
from contextlib import ExitStack
import concourse.bass as bass
import concourse.mybir as mybir
from concourse.bass_utils import run_bass_kernel_spmd

F32 = mybir.dt.float32
BF16 = mybir.dt.bfloat16
AF = mybir.ActivationFunctionType
ALU = mybir.AluOpType
AX = mybir.AxisListType.X

P = 128
T, D, E, NS, RK, FF = 2048, 1024, 2048, 16, 64, 4096
TC = 512
NCH = T // TC
NET = E // P
NXT = 2 * E // P
NKD = D // P
NTT = T // P
NFT = FF // P
GRP = 4
NGR = NS // GRP
EPSQ = 1e-5
ENGS = ("sync", "gpsimd", "vector", "scalar", "tensor")


class Sem:
    def __init__(self, nc, ctx, name):
        self.h = ctx.enter_context(nc.semaphore(name))
        self.n = 0


class Bld:
    def __init__(self, nc):
        self.nc = nc
        self.st = {e: [] for e in ENGS}

    def em(self, eng, fn):
        self.st[eng].append(fn)

    def wait(self, eng, sem, v):
        if v and v > 0:
            self.em(eng, lambda e, h=sem.h, vv=v: e.wait_ge(h, vv))

    def dma(self, eng, sem, out, in_):
        sem.n += 16
        self.em(eng, lambda e, o=out, i=in_, h=sem.h:
                e.dma_start(out=o, in_=i).then_inc(h, 16))
        return sem.n

    def op(self, eng, fn, sem=None):
        if sem is None:
            self.em(eng, fn)
            return None
        sem.n += 1
        self.em(eng, lambda e, f=fn, h=sem.h: f(e).then_inc(h, 1))
        return sem.n

    def barrier(self, bar, dma_sems=()):
        for eng, sem in dma_sems:
            self.wait(eng, sem, sem.n)
        for eng in ENGS:
            self.em(eng, lambda e, h=bar.h: e.sem_inc(h, 1))
        bar.n += 5
        tgt = bar.n
        for eng in ENGS:
            self.wait(eng, bar, tgt)

    def flush(self):
        st = self.st
        self.st = {e: [] for e in ENGS}
        with self.nc.Block() as blk:
            for eng in ENGS:
                def runner(e, fns=tuple(st[eng])):
                    for f in fns:
                        f(e)
                getattr(blk, eng)(runner)


def build_program():
    nc = bass.Bass("TRN2", target_bir_lowering=False)
    dr = {}

    def din(name, shape):
        dr[name] = nc.dram_tensor(name, list(shape), F32, kind="ExternalInput").ap()

    din("x", (T, D))
    for s in "fb":
        din(f"win_sl_{s}", (NXT, P, D))
        din(f"wout_sl_{s}", (NET, P, D))
        din(f"wx_sl_{s}", (NET, P, 96))
        din(f"wdt_{s}", (RK, E))
        din(f"cvec_{s}", (P, 112))
        din(f"alog_{s}", (P, NET * NS))
    din("w1t_sl", (NKD, P, FF))
    din("w2t_sl", (NFT, P, D))
    din("wproj_sl", (NET, P, D))
    din("b1col", (P, NFT))
    din("b2row", (1, D))
    din("bprojrow", (1, D))
    din("identity", (P, P))

    cat_scr = nc.dram_tensor("cat_scr", [2 * D, T], BF16, kind="Internal").ap()
    bc_scr = nc.dram_tensor("bc_scr", [2, NS * TC], BF16, kind="Internal").ap()
    w1q_scr = nc.dram_tensor("w1q_scr", [NKD, P, FF], BF16, kind="Internal").ap()
    w2q_scr = nc.dram_tensor("w2q_scr", [NFT, P, D], BF16, kind="Internal").ap()
    h1_scr = nc.dram_tensor("h1_scr", [NFT, P, T], BF16, kind="Internal").ap()
    y_out = nc.dram_tensor("y", [T, D], F32, kind="ExternalOutput").ap()

    with ExitStack() as ctx:
        sPE = Sem(nc, ctx, "sPE")
        sV = Sem(nc, ctx, "sV")
        sA = Sem(nc, ctx, "sA")
        bar = Sem(nc, ctx, "bar")
        sBulkH = Sem(nc, ctx, "sBulkH")
        sBulkG = Sem(nc, ctx, "sBulkG")
        sX = [Sem(nc, ctx, f"sX{i}") for i in range(2)]
        sGam = [Sem(nc, ctx, f"sGam{i}") for i in range(2)]
        sPreL = [Sem(nc, ctx, f"sPreL{i}") for i in range(2)]
        sPreS = [Sem(nc, ctx, f"sPreS{i}") for i in range(2)]
        sWin = [Sem(nc, ctx, f"sWin{i}") for i in range(2)]
        sBC = Sem(nc, ctx, "sBC")
        sQl = Sem(nc, ctx, "sQl")
        sQs = Sem(nc, ctx, "sQs")
        sCat = [Sem(nc, ctx, f"sCat{i}") for i in range(2)]
        sW2s = [Sem(nc, ctx, f"sW2s{i}") for i in range(2)]
        sCatL = [Sem(nc, ctx, f"sCatL{i}") for i in range(2)]
        sSt = [Sem(nc, ctx, f"sSt{i}") for i in range(2)]
        sGp = Sem(nc, ctx, "sGp")
        sBC2 = Sem(nc, ctx, "sBC2")
        sBCr = Sem(nc, ctx, "sBCr")
        sBC2r = Sem(nc, ctx, "sBC2r")
        sCatG = [Sem(nc, ctx, f"sCatG{i}") for i in range(2)]
        sXG = [Sem(nc, ctx, f"sXG{i}") for i in range(2)]
        sWinG = [Sem(nc, ctx, f"sWinG{i}") for i in range(2)]
        sW1f = [Sem(nc, ctx, f"sW1f{i}") for i in range(2)]
        sH1p = [Sem(nc, ctx, f"sH1p{i}") for i in range(2)]
        sH1G = [Sem(nc, ctx, f"sH1G{i}") for i in range(2)]
        sH1G2 = [Sem(nc, ctx, f"sH1G2{i}") for i in range(2)]

        xn = ctx.enter_context(nc.sbuf_tensor("xn_cm", [P, NKD * T], BF16))
        ident = ctx.enter_context(nc.sbuf_tensor("ident", [P, P], BF16))
        gfin = ctx.enter_context(nc.sbuf_tensor("gfin", [P, 16], F32))

        px = [ctx.enter_context(nc.psum_tensor(f"px{i}", [P, TC], F32)) for i in range(2)]
        pdb = ctx.enter_context(nc.psum_tensor("pdb", [P, TC], F32))
        pd = [ctx.enter_context(nc.psum_tensor(f"pd{i}", [P, TC], F32)) for i in range(2)]
        py = [ctx.enter_context(nc.psum_tensor(f"py{i}", [P, TC], F32)) for i in range(2)]
        pw = ctx.enter_context(nc.psum_tensor("pw", [P, TC], F32))
        pall = [px[0], px[1], pdb, pd[0], pd[1], py[0], py[1], pw]

        b = Bld(nc)

        def xn_sl(k, t0, tl):
            return xn[:, k * T + t0: k * T + t0 + tl]

        with ExitStack() as p0:
            xst = p0.enter_context(nc.sbuf_tensor("xst", [P, 2 * D], F32))
            sqst = p0.enter_context(nc.sbuf_tensor("sqst", [P, D], F32))
            stats = p0.enter_context(nc.sbuf_tensor("stats", [P, 8 * NTT], F32))
            xnt = p0.enter_context(nc.sbuf_tensor("xnt", [P, 2 * D], F32))
            wstg = p0.enter_context(nc.sbuf_tensor("wstg", [P, 2 * FF], F32))
            gacc = p0.enter_context(nc.sbuf_tensor("gacc", [P, 48], F32))
            onesf = p0.enter_context(nc.sbuf_tensor("onesf", [P, P], F32))

            identf = p0.enter_context(nc.sbuf_tensor("identf", [P, P], F32))
            b.dma("gpsimd", sBulkG, ident[:], dr["identity"])
            v_id = b.dma("sync", sBulkH, identf[:], dr["identity"])
            b.op("vector", lambda e: e.memset(onesf[:], 1.0))
            b.op("vector", lambda e: e.memset(gfin[:, 15:16], 1e-5), sV)

            gunits = [("w1", k) for k in range(NKD)] + \
                     [("w2", jj) for jj in range(NFT)]
            gst = {"ld": 0, "rd": 0, "vld": {}, "vr": {}}

            def gpump(nu):
                for _ in range(nu):
                    if gst["rd"] < gst["ld"]:
                        u = gst["rd"]
                        gst["rd"] = u + 1
                        kind, a = gunits[u]
                        w = FF if kind == "w1" else D
                        sl = wstg[:, (u % 2) * FF:(u % 2) * FF + w]
                        b.wait("vector", sGam[u % 2], gst["vld"][u])
                        gst["vr"][u] = b.op("vector", lambda e,
                            o=gacc[:, (u if u < NKD else 8 + u - NKD):
                                   (u if u < NKD else 8 + u - NKD) + 1], s=sl:
                            e.tensor_reduce(out=o, in_=s, axis=AX, op=ALU.add,
                                            apply_absolute_value=True), sV)
                    if gst["ld"] < len(gunits):
                        u = gst["ld"]
                        gst["ld"] = u + 1
                        kind, a = gunits[u]
                        w = FF if kind == "w1" else D
                        sl = wstg[:, (u % 2) * FF:(u % 2) * FF + w]
                        if u >= 2:
                            b.wait("gpsimd", sV, gst["vr"][u - 2])
                        src_ap = dr["w1t_sl"][a] if kind == "w1" \
                            else dr["w2t_sl"][a]
                        gst["vld"][u] = b.dma("gpsimd", sGam[u % 2], sl, src_ap)

            st1 = []
            for i in range(NTT):
                if i >= 2:
                    b.wait("sync", sV, st1[i - 2][0])
                    b.wait("sync", sA, st1[i - 2][1])
                sl = xst[:, (i % 2) * D:(i % 2 + 1) * D]
                vld = b.dma("sync", sX[i % 2], sl, dr["x"][i * P:(i + 1) * P, :])
                b.wait("vector", sX[i % 2], vld)
                vr = b.op("vector", lambda e, o=stats[:, i:i + 1], s=sl:
                          e.reduce_sum(out=o, in_=s, axis=AX), sV)
                b.wait("scalar", sX[i % 2], vld)
                if i >= 1:
                    b.wait("scalar", sA, st1[i - 1][1])
                va = b.op("scalar", lambda e, o=sqst[:], s=sl,
                          a=stats[:, NTT + i:NTT + i + 1]:
                          e.activation(out=o, in_=s, func=AF.Square, accum_out=a), sA)
                st1.append((vr, va))
                gpump(1)

            sum_c = stats[:, 0:NTT]
            m2_c = stats[:, NTT:2 * NTT]
            mean_c = stats[:, 2 * NTT:3 * NTT]
            q_c = stats[:, 3 * NTT:4 * NTT]
            var_c = stats[:, 4 * NTT:5 * NTT]
            std_c = stats[:, 5 * NTT:6 * NTT]
            rstd_c = stats[:, 6 * NTT:7 * NTT]
            nmr_c = stats[:, 7 * NTT:8 * NTT]
            b.wait("vector", sA, sA.n)
            b.wait("vector", sV, st1[-1][0])
            v_mean = b.op("vector", lambda e: e.tensor_scalar(out=mean_c, in0=sum_c,
                          scalar1=1.0 / D, scalar2=None, op0=ALU.mult), sV)
            b.wait("vector", sV, v_mean)
            v_q = b.op("vector", lambda e: e.tensor_mul(out=q_c, in0=mean_c,
                       in1=mean_c), sV)
            b.wait("vector", sV, v_q)
            b.op("vector", lambda e: e.scalar_tensor_tensor(out=var_c, in0=m2_c,
                 scalar=1.0 / D, in1=q_c, op0=ALU.mult, op1=ALU.subtract), sV)
            b.wait("scalar", sV, sV.n)
            va_std = b.op("scalar", lambda e: e.activation(out=std_c, in_=var_c,
                          func=AF.Sqrt, bias=gfin[:, 15:16]), sA)
            b.wait("vector", sA, va_std)
            v_rec = b.op("vector", lambda e: e.reciprocal(out=rstd_c, in_=std_c), sV)
            b.op("vector", lambda e: e.tensor_copy(out=sqst[:, 0:NTT], in_=std_c))
            b.wait("vector", sV, v_rec)
            v_nmr = b.op("vector", lambda e: e.scalar_tensor_tensor(out=nmr_c,
                         in0=mean_c, scalar=-1.0, in1=rstd_c, op0=ALU.mult,
                         op1=ALU.mult), sV)

            ev_tr = []
            ap2 = []
            b.wait("sync", sV, sV.n)
            b.wait("sync", sA, sA.n)
            b.wait("tensor", sBulkH, v_id)
            b.wait("tensor", sBulkG, 16)
            for i in range(NTT):
                if i >= 2:
                    b.wait("sync", sV, ap2[i - 2][0])
                sl = xst[:, (i % 2) * D:(i % 2 + 1) * D]
                xnsl = xnt[:, (i % 2) * D:(i % 2 + 1) * D]
                vld = b.dma("sync", sX[i % 2], sl, dr["x"][i * P:(i + 1) * P, :])
                b.wait("vector", sX[i % 2], vld)
                b.wait("vector", sV, v_nmr)
                if i >= 2:
                    b.wait("vector", sPE, ap2[i - 2][1])
                va = b.op("vector", lambda e, o=xnsl, s=sl, r=rstd_c[:, i:i + 1],
                          nm=nmr_c[:, i:i + 1]:
                          e.tensor_scalar(out=o, in0=s, scalar1=r, scalar2=nm,
                                          op0=ALU.mult, op1=ALU.add), sV)
                b.wait("tensor", sV, va)
                vpe_last = 0
                for k in range(NKD):
                    idx = i * NKD + k
                    bank = px[idx % 2]
                    if idx >= 2:
                        b.wait("tensor", sA, ev_tr[idx - 2])
                    vpe = b.op("tensor", lambda e, o=bank[:, 0:P],
                               inp=xnsl[:, k * P:(k + 1) * P]:
                               e.transpose(out=o, in_=inp, identity=identf[:]), sPE)
                    b.wait("scalar", sPE, vpe)
                    ev = b.op("scalar", lambda e, o=xn_sl(k, i * P, P), pb=bank:
                              e.activation(out=o, in_=pb[:, 0:P], func=AF.Copy), sA)
                    ev_tr.append(ev)
                    vpe_last = vpe
                ap2.append((va, vpe_last))
                gpump(2)


            while gst["rd"] < len(gunits):
                gpump(1)
            b.wait("vector", sV, gst["vr"][len(gunits) - 1])
            v_r1 = b.op("vector", lambda e: e.reduce_sum(out=gacc[:, 40:41],
                        in_=gacc[:, 0:NKD], axis=AX), sV)
            v_ga = b.op("vector", lambda e: e.reduce_sum(out=gacc[:, 41:42],
                        in_=gacc[:, 8:8 + NFT], axis=AX), sV)
            b.wait("tensor", sV, v_ga)
            vmm = b.op("tensor", lambda e: e.matmul(out=pw[:, 0:2], lhsT=onesf[:],
                       rhs=gacc[:, 40:42], start=True, stop=True), sPE)
            b.wait("vector", sPE, vmm)
            v_gc = b.op("vector", lambda e: e.tensor_copy(out=gfin[:, 10:12],
                        in_=pw[:, 0:2]), sV)
            b.wait("vector", sV, v_gc)
            v_t1 = b.op("vector", lambda e: e.tensor_scalar(out=gfin[:, 0:1],
                 in0=gfin[:, 10:11], scalar1=0.5 / float(D * FF), scalar2=0.5 * EPSQ,
                 op0=ALU.mult, op1=ALU.add), sV)
            v_t2 = b.op("vector", lambda e: e.tensor_scalar(out=gfin[:, 1:2],
                 in0=gfin[:, 11:12], scalar1=0.5 / float(FF * D), scalar2=0.5 * EPSQ,
                 op0=ALU.mult, op1=ALU.add), sV)
            b.wait("vector", sV, v_t2)
            b.op("vector", lambda e: e.tensor_scalar(out=gfin[:, 2:3], in0=gfin[:, 0:1],
                 scalar1=-1.0, scalar2=None, op0=ALU.mult), sV)
            b.op("vector", lambda e: e.tensor_scalar(out=gfin[:, 3:4], in0=gfin[:, 1:2],
                 scalar1=-1.0, scalar2=None, op0=ALU.mult), sV)
            b.barrier(bar, [("sync", sBulkH), ("sync", sX[0]), ("sync", sX[1]), ("sync", sPreS[0]), ("sync", sPreS[1]), ("sync", sWin[0]), ("sync", sWin[1]), ("sync", sW2s[0]), ("sync", sW2s[1]), ("sync", sCatL[0]), ("sync", sCatL[1]), ("gpsimd", sBulkG), ("gpsimd", sGam[0]), ("gpsimd", sGam[1]), ("gpsimd", sPreL[0]), ("gpsimd", sPreL[1]), ("gpsimd", sBC), ("gpsimd", sBC2), ("gpsimd", sBCr), ("gpsimd", sBC2r), ("gpsimd", sQl), ("gpsimd", sQs), ("gpsimd", sCatG[0]), ("gpsimd", sCatG[1]), ("gpsimd", sXG[0]), ("gpsimd", sXG[1]), ("gpsimd", sWinG[0]), ("gpsimd", sWinG[1]), ("gpsimd", sCat[0]), ("gpsimd", sCat[1])])
            b.flush()

        with ExitStack() as p12:
            wslot = p12.enter_context(nc.sbuf_tensor("wslot", [P, 2 * D], BF16))
            wout_sb = p12.enter_context(nc.sbuf_tensor("wout_sb", [P, NET * D], BF16))
            wx_sb = p12.enter_context(nc.sbuf_tensor("wx_sb", [P, 2 * NET * 96], BF16))
            wdt_sb = p12.enter_context(nc.sbuf_tensor("wdt_sb", [RK, E], BF16))
            cvec = p12.enter_context(nc.sbuf_tensor("cvec", [P, 2 * 112], F32))
            atmp = p12.enter_context(nc.sbuf_tensor("atmp", [P, NET * NS], F32))
            A_sb = p12.enter_context(nc.sbuf_tensor("A_sb", [P, 2 * NET * NS], F32))
            u_ext = p12.enter_context(nc.sbuf_tensor("u_ext", [P, 2 * (TC + 16)], BF16))
            utail = p12.enter_context(nc.sbuf_tensor("utail", [P, 4 * NET], BF16))
            cacc = p12.enter_context(nc.sbuf_tensor("cacc", [P, 2 * TC], BF16))
            u_sil = p12.enter_context(nc.sbuf_tensor("u_sil", [P, NET * TC], BF16))
            z_sb = p12.enter_context(nc.sbuf_tensor("z_sb", [P, NET * TC], BF16))
            dbc_sb = p12.enter_context(nc.sbuf_tensor("dbc_sb", [P, TC], BF16))
            B_bc = p12.enter_context(nc.sbuf_tensor("B_bc", [P, NS * TC], BF16))
            C_bc = p12.enter_context(nc.sbuf_tensor("C_bc", [P, NS * TC], BF16))
            delta_sb = p12.enter_context(nc.sbuf_tensor("delta_sb", [P, 2 * TC], BF16))
            spe = p12.enter_context(nc.sbuf_tensor("spe", [P, TC], BF16))
            du_sb = p12.enter_context(nc.sbuf_tensor("du_sb", [P, 2 * TC], BF16))
            TCP = TC + 2
            dA_sl = p12.enter_context(nc.sbuf_tensor("dA_sl", [P, 2 * GRP * TCP], BF16))
            h_sl = p12.enter_context(nc.sbuf_tensor("h_sl", [P, 2 * GRP * TCP], BF16))
            dBu_sb = p12.enter_context(nc.sbuf_tensor("dBu_sb", [P, GRP * TCP], BF16))
            hcar = p12.enter_context(nc.sbuf_tensor("hcar", [P, NET * NS], F32))
            ytmp = p12.enter_context(nc.sbuf_tensor("ytmp", [P, TC], BF16))
            Y_sb = p12.enter_context(nc.sbuf_tensor("Y_sb", [P, NET * TC], BF16))
            cstage = p12.enter_context(nc.sbuf_tensor("cstage", [P, 2 * TC], BF16))
            qst = p12.enter_context(nc.sbuf_tensor("qst", [P, TC], F32))
            qsg = p12.enter_context(nc.sbuf_tensor("qsg", [P, 2 * TC], BF16))
            w1f = p12.enter_context(nc.sbuf_tensor("w1f", [P, 2 * NKD * P], BF16))
            h1st = p12.enter_context(nc.sbuf_tensor("h1st", [P, 2 * TC], BF16))
            b1c12 = p12.enter_context(nc.sbuf_tensor("b1c12", [P, NFT], F32))

            jobs = []
            for half, s in enumerate("fb"):
                crange = range(NCH) if s == "f" else range(NCH - 1, -1, -1)
                for c in crange:
                    jobs.append((s, c, half))

            qunits = [("w1", k, q) for k in range(NKD) for q in range(8)] + \
                     [("w2", jj, q) for jj in range(NFT) for q in range(2)]
            qstate = {"ld": 0, "cp": 0, "sign": 0, "store": 0, "vld": {}, "stv": {}}

            def q_src_dst(i):
                kind, a, q = qunits[i]
                if kind == "w1":
                    return (dr["w1t_sl"][a][:, q * TC:(q + 1) * TC],
                            w1q_scr[a][:, q * TC:(q + 1) * TC],
                            gfin[:, 2:3], gfin[:, 0:1])
                return (dr["w2t_sl"][a][:, q * TC:(q + 1) * TC],
                        w2q_scr[a][:, q * TC:(q + 1) * TC],
                        gfin[:, 3:4], gfin[:, 1:2])

            def pump_quant(nu):
                for _ in range(nu):
                    if qstate["cp"] < qstate["ld"]:
                        i = qstate["cp"]
                        qstate["cp"] = i + 1
                        srcp, dst, thn, thp = q_src_dst(i)
                        qsl = qst[:, 0:TC]
                        s1 = qsg[:, 0:TC]
                        s2 = qsg[:, TC:2 * TC]
                        b.wait("scalar", sQl, qstate["vld"][i])
                        b.wait("scalar", sQs, qstate["store"])
                        b.op("scalar", lambda e, o=s1, s=qsl, t=thn:
                             e.activation(out=o, in_=s, func=AF.Sign, bias=t))
                        qstate["sign"] = b.op("scalar", lambda e, o=s2, s=qsl, t=thp:
                                              e.activation(out=o, in_=s, func=AF.Sign,
                                                           bias=t), sA)
                        b.wait("vector", sA, qstate["sign"])
                        v_ad = b.op("vector", lambda e, o=s1, a1=s1, a2=s2:
                                    e.tensor_add(out=o, in0=a1, in1=a2), sV)
                        b.wait("vector", sV, v_ad)
                        v_ts = b.op("vector", lambda e, o=s1, a1=s1:
                                    e.tensor_scalar(out=o, in0=a1, scalar1=0.5,
                                                    scalar2=None, op0=ALU.mult), sV)
                        b.wait("gpsimd", sV, v_ts)
                        qstate["store"] = b.dma("gpsimd", sQs, dst, s1)
                        qstate["stv"][i] = qstate["store"]
                    if qstate["ld"] < len(qunits):
                        i = qstate["ld"]
                        qstate["ld"] = i + 1
                        srcp, dst, thn, thp = q_src_dst(i)
                        qsl = qst[:, 0:TC]
                        if i >= 1:
                            b.wait("gpsimd", sA, qstate["sign"])
                        qstate["vld"][i] = b.dma("gpsimd", sQl, qsl, srcp)

            h1s = {"qm": 0, "qe": 0, "vld": {}, "mmv": {}, "mmft": {},
                   "relu": {}, "stv": {}}

            def h1_evac():
                    if h1s["qe"] < h1s["qm"]:
                        q = h1s["qe"]
                        h1s["qe"] = q + 1
                        ft, c = q // NCH, q % NCH
                        b.wait("scalar", sPE, h1s["mmv"][q])
                        pst = h1s["stv"].get(q - 2)
                        if pst:
                            b.wait("scalar", sH1p[q % 2], pst)
                        h1s["relu"][q] = b.op("scalar", lambda e,
                            o=h1st[:, (q % 2) * TC:(q % 2 + 1) * TC], pb=pw,
                            bc=b1c12[:, ft:ft + 1]:
                            e.activation(out=o, in_=pb[:], func=AF.Relu, bias=bc),
                            sA)
                        b.wait("sync", sA, h1s["relu"][q])
                        h1s["stv"][q] = b.dma("sync", sH1p[q % 2],
                                              h1_scr[ft][:, c * TC:(c + 1) * TC],
                                              h1st[:, (q % 2) * TC:(q % 2 + 1) * TC])

            def pump_h1(nu):
                for _ in range(nu):
                    h1_evac()
                    q = h1s["qm"]
                    if q >= NFT * NCH:
                        continue
                    h1s["qm"] = q + 1
                    ft, c = q // NCH, q % NCH
                    fpar = ft % 2
                    if c == 0:
                        prev_mm = h1s["mmft"].get(ft - 2)
                        if prev_mm:
                            b.wait("sync", sPE, prev_mm)
                        b.wait("sync", sQs, qstate["stv"][7 * NKD + ft // 4])
                        h1s["vld"][ft] = b.dma("sync", sW1f[fpar],
                            w1f[:, fpar * NKD * P:(fpar + 1) * NKD * P],
                            w1q_scr[0:NKD, :, ft * P:(ft + 1) * P]
                            .transpose([1, 0, 2]))
                        b.wait("tensor", sW1f[fpar], h1s["vld"][ft])
                    pr = h1s["relu"].get(q - 1)
                    if pr:
                        b.wait("tensor", sA, pr)
                    wv = mk.get("pw_ev")
                    if wv:
                        b.wait("tensor", sV, wv)
                    for k in range(NKD):
                        lastk = (k == NKD - 1)
                        v = b.op("tensor", lambda e, o=pw[:],
                                 w=w1f[:, fpar * NKD * P + k * P:
                                       fpar * NKD * P + (k + 1) * P],
                                 r=xn_sl(k, c * TC, TC), st=(k == 0), sp=lastk:
                                 e.matmul(out=o, lhsT=w, rhs=r, start=st, stop=sp),
                                 sPE if lastk else None)
                    h1s["mmv"][q] = v
                    h1s["mmft"][ft] = v

            mk = {
                "xz_mm": {}, "xz_ev": {}, "conv": {}, "usil": {},
                "dbc_ev": {}, "bc_dma": {}, "minev": {}, "lnsp": {}, "dA": {},
                "scan": {}, "yacc": {}, "gate": {}, "wout_last": {}, "wout_ev": {},
                "cat_dma": {}, "delta_mm": {}, "zsil": {},
                "bc_users": {}, "prep": {}, "b_dma": {},
                "b_rest": {}, "bc_rest": {}, "wout_mm": {},
            }
            GG = {"n": 0}
            WINLD = {"cnt": [0, 0]}

            def dirpar(j):
                return jobs[j][2]

            def emit_scan_memsets(fwd):
                bk = 0 if fwd else (TC + 1)
                fl = 1 if fwd else TC
                v = None
                for gp in range(2):
                    base = gp * GRP * TCP
                    v = b.op("vector", lambda e, o=dA_sl[:, base + bk:base + bk
                             + (GRP - 1) * TCP + 1:TCP]: e.memset(o, 0.0), sV)
                    v = b.op("vector", lambda e, o=dA_sl[:, base + fl:base + fl
                             + (GRP - 1) * TCP + 1:TCP]: e.memset(o, 1.0), sV)
                v = b.op("vector", lambda e, o=dBu_sb[:, fl:fl
                         + (GRP - 1) * TCP + 1:TCP]: e.memset(o, 0.0), sV)
                return v

            def emit_dir_consts(s, half):
                b.dma("sync", sBulkH, cvec[:, half * 112:(half + 1) * 112],
                      dr[f"cvec_{s}"])
                v_al = b.dma("sync", sBulkH, atmp[:], dr[f"alog_{s}"])
                b.wait("scalar", sBulkH, v_al)
                asl = A_sb[:, half * NET * NS:(half + 1) * NET * NS]
                v_ex = b.op("scalar", lambda e, o=asl: e.activation(
                    out=o, in_=atmp[:], func=AF.Exp), sA)
                b.wait("vector", sA, v_ex)
                v_neg = b.op("vector", lambda e, o=asl: e.tensor_scalar(
                    out=o, in0=o, scalar1=-1.0, scalar2=None, op0=ALU.mult), sV)
                for jj in range(2):
                    b.dma("gpsimd", sBulkG,
                          wx_sb[:, (half * NET + jj * 8) * 96:(half * NET + (jj + 1) * 8) * 96],
                          dr[f"wx_sl_{s}"][8 * jj:8 * (jj + 1)].transpose([1, 0, 2]))
                return v_neg, sBulkH.n, sBulkG.n

            def emit_wdt_load(s, gate_pe):
                if gate_pe:
                    b.wait("gpsimd", sPE, gate_pe)
                b.dma("gpsimd", sBulkG, wdt_sb[:], dr[f"wdt_{s}"])
                return sBulkG.n

            def emit_wout_load(s, gate_pe):
                b.wait("gpsimd", sPE, gate_pe)
                for jj in range(NKD):
                    b.dma("gpsimd", sBulkG, wout_sb[:, jj * 2 * D:(jj + 1) * 2 * D],
                          dr[f"wout_sl_{s}"][2 * jj:2 * (jj + 1)].transpose([1, 0, 2]))
                return sBulkG.n

            def emit_A_mm(j, et):
                s, c, half = jobs[j]
                fwd = (s == "f")
                t0 = c * TC
                par = et % 2
                for xt, bank in ((et, px[par]), (NET + et, pdb)):
                    pk = WINLD["cnt"][par]
                    prev_use = mk["xz_mm"].get(("ld", par, pk - 1))
                    if prev_use:
                        b.wait("gpsimd", sPE, prev_use)
                    vld = b.dma("gpsimd", sWinG[par], wslot[:, par * D:(par + 1) * D],
                                dr[f"win_sl_{s}"][xt])
                    WINLD["cnt"][par] = pk + 1
                    b.wait("tensor", sWinG[par], vld)
                    if xt < NET:
                        prev_ev = mk["xz_ev"].get((j, et - 2)) or \
                            (mk["xz_ev"].get((j - 1, NET - 1)) if et < 2 else None)
                    else:
                        prev_ev = mk["zsil"].get((j, et - 1)) or \
                            (mk["zsil"].get((j - 1, NET - 1)) if et == 0 else None)
                        pz = mk.get("pdb_ev")
                        if pz and et == 0:
                            b.wait("tensor", sV, pz)
                    if prev_ev:
                        b.wait("tensor", sA, prev_ev)
                    for k in range(NKD):
                        last = (k == NKD - 1)
                        v = b.op("tensor", lambda e, o=bank[:],
                                 w=wslot[:, par * D + k * P:par * D + (k + 1) * P],
                                 r=xn_sl(k, t0, TC), st=(k == 0), sp=last:
                                 e.matmul(out=o, lhsT=w, rhs=r, start=st, stop=sp),
                                 sPE if last else None)
                    mk["xz_mm"][("ld", par, pk)] = v
                    mk["xz_mm"][("mm", j, xt)] = v

            def emit_A_evac_u(j, et):
                s, c, half = jobs[j]
                fwd = (s == "f")
                par = et % 2
                uo = u_ext[:, par * (TC + 16) + (3 if fwd else 0):
                           par * (TC + 16) + (3 if fwd else 0) + TC]
                b.wait("scalar", sPE, mk["xz_mm"][("mm", j, et)])
                prev_conv = mk["conv"].get((j, et - 2)) or \
                    (mk["conv"].get((j - 1, NET - 2 + et)) if et < 2 else None)
                if prev_conv:
                    b.wait("scalar", sV, prev_conv)
                if j > 0:
                    g8 = mk["gate"].get((j - 1, et))
                    if g8:
                        b.wait("scalar", sV, g8)
                mk["xz_ev"][(j, et)] = b.op("scalar", lambda e, o=uo,
                    pb=px[par]: e.activation(out=o, in_=pb[:], func=AF.Copy), sA)

            def emit_A_evac_z(j, et):
                b.wait("scalar", sPE, mk["xz_mm"][("mm", j, NET + et)])
                if j > 0:
                    g8 = mk["gate"].get((j - 1, et))
                    if g8:
                        b.wait("scalar", sV, g8)
                mk["zsil"][(j, et)] = b.op("scalar", lambda e,
                    o=z_sb[:, et * TC:(et + 1) * TC], pb=pdb:
                    e.activation(out=o, in_=pb[:], func=AF.Copy), sA)

            def emit_A_conv(j, et):
                s, c, half = jobs[j]
                fwd = (s == "f")
                par = et % 2
                b.wait("vector", sA, mk["xz_ev"][(j, et)])
                prev_sil = mk["usil"].get((j, et - 2)) or \
                    (mk["usil"].get((j - 1, NET - 2 + et)) if et < 2 else None)
                if prev_sil:
                    b.wait("vector", sA, prev_sil)
                ue0 = par * (TC + 16)
                tl = utail[:, et * 4:et * 4 + 3]
                if fwd:
                    v_h = b.op("vector", lambda e, o=u_ext[:, ue0:ue0 + 3], i=tl:
                               e.tensor_copy(out=o, in_=i), sV)
                else:
                    v_h = b.op("vector", lambda e, o=u_ext[:, ue0 + TC:ue0 + TC + 3],
                               i=tl: e.tensor_copy(out=o, in_=i), sV)
                ca = cacc[:, par * TC:(par + 1) * TC]
                wks = [half * 112 + 16 + et * 4 + k for k in range(4)]
                order = [0, 1, 2, 3] if fwd else [3, 2, 1, 0]
                b.wait("vector", sV, v_h)
                v_c = b.op("vector", lambda e, o=ca, i=u_ext[:, ue0:ue0 + TC],
                           w=cvec[:, wks[order[0]]:wks[order[0]] + 1],
                           bc=cvec[:, half * 112 + et:half * 112 + et + 1]:
                           e.tensor_scalar(out=o, in0=i, scalar1=w, scalar2=bc,
                                           op0=ALU.mult, op1=ALU.add), sV)
                for jj in (1, 2, 3):
                    b.wait("vector", sV, v_c)
                    dst = ca if jj < 3 else u_sil[:, et * TC:(et + 1) * TC]
                    v_c = b.op("vector", lambda e, o=dst,
                               i=u_ext[:, ue0 + jj:ue0 + jj + TC],
                               w=cvec[:, wks[order[jj]]:wks[order[jj]] + 1], acc=ca:
                               e.scalar_tensor_tensor(out=o, in0=i, scalar=w,
                                                      in1=acc, op0=ALU.mult,
                                                      op1=ALU.add), sV)
                b.wait("vector", sV, v_c)
                if fwd:
                    mk["conv"][(j, et)] = b.op("vector", lambda e, o=tl,
                        i=u_ext[:, ue0 + TC:ue0 + TC + 3]:
                        e.tensor_copy(out=o, in_=i), sV)
                else:
                    mk["conv"][(j, et)] = b.op("vector", lambda e, o=tl,
                        i=u_ext[:, ue0:ue0 + 3]:
                        e.tensor_copy(out=o, in_=i), sV)

            def emit_silu_batch(j, ets, zets):
                s, c, half = jobs[j]
                if ets:
                    b.wait("scalar", sV, mk["conv"][(j, ets[-1])])
                if zets:
                    b.wait("scalar", sA, mk["zsil"][(j, zets[-1])])
                if ets:
                    e0, e1 = ets[0], ets[-1]
                    assert ets == list(range(e0, e1 + 1))
                    vu = b.op("scalar", lambda e,
                              o=u_sil[:, e0 * TC:(e1 + 1) * TC]:
                              e.activation(out=o, in_=o, func=AF.Silu), sA)
                    for et in ets:
                        mk["usil"][(j, et)] = vu
                if zets:
                    e0, e1 = zets[0], zets[-1]
                    assert zets == list(range(e0, e1 + 1))
                    vz = b.op("scalar", lambda e,
                              o=z_sb[:, e0 * TC:(e1 + 1) * TC]:
                              e.activation(out=o, in_=o, func=AF.Silu), sA)
                    for et in zets:
                        mk["usil"][(j, NET + et)] = vz


            def emit_dbc(j):
                s, c, half = jobs[j]
                b.wait("tensor", sA, mk["usil"][(j, NET - 1)])
                b.wait("tensor", sA, mk["zsil"][(j, NET - 1)])
                pz = mk.get("pdb_ev")
                if pz:
                    b.wait("tensor", sV, pz)
                if j > 0:
                    b.wait("tensor", sA, mk["dbc_ev"][j - 1])
                for et in range(NET):
                    last = (et == NET - 1)
                    v = b.op("tensor", lambda e, o=pdb[0:96, :],
                             w=wx_sb[:, (half * NET + et) * 96:(half * NET + et + 1) * 96],
                             r=u_sil[:, et * TC:(et + 1) * TC],
                             st=(et == 0), sp=last:
                             e.matmul(out=o, lhsT=w, rhs=r, start=st, stop=sp),
                             sPE if last else None)
                b.wait("scalar", sPE, v)
                if j > 0:
                    b.wait("scalar", sBC2r, mk["bc_rest"][j - 1])
                mk["dbc_ev"][j] = b.op("scalar", lambda e:
                                       e.activation(out=dbc_sb[0:96, 0:TC],
                                                    in_=pdb[0:96, :], func=AF.Copy), sA)
                b.wait("gpsimd", sA, mk["dbc_ev"][j])
                if j > 0:
                    b.wait("gpsimd", sV, mk["bc_users"][j - 1])
                vf1 = b.dma("gpsimd", sBC, bc_scr[0:1, :], dbc_sb[RK:RK + NS, 0:TC])
                vf2 = b.dma("gpsimd", sBC, bc_scr[1:2, :],
                            dbc_sb[RK + NS:RK + 2 * NS, 0:TC])
                b.wait("gpsimd", sBC, vf2)
                Q1 = GRP * TC
                mk["b_dma"][j] = b.dma("gpsimd", sBC, B_bc[:, 0:Q1],
                                       bc_scr[0:1, 0:Q1]
                                       .unsqueeze(1).broadcast_to([1, P, Q1]))
                mk["bc_dma"][j] = b.dma("gpsimd", sBC2, C_bc[:, 0:Q1],
                                        bc_scr[1:2, 0:Q1]
                                        .unsqueeze(1).broadcast_to([1, P, Q1]))
                mk["b_rest"][j] = b.dma("gpsimd", sBCr, B_bc[:, Q1:NS * TC],
                                        bc_scr[0:1, Q1:NS * TC]
                                        .unsqueeze(1)
                                        .broadcast_to([1, P, NS * TC - Q1]))
                mk["bc_rest"][j] = b.dma("gpsimd", sBC2r, C_bc[:, Q1:NS * TC],
                                         bc_scr[1:2, Q1:NS * TC]
                                         .unsqueeze(1)
                                         .broadcast_to([1, P, NS * TC - Q1]))

            def emit_delta(j, et):
                s, c, half = jobs[j]
                par = et % 2
                b.wait("tensor", sA, mk["dbc_ev"][j])
                wl = mk.get("wdt_pend")
                if wl is not None and j >= 4:
                    b.wait("tensor", sBulkG, wl)
                    mk["wdt_pend"] = None
                prev = mk["minev"].get((j, et - 2)) or \
                    (mk["minev"].get((j - 1, NET - 2 + et)) if et < 2 else None)
                if prev:
                    b.wait("tensor", sV, prev)
                mk["delta_mm"][(j, et)] = b.op("tensor", lambda e, o=pd[par][:],
                    w=wdt_sb[:, et * P:(et + 1) * P],
                    r=dbc_sb[0:RK, 0:TC]:
                    e.matmul(out=o, lhsT=w, rhs=r, start=True, stop=True), sPE)

            def emit_prep_v(j, et):
                s, c, half = jobs[j]
                par = et % 2
                dsl = delta_sb[:, par * TC:(par + 1) * TC]
                b.wait("vector", sPE, mk["delta_mm"][(j, et)])
                prev_dA = mk["dA"].get((j, et - 2, NGR - 1)) or \
                    (mk["dA"].get((j - 1, NET - 2 + et, NGR - 1)) if et < 2 else None)
                if prev_dA:
                    b.wait("vector", sA, prev_dA)
                mk["minev"][(j, et)] = b.op("vector", lambda e, o=dsl, pb=pd[par],
                    bd=cvec[:, half * 112 + 80 + et:half * 112 + 81 + et]:
                    e.tensor_scalar(out=o, in0=pb[:], scalar1=bd, scalar2=25.0,
                                    op0=ALU.add, op1=ALU.min), sV)

            def emit_prep_a(j, et):
                par = et % 2
                dsl = delta_sb[:, par * TC:(par + 1) * TC]
                b.wait("scalar", sV, mk["minev"][(j, et)])
                v_se = b.op("scalar", lambda e, o=spe[:], i=dsl:
                            e.activation(out=o, in_=i, func=AF.Exp), sA)
                b.wait("scalar", sA, v_se)
                mk["lnsp"][(j, et)] = b.op("scalar", lambda e, o=dsl, i=spe[:]:
                                           e.activation(out=o, in_=i, func=AF.Ln,
                                                        bias=1.0), sA)

            def emit_B_et(j, et):
                s, c, half = jobs[j]
                fwd = (s == "f")
                par = et % 2
                dsl = delta_sb[:, par * TC:(par + 1) * TC]
                dusl = du_sb[:, par * TC:(par + 1) * TC]
                b.wait("vector", sA, mk["lnsp"][(j, et)])
                b.wait("scalar", sA, mk["lnsp"][(j, et)])
                v_du = b.op("vector", lambda e, o=dusl, a1=dsl,
                            a2=u_sil[:, et * TC:(et + 1) * TC]:
                            e.tensor_mul(out=o, in0=a1, in1=a2), sV)
                b.wait("vector", sV, v_du)
                for g in range(NGR):
                    gg = GG["n"]
                    GG["n"] += 1
                    gp = gg % 2
                    if gg >= 2:
                        b.wait("scalar", sV, mk["scan"][gg - 2])
                    doff = 2 if fwd else 0
                    for jj in range(GRP):
                        n = g * GRP + jj
                        last = (jj == GRP - 1)
                        sl0 = (gp * GRP + jj) * TCP + doff
                        v = b.op("scalar", lambda e,
                                 o=dA_sl[:, sl0:sl0 + TC],
                                 i=dsl, sc=A_sb[:, half * NET * NS + et * NS + n:
                                                half * NET * NS + et * NS + n + 1]:
                                 e.activation(out=o, in_=i, func=AF.Exp, scale=sc),
                                 sA if last else None)
                    mk["dA"][(j, et, g)] = v
                    b.wait("vector", sA, v)
                    if gg >= 2:
                        b.wait("vector", sPE, mk["yacc"][gg - 2])
                    if et == 0 and g == 0:
                        b.wait("vector", sBC, mk["b_dma"][j])
                    if et == 0 and g == 1:
                        b.wait("vector", sBCr, mk["b_rest"][j])
                    base = gp * GRP * TCP
                    v_db = b.op("vector", lambda e,
                                o=dBu_sb[:].rearrange("p (s t) -> p s t", s=GRP)
                                [:, :, doff:doff + TC],
                                a1=dusl.unsqueeze(1).broadcast_to([P, GRP, TC]),
                                a2=B_bc[:, g * GRP * TC:(g + 1) * GRP * TC]
                                .rearrange("p (s t) -> p s t", s=GRP):
                                e.tensor_mul(out=o, in0=a1, in1=a2), sV)
                    b.wait("vector", sV, v_db)
                    for jj in range(GRP):
                        n = g * GRP + jj
                        sl0 = (gp * GRP + jj) * TCP + doff
                        db0 = jj * TCP + doff
                        ic = hcar[:, et * NS + n:et * NS + n + 1]
                        if fwd:
                            v_sc = b.op("vector", lambda e, o=h_sl[:, sl0:sl0 + TC],
                                        d0=dA_sl[:, sl0:sl0 + TC],
                                        d1=dBu_sb[:, db0:db0 + TC], i=ic:
                                        e.tensor_tensor_scan(out=o, data0=d0, data1=d1,
                                        initial=i, op0=ALU.mult, op1=ALU.add),
                                        sV if jj == GRP - 1 else None)
                        else:
                            v_sc = b.op("vector", lambda e, o=h_sl[:, sl0:sl0 + TC],
                                        d0=dA_sl[:, sl0:sl0 + TC],
                                        d1=dBu_sb[:, db0:db0 + TC], i=ic:
                                        e.tensor_tensor_scan(out=o[:, ::-1],
                                        data0=d0[:, ::-1], data1=d1[:, ::-1],
                                        initial=i, op0=ALU.mult, op1=ALU.add),
                                        sV if jj == GRP - 1 else None)
                    b.wait("vector", sV, v_sc)
                    off = (doff + TC - 1) if fwd else doff
                    v_tl = b.op("vector", lambda e,
                                o=hcar[:, et * NS + g * GRP:et * NS + (g + 1) * GRP],
                                i=h_sl[:, base + off:base + off
                                       + (GRP - 1) * TCP + 1:TCP]:
                                e.tensor_copy(out=o, in_=i), sV)
                    b.wait("vector", sV, v_tl)
                    if et == 0 and g == 0:
                        b.wait("vector", sBC2, mk["bc_dma"][j])
                    if et == 0 and g == 1:
                        b.wait("vector", sBC2r, mk["bc_rest"][j])
                    v = b.op("vector", lambda e,
                             o=h_sl[:, base:base + GRP * TCP]
                             .rearrange("p (s t) -> p s t", s=GRP)[:, :, doff:doff + TC],
                             a2=C_bc[:, g * GRP * TC:(g + 1) * GRP * TC]
                             .rearrange("p (s t) -> p s t", s=GRP):
                             e.tensor_mul(out=o, in0=o, in1=a2), sV)
                    mk["scan"][gg] = v
                    b.wait("tensor", sV, mk["scan"][gg])
                    if g == 0:
                        prev_gate = mk["gate"].get((j, et - 2)) or \
                            (mk["gate"].get((j - 1, NET - 2 + et)) if et < 2 else None)
                        if prev_gate:
                            b.wait("tensor", sV, prev_gate)
                    for jj in range(GRP):
                        n = g * GRP + jj
                        last = (jj == GRP - 1)
                        sl0 = (gp * GRP + jj) * TCP + doff
                        v = b.op("tensor", lambda e, o=py[par][:],
                                 r=h_sl[:, sl0:sl0 + TC],
                                 st=(n == 0), sp=(n == NS - 1):
                                 e.matmul(out=o, lhsT=ident[:], rhs=r, start=st,
                                          stop=sp), sPE if last else None)
                    mk["yacc"][gg] = v
                    cp = mk.get("cpend")
                    if et == 0 and cp is not None:
                        if g < 3:
                            emit_C_ev(cp, 2 * g)
                            emit_C_ev(cp, 2 * g + 1)
                            emit_C_mm(cp, 2 * g + 2)
                            emit_C_mm(cp, 2 * g + 3)
                        else:
                            emit_C_ev(cp, NKD - 2)
                            emit_C_ev(cp, NKD - 1)
                            mk["cpend"] = None
                    if g == 0 and et + 1 < NET:
                        emit_prep_v(j, et + 1)
                    if g == 0 and et == 1 and j >= 4:
                        pump_h1(1)
                    if g == 1:
                        if j < 7 and et >= 1:
                            emit_A_evac_u(j + 1, et - 1)
                        pump_quant(1)
                        if j >= 4 and not (et == 0
                                           and mk.get("cpend") is not None):
                            pump_h1(1)
                    if g == 2:
                        if j < 7 and et >= 1:
                            emit_A_conv(j + 1, et - 1)
                        if et + 1 < NET:
                            emit_prep_a(j, et + 1)
                    if g == 3:
                        if j < 7 and et >= 1:
                            emit_A_evac_z(j + 1, et - 1)
                        if j >= 4:
                            pump_h1(1)
                        if j < 7 and et == NET - 1:
                            emit_silu_batch(j + 1, list(range(NET - 1)),
                                            list(range(NET - 1)))
                b.wait("vector", sPE, mk["yacc"][GG["n"] - 1])
                v_g1 = b.op("vector", lambda e, o=ytmp[:],
                            i=u_sil[:, et * TC:(et + 1) * TC],
                            dp=cvec[:, half * 112 + 96 + et:half * 112 + 97 + et],
                            pb=py[par]:
                            e.scalar_tensor_tensor(out=o, in0=i, scalar=dp, in1=pb[:],
                                                   op0=ALU.mult, op1=ALU.add), sV)
                b.wait("vector", sV, v_g1)
                mk["gate"][(j, et)] = b.op("vector", lambda e,
                    o=Y_sb[:, et * TC:(et + 1) * TC], a1=ytmp[:],
                    a2=z_sb[:, et * TC:(et + 1) * TC]:
                    e.tensor_mul(out=o, in0=a1, in1=a2), sV)
                mk["bc_users"][j] = mk["scan"][GG["n"] - 1]


            def emit_C_mm(j, dt):
                s, c, half = jobs[j]
                bank = pw if dt % 2 == 0 else pdb
                if dt == 0:
                    b.wait("tensor", sV, mk["gate"][(j, NET - 1)])
                    hv = h1s["relu"].get(h1s["qe"] - 1)
                    if hv:
                        b.wait("tensor", sA, hv)
                    if j > 0:
                        b.wait("tensor", sBulkG, mk["wout_ld"])
                if dt == 1:
                    b.wait("tensor", sA, mk["dbc_ev"][min(j + 1, 7)])
                pv = mk.get("pw_ev") if dt % 2 == 0 else mk.get("pdb_ev")
                if pv:
                    b.wait("tensor", sV, pv)
                for et in range(NET):
                    last = (et == NET - 1)
                    v = b.op("tensor", lambda e, o=bank[:],
                             w=wout_sb[:, et * D + dt * P:et * D + (dt + 1) * P],
                             r=Y_sb[:, et * TC:(et + 1) * TC],
                             st=(et == 0), sp=last:
                             e.matmul(out=o, lhsT=w, rhs=r, start=st, stop=sp),
                             sPE if last else None)
                mk["wout_mm"][dt] = v
                mk["wout_last"][j] = v

            def emit_C_ev(j, dt):
                s, c, half = jobs[j]
                fwd = (s == "f")
                t0 = c * TC
                row0 = 0 if fwd else D
                bank = pw if dt % 2 == 0 else pdb
                b.wait("vector", sPE, mk["wout_mm"][dt])
                if dt >= 2:
                    b.wait("vector", sCat[dt % 2], mk["cat_dma"][dt - 2])
                elif j > 0:
                    b.wait("vector", sCat[dt % 2], mk["cat_dma"][NKD - 2 + dt])
                vv = b.op("vector", lambda e,
                          o=cstage[:, (dt % 2) * TC:(dt % 2 + 1) * TC], pb=bank:
                          e.tensor_copy(out=o, in_=pb[:]), sV)
                mk["wout_ev"][dt] = vv
                if dt % 2 == 0:
                    mk["pw_ev"] = vv
                else:
                    mk["pdb_ev"] = vv
                b.wait("gpsimd", sV, vv)
                mk["cat_dma"][dt] = b.dma("gpsimd", sCat[dt % 2],
                    cat_scr[row0 + dt * P:row0 + (dt + 1) * P, t0:t0 + TC],
                    cstage[:, (dt % 2) * TC:(dt % 2 + 1) * TC])

            def emit_C(j):
                emit_C_mm(j, 0)
                emit_C_mm(j, 1)
                for dt in range(2, NKD):
                    emit_C_ev(j, dt - 2)
                    emit_C_mm(j, dt)
                emit_C_ev(j, NKD - 2)
                emit_C_ev(j, NKD - 1)

            b.dma("gpsimd", sBulkG, b1c12[:], dr["b1col"])
            v_neg, vbh, vbg = emit_dir_consts("f", 0)
            emit_wdt_load("f", 0)
            mk["wout_ld"] = emit_wout_load("f", 0)
            vbg = sBulkG.n
            for eng in ENGS:
                b.wait(eng, sBulkH, vbh)
                b.wait(eng, sBulkG, vbg)
                b.wait(eng, sV, v_neg)
            v_ms = b.op("vector", lambda e: e.memset(hcar[:], 0.0), sV)
            v_ms = b.op("vector", lambda e: e.memset(utail[:], 0.0), sV)
            v_ms = emit_scan_memsets(True)
            b.wait("scalar", sV, v_ms)
            b.wait("tensor", sV, v_ms)

            for et in range(NET):
                emit_A_mm(0, et)
                emit_A_evac_u(0, et)
                emit_A_evac_z(0, et)
                emit_A_conv(0, et)
            emit_silu_batch(0, list(range(NET)), list(range(NET)))
            emit_dbc(0)
            emit_delta(0, 0)
            emit_delta(0, 1)
            emit_prep_v(0, 0)
            emit_prep_a(0, 0)

            for j in range(8):
                b.wait("vector", sV, sV.n)
                b.wait("scalar", sA, sA.n)
                if j == 3:
                    v_neg, vbh, vbg = emit_dir_consts("b", 1)
                    for eng in ("vector", "scalar", "tensor", "sync", "gpsimd"):
                        b.wait(eng, sBulkH, vbh)
                        b.wait(eng, sBulkG, vbg)
                        b.wait(eng, sV, v_neg)
                    v_ms = b.op("vector", lambda e: e.memset(utail[:], 0.0), sV)
                    b.wait("scalar", sV, v_ms)
                    mk["bwd_ms"] = None
                if j >= 5:
                    h1_evac()
                if j >= 1:
                    emit_C_mm(j - 1, 0)
                    emit_C_mm(j - 1, 1)
                    mk["cpend"] = j - 1
                if j == 4:
                    v_ms = b.op("vector", lambda e: e.memset(hcar[:], 0.0), sV)
                    emit_scan_memsets(False)
                for et in range(NET):
                    emit_B_et(j, et)
                    if j == 4 and et == 0:
                        mk["wout_ld"] = emit_wout_load("b", mk["wout_last"][3])
                    if et + 2 < NET:
                        emit_delta(j, et + 2)
                    if j == 3 and et == 14:
                        mk["wdt_pend"] = emit_wdt_load("b",
                                                       mk["delta_mm"][(3, 15)])
                    if j < 7:
                        emit_A_mm(j + 1, et)
                if j < 7:
                    emit_A_evac_u(j + 1, NET - 1)
                    emit_A_evac_z(j + 1, NET - 1)
                    emit_A_conv(j + 1, NET - 1)
                    emit_silu_batch(j + 1, [NET - 1], [NET - 1])
                    emit_dbc(j + 1)
                    emit_delta(j + 1, 0)
                    emit_delta(j + 1, 1)
                    emit_prep_v(j + 1, 0)
                    emit_prep_a(j + 1, 0)
            pump_quant(4)
            pump_h1(1)
            emit_C(7)
            b.barrier(bar, [("sync", sBulkH), ("sync", sX[0]), ("sync", sX[1]), ("sync", sPreS[0]), ("sync", sPreS[1]), ("sync", sWin[0]), ("sync", sWin[1]), ("sync", sW2s[0]), ("sync", sW2s[1]), ("sync", sCatL[0]), ("sync", sCatL[1]), ("sync", sW1f[0]), ("sync", sW1f[1]), ("sync", sH1p[0]), ("sync", sH1p[1]), ("gpsimd", sBulkG), ("gpsimd", sGam[0]), ("gpsimd", sGam[1]), ("gpsimd", sPreL[0]), ("gpsimd", sPreL[1]), ("gpsimd", sBC), ("gpsimd", sBC2), ("gpsimd", sBCr), ("gpsimd", sBC2r), ("gpsimd", sQl), ("gpsimd", sQs), ("gpsimd", sCatG[0]), ("gpsimd", sCatG[1]), ("gpsimd", sXG[0]), ("gpsimd", sXG[1]), ("gpsimd", sWinG[0]), ("gpsimd", sWinG[1]), ("gpsimd", sCat[0]), ("gpsimd", sCat[1])])
            b.flush()
        with ExitStack() as p34:
            b2bc = p34.enter_context(nc.sbuf_tensor("b2bc", [P, D], F32))
            bpbc = p34.enter_context(nc.sbuf_tensor("bpbc", [P, D], F32))
            gstage = p34.enter_context(nc.sbuf_tensor("gstage", [P, 2 * TC], BF16))
            w2q_sb = p34.enter_context(nc.sbuf_tensor("w2q_sb", [P, NFT * D], BF16))
            h1sl = p34.enter_context(nc.sbuf_tensor("h1sl", [P, 8 * TC], BF16))
            g_sb = p34.enter_context(nc.sbuf_tensor("g_sb", [P, 4 * D], BF16))
            wproj_sb = p34.enter_context(nc.sbuf_tensor("wproj_sb", [P, NET * D], BF16))
            catsl = p34.enter_context(nc.sbuf_tensor("catsl", [P, 2 * NET * P], BF16))
            xsl = p34.enter_context(nc.sbuf_tensor("xsl", [P, 2 * D], F32))
            sstage = p34.enter_context(nc.sbuf_tensor("sstage", [P, 2 * TC], BF16))

            b.dma("gpsimd", sBulkG, b2bc[:],
                  dr["b2row"].unsqueeze(1).broadcast_to([1, P, D]))
            b.dma("gpsimd", sBulkG, bpbc[:],
                  dr["bprojrow"].unsqueeze(1).broadcast_to([1, P, D]))
            for jj in range(NKD):
                b.dma("gpsimd", sBulkG, wproj_sb[:, jj * 2 * D:(jj + 1) * 2 * D],
                      dr["wproj_sl"][2 * jj:2 * (jj + 1)].transpose([1, 0, 2]))
            v_wp = sBulkG.n
            w2half = []
            for hl in range(2):
                for fp in range(NFT // 4):
                    fpp = hl * (NFT // 4) + fp
                    v = b.dma("sync", sW2s[hl],
                              w2q_sb[:, 2 * fpp * D:(2 * fpp + 2) * D],
                              w2q_scr[2 * fpp:2 * fpp + 2].transpose([1, 0, 2]))
                w2half.append(v)

            HB = TC // 2
            h1mm = {}
            gsil = {}
            pj_mm = {}
            pj_out = {}
            pj_st = {}
            PASSES = [([pdb, py[0], py[1], pw], [(0, 0), (0, 1), (1, 0), (1, 1)],
                       sH1G),
                      ([px[0], px[1], pd[0], pd[1]],
                       [(2, 0), (2, 1), (3, 0), (3, 1)], sH1G2)]
            NG4 = NFT // 4
            for c in range(NCH):
                t0 = c * TC
                for ps, (banks, tiles, sems) in enumerate(PASSES):
                    if ps == 0:
                        for i4, (tt, hf) in enumerate(tiles):
                            pv = gsil.get((c - 1, tt, hf))
                            if pv:
                                b.wait("tensor", sA, pv)
                    else:
                        if c > 0:
                            b.wait("tensor", sV, pj_out[4 * (c - 1) + 2])
                            b.wait("tensor", sV, pj_out[4 * (c - 1) + 3])
                    for g4 in range(NG4):
                        par = g4 % 2
                        prev = h1mm.get((c, ps, g4 - 2)) or \
                               (h1mm.get((c - 1, ps, NG4 - 2 + g4))
                                if g4 < 2 and c > 0 else None)
                        if prev:
                            b.wait("gpsimd", sPE, prev)
                        vh = b.dma("gpsimd", sems[par],
                                   h1sl[:, (ps * 2 + par) * 4 * HB:
                                        (ps * 2 + par + 1) * 4 * HB],
                                   h1_scr[4 * g4:4 * (g4 + 1),
                                          :, t0 + ps * 2 * P:t0 + ps * 2 * P + 2 * P]
                                   .transpose([1, 0, 2]))
                        b.wait("tensor", sems[par], vh)
                        if c == 0 and ps == 0:
                            b.wait("tensor", sW2s[g4 // (NG4 // 2)],
                                   w2half[g4 // (NG4 // 2)])
                        for sub in range(4):
                            ft = 4 * g4 + sub
                            for i4, (tt, hf) in enumerate(tiles):
                                last = (i4 == 3 and sub == 3)
                                v = b.op("tensor", lambda e, o=banks[i4][:],
                                         w=h1sl[:, (ps * 2 + par) * 4 * HB
                                                + sub * HB + (tt - 2 * ps) * P:
                                                (ps * 2 + par) * 4 * HB
                                                + sub * HB + (tt - 2 * ps + 1) * P],
                                         r=w2q_sb[:, ft * D + hf * TC:
                                                  ft * D + (hf + 1) * TC],
                                         st=(ft == 0), sp=(ft == NFT - 1):
                                         e.matmul(out=o, lhsT=w, rhs=r, start=st,
                                                  stop=sp), sPE if last else None)
                        h1mm[(c, ps, g4)] = v
                    b.wait("vector", sPE, v)
                    if c == 0 and ps == 0:
                        b.wait("vector", sBulkG, v_wp)
                    for i4, (tt, hf) in enumerate(tiles):
                        gsl = gstage[:, hf * TC:(hf + 1) * TC]
                        ps_prev = gsil.get((c, tt - 1, hf)) if i4 >= 2 else \
                            (gsil.get((c - 1, 3, hf)) if ps == 0 and c > 0 else
                             gsil.get((c, 1, hf)) if ps == 1 else None)
                        if ps_prev:
                            b.wait("vector", sA, ps_prev)
                        vv = b.op("vector", lambda e, o=gsl, pb=banks[i4],
                                  bb=b2bc[:, hf * TC:(hf + 1) * TC]:
                                  e.tensor_add(out=o, in0=pb[:], in1=bb), sV)
                        b.wait("scalar", sV, vv)
                        if c > 0:
                            b.wait("scalar", sV, pj_out[4 * (c - 1) + tt])
                        gsil[(c, tt, hf)] = b.op("scalar", lambda e,
                            o=g_sb[:, tt * D + hf * TC:tt * D + (hf + 1) * TC],
                            i=gsl:
                            e.activation(out=o, in_=i, func=AF.Silu), sA)
                b.wait("tensor", sBulkG, v_wp)
                for tl in range(4):
                    ti = 4 * c + tl
                    par = ti % 2
                    prev = pj_mm.get(ti - 2)
                    if prev:
                        b.wait("gpsimd", sPE, prev)
                    if ti >= 2:
                        b.wait("gpsimd", sV, pj_out[ti - 2])
                        b.wait("gpsimd", sSt[par], pj_st[ti - 2])
                    vc = b.dma("gpsimd", sCatG[par],
                               catsl[:, par * NET * P:(par + 1) * NET * P],
                               cat_scr[:, ti * P:(ti + 1) * P]
                               .rearrange("(k p) t -> p k t", p=P))
                    vx = b.dma("gpsimd", sXG[par], xsl[:, par * D:(par + 1) * D],
                               dr["x"][ti * P:(ti + 1) * P, :])
                    b.wait("tensor", sCatG[par], vc)
                    if ti >= 2:
                        b.wait("tensor", sV, pj_out[ti - 2])
                    pjbanks = [px[0], px[1]] if tl % 2 == 0 else [pd[0], pd[1]]
                    pjtt = 2 if tl % 2 == 0 else 3
                    for hf in range(2):
                        bank = pjbanks[hf]
                        b.wait("tensor", sA, gsil[(c, pjtt, hf)])
                        for k in range(16):
                            last = (k == 15)
                            v = b.op("tensor", lambda e, o=bank[:],
                                     w=catsl[:, par * NET * P + k * P:
                                             par * NET * P + (k + 1) * P],
                                     r=wproj_sb[:, k * D + hf * TC:
                                                k * D + (hf + 1) * TC],
                                     st=(k == 0), sp=last:
                                     e.matmul(out=o, lhsT=w, rhs=r, start=st,
                                              stop=sp),
                                     sPE if last else None)
                    pj_mm[ti] = v
                    b.wait("vector", sPE, v)
                    b.wait("vector", sXG[par], vx)
                    b.wait("vector", sA, gsil[(c, tl, 1)])
                    if ti >= 2:
                        b.wait("vector", sSt[par], pj_st[ti - 2])
                    for hf in range(2):
                        bank = pjbanks[hf]
                        ssl = sstage[:, hf * TC:(hf + 1) * TC]
                        v_a = b.op("vector", lambda e, o=ssl, pb=bank,
                                   bb=bpbc[:, hf * TC:(hf + 1) * TC]:
                                   e.tensor_add(out=o, in0=pb[:], in1=bb), sV)
                        b.wait("vector", sV, v_a)
                        v_m = b.op("vector", lambda e, o=ssl, a1=ssl,
                                   a2=g_sb[:, tl * D + hf * TC:
                                           tl * D + (hf + 1) * TC]:
                                   e.tensor_mul(out=o, in0=a1, in1=a2), sV)
                        b.wait("vector", sV, v_m)
                        v2 = b.op("vector", lambda e,
                                  o=xsl[:, par * D + hf * TC:par * D + (hf + 1) * TC],
                                  a1=ssl, a2=xsl[:, par * D + hf * TC:
                                                 par * D + (hf + 1) * TC]:
                                  e.tensor_add(out=o, in0=a1, in1=a2), sV)
                    pj_out[ti] = v2
                    b.wait("sync", sV, v2)
                    pj_st[ti] = b.dma("sync", sSt[par], y_out[ti * P:(ti + 1) * P, :],
                                      xsl[:, par * D:(par + 1) * D])
            b.wait("sync", sSt[0], sSt[0].n)
            b.wait("sync", sSt[1], sSt[1].n)
            b.flush()
    return nc


_PROG = None


def _get_prog():
    global _PROG
    if _PROG is None:
        _PROG = build_program()
    return _PROG


def kernel(**inputs):
    f32 = np.float32

    def cc(a):
        return np.ascontiguousarray(a, dtype=f32)

    ins = {k[len("L0_attn_"):]: v for k, v in inputs.items() if k.startswith("L0_attn_")}
    ins.update({k[len("L0_ffn_"):]: v for k, v in inputs.items() if k.startswith("L0_ffn_")})

    base = {}
    for s in "fb":
        win = np.asarray(ins[f"win_{s}"], dtype=f32)
        base[f"win_sl_{s}"] = cc(win.reshape(NXT, P, NKD, P).transpose(0, 3, 2, 1)
                                 .reshape(NXT, P, D))
        base[f"wout_sl_{s}"] = cc(np.asarray(ins[f"wout_{s}"], dtype=f32).T
                                  .reshape(NET, P, D))
        base[f"wx_sl_{s}"] = cc(np.asarray(ins[f"wx_{s}"], dtype=f32).T
                                .reshape(NET, P, 96))
        base[f"wdt_{s}"] = cc(np.asarray(ins[f"wdt_{s}"], dtype=f32).T)
        bconv = np.asarray(ins[f"bconv_{s}"], dtype=f32).reshape(NET, P).T
        wconv = np.asarray(ins[f"wconv_{s}"], dtype=f32)[:, 0, :] \
            .reshape(NET, P, 4).transpose(1, 0, 2).reshape(P, NET * 4)
        bdt = np.asarray(ins[f"bdt_{s}"], dtype=f32).reshape(NET, P).T
        Dp = np.asarray(ins[f"Dp_{s}"], dtype=f32).reshape(NET, P).T
        base[f"cvec_{s}"] = cc(np.concatenate([bconv, wconv, bdt, Dp], axis=1))
        base[f"alog_{s}"] = cc(np.asarray(ins[f"Alog_{s}"], dtype=f32)
                               .reshape(NET, P, NS).transpose(1, 0, 2).reshape(P, NET * NS))
    base["w1t_sl"] = cc(np.asarray(ins["W1"], dtype=f32).T.reshape(NKD, P, FF))
    base["w2t_sl"] = cc(np.asarray(ins["W2"], dtype=f32).T.reshape(NFT, P, D))
    base["wproj_sl"] = cc(np.asarray(ins["wproj"], dtype=f32).T.reshape(NET, P, D))
    base["b1col"] = cc(np.asarray(ins["b1"], dtype=f32).reshape(NFT, P).T)
    base["b2row"] = cc(np.asarray(ins["b2"], dtype=f32).reshape(1, D))
    base["bprojrow"] = cc(np.asarray(ins["bproj"], dtype=f32).reshape(1, D))
    base["identity"] = np.eye(P, dtype=f32)

    x_all = np.asarray(ins["x"], dtype=f32)
    n_cores = x_all.shape[0]
    in_maps = []
    for c in range(n_cores):
        m = dict(base)
        m["x"] = cc(x_all[c])
        in_maps.append(m)

    nc = _get_prog()
    res = run_bass_kernel_spmd(nc, in_maps, core_ids=list(range(n_cores)))
    out = np.stack([np.asarray(r["y"], dtype=np.float32) for r in res.results], axis=0)
    return out


# version 23
# speedup vs baseline: 1.0842x; 1.0076x over previous
import numpy as np
import ml_dtypes
from contextlib import ExitStack
import concourse.bass as bass
import concourse.mybir as mybir
from concourse.bass_utils import run_bass_kernel_spmd

F32 = mybir.dt.float32
BF16 = mybir.dt.bfloat16
AF = mybir.ActivationFunctionType
ALU = mybir.AluOpType
AX = mybir.AxisListType.X

P = 128
T, D, E, NS, RK, FF = 2048, 1024, 2048, 16, 64, 4096
TC = 512
NCH = T // TC
NET = E // P
NXT = 2 * E // P
NKD = D // P
NTT = T // P
NFT = FF // P
GRP = 4
NGR = NS // GRP
EPSQ = 1e-5
ENGS = ("sync", "gpsimd", "vector", "scalar", "tensor")


class Sem:
    def __init__(self, nc, ctx, name):
        self.h = ctx.enter_context(nc.semaphore(name))
        self.n = 0


class Bld:
    def __init__(self, nc):
        self.nc = nc
        self.st = {e: [] for e in ENGS}

    def em(self, eng, fn):
        self.st[eng].append(fn)

    def wait(self, eng, sem, v):
        if v and v > 0:
            self.em(eng, lambda e, h=sem.h, vv=v: e.wait_ge(h, vv))

    def dma(self, eng, sem, out, in_):
        sem.n += 16
        self.em(eng, lambda e, o=out, i=in_, h=sem.h:
                e.dma_start(out=o, in_=i).then_inc(h, 16))
        return sem.n

    def op(self, eng, fn, sem=None):
        if sem is None:
            self.em(eng, fn)
            return None
        sem.n += 1
        self.em(eng, lambda e, f=fn, h=sem.h: f(e).then_inc(h, 1))
        return sem.n

    def barrier(self, bar, dma_sems=()):
        for eng, sem in dma_sems:
            self.wait(eng, sem, sem.n)
        for eng in ENGS:
            self.em(eng, lambda e, h=bar.h: e.sem_inc(h, 1))
        bar.n += 5
        tgt = bar.n
        for eng in ENGS:
            self.wait(eng, bar, tgt)

    def flush(self):
        st = self.st
        self.st = {e: [] for e in ENGS}
        with self.nc.Block() as blk:
            for eng in ENGS:
                def runner(e, fns=tuple(st[eng])):
                    for f in fns:
                        f(e)
                getattr(blk, eng)(runner)


def build_program():
    nc = bass.Bass("TRN2", target_bir_lowering=False)
    dr = {}

    def din(name, shape):
        dr[name] = nc.dram_tensor(name, list(shape), F32, kind="ExternalInput").ap()

    din("x", (T, D))
    for s in "fb":
        din(f"win_sl_{s}", (NXT, P, D))
        din(f"wout_sl_{s}", (NET, P, D))
        din(f"wx_sl_{s}", (NET, P, 96))
        din(f"wdt_{s}", (RK, E))
        din(f"cvec_{s}", (P, 112))
        din(f"alog_{s}", (P, NET * NS))
    din("w1t_sl", (NKD, P, FF))
    din("w2t_sl", (NFT, P, D))
    din("wproj_sl", (NET, P, D))
    din("b1col", (P, NFT))
    din("b2row", (1, D))
    din("bprojrow", (1, D))
    din("identity", (P, P))

    cat_scr = nc.dram_tensor("cat_scr", [2 * D, T], BF16, kind="Internal").ap()
    bc_scr = nc.dram_tensor("bc_scr", [2, NS * TC], BF16, kind="Internal").ap()
    w1q_scr = nc.dram_tensor("w1q_scr", [NKD, P, FF], BF16, kind="Internal").ap()
    w2q_scr = nc.dram_tensor("w2q_scr", [NFT, P, D], BF16, kind="Internal").ap()
    h1_scr = nc.dram_tensor("h1_scr", [NFT, P, T], BF16, kind="Internal").ap()
    y_out = nc.dram_tensor("y", [T, D], F32, kind="ExternalOutput").ap()

    with ExitStack() as ctx:
        sPE = Sem(nc, ctx, "sPE")
        sV = Sem(nc, ctx, "sV")
        sA = Sem(nc, ctx, "sA")
        bar = Sem(nc, ctx, "bar")
        sBulkH = Sem(nc, ctx, "sBulkH")
        sBulkG = Sem(nc, ctx, "sBulkG")
        sX = [Sem(nc, ctx, f"sX{i}") for i in range(2)]
        sGam = [Sem(nc, ctx, f"sGam{i}") for i in range(2)]
        sPreL = [Sem(nc, ctx, f"sPreL{i}") for i in range(2)]
        sPreS = [Sem(nc, ctx, f"sPreS{i}") for i in range(2)]
        sWin = [Sem(nc, ctx, f"sWin{i}") for i in range(2)]
        sBC = Sem(nc, ctx, "sBC")
        sQl = Sem(nc, ctx, "sQl")
        sQs = Sem(nc, ctx, "sQs")
        sCat = [Sem(nc, ctx, f"sCat{i}") for i in range(2)]
        sW2s = [Sem(nc, ctx, f"sW2s{i}") for i in range(2)]
        sCatL = [Sem(nc, ctx, f"sCatL{i}") for i in range(2)]
        sSt = [Sem(nc, ctx, f"sSt{i}") for i in range(2)]
        sGp = Sem(nc, ctx, "sGp")
        sBC2 = Sem(nc, ctx, "sBC2")
        sBCr = Sem(nc, ctx, "sBCr")
        sBC2r = Sem(nc, ctx, "sBC2r")
        sCatG = [Sem(nc, ctx, f"sCatG{i}") for i in range(2)]
        sXG = [Sem(nc, ctx, f"sXG{i}") for i in range(2)]
        sWinG = [Sem(nc, ctx, f"sWinG{i}") for i in range(2)]
        sW1f = [Sem(nc, ctx, f"sW1f{i}") for i in range(2)]
        sH1p = [Sem(nc, ctx, f"sH1p{i}") for i in range(2)]
        sH1G = [Sem(nc, ctx, f"sH1G{i}") for i in range(2)]
        sH1G2 = [Sem(nc, ctx, f"sH1G2{i}") for i in range(2)]

        xn = ctx.enter_context(nc.sbuf_tensor("xn_cm", [P, NKD * T], BF16))
        ident = ctx.enter_context(nc.sbuf_tensor("ident", [P, P], BF16))
        gfin = ctx.enter_context(nc.sbuf_tensor("gfin", [P, 16], F32))

        px = [ctx.enter_context(nc.psum_tensor(f"px{i}", [P, TC], F32)) for i in range(2)]
        pdb = ctx.enter_context(nc.psum_tensor("pdb", [P, TC], F32))
        pd = [ctx.enter_context(nc.psum_tensor(f"pd{i}", [P, TC], F32)) for i in range(2)]
        py = [ctx.enter_context(nc.psum_tensor(f"py{i}", [P, TC], F32)) for i in range(2)]
        pw = ctx.enter_context(nc.psum_tensor("pw", [P, TC], F32))
        pall = [px[0], px[1], pdb, pd[0], pd[1], py[0], py[1], pw]

        b = Bld(nc)

        def xn_sl(k, t0, tl):
            return xn[:, k * T + t0: k * T + t0 + tl]

        with ExitStack() as p0:
            xst = p0.enter_context(nc.sbuf_tensor("xst", [P, 2 * D], F32))
            sqst = p0.enter_context(nc.sbuf_tensor("sqst", [P, D], F32))
            stats = p0.enter_context(nc.sbuf_tensor("stats", [P, 8 * NTT], F32))
            xnt = p0.enter_context(nc.sbuf_tensor("xnt", [P, 2 * D], F32))
            wstg = p0.enter_context(nc.sbuf_tensor("wstg", [P, 2 * FF], F32))
            gacc = p0.enter_context(nc.sbuf_tensor("gacc", [P, 48], F32))
            onesf = p0.enter_context(nc.sbuf_tensor("onesf", [P, P], F32))

            identf = p0.enter_context(nc.sbuf_tensor("identf", [P, P], F32))
            b.dma("gpsimd", sBulkG, ident[:], dr["identity"])
            v_id = b.dma("sync", sBulkH, identf[:], dr["identity"])
            b.op("vector", lambda e: e.memset(onesf[:], 1.0))
            b.op("vector", lambda e: e.memset(gfin[:, 15:16], 1e-5), sV)

            gunits = [("w1", k) for k in range(NKD)] + \
                     [("w2", jj) for jj in range(NFT)]
            gst = {"ld": 0, "rd": 0, "vld": {}, "vr": {}}

            def gpump(nu):
                for _ in range(nu):
                    if gst["rd"] < gst["ld"]:
                        u = gst["rd"]
                        gst["rd"] = u + 1
                        kind, a = gunits[u]
                        w = FF if kind == "w1" else D
                        sl = wstg[:, (u % 2) * FF:(u % 2) * FF + w]
                        b.wait("vector", sGam[u % 2], gst["vld"][u])
                        gst["vr"][u] = b.op("vector", lambda e,
                            o=gacc[:, (u if u < NKD else 8 + u - NKD):
                                   (u if u < NKD else 8 + u - NKD) + 1], s=sl:
                            e.tensor_reduce(out=o, in_=s, axis=AX, op=ALU.add,
                                            apply_absolute_value=True), sV)
                    if gst["ld"] < len(gunits):
                        u = gst["ld"]
                        gst["ld"] = u + 1
                        kind, a = gunits[u]
                        w = FF if kind == "w1" else D
                        sl = wstg[:, (u % 2) * FF:(u % 2) * FF + w]
                        if u >= 2:
                            b.wait("gpsimd", sV, gst["vr"][u - 2])
                        src_ap = dr["w1t_sl"][a] if kind == "w1" \
                            else dr["w2t_sl"][a]
                        gst["vld"][u] = b.dma("gpsimd", sGam[u % 2], sl, src_ap)

            st1 = []
            for i in range(NTT):
                if i >= 2:
                    b.wait("sync", sV, st1[i - 2][0])
                    b.wait("sync", sA, st1[i - 2][1])
                sl = xst[:, (i % 2) * D:(i % 2 + 1) * D]
                vld = b.dma("sync", sX[i % 2], sl, dr["x"][i * P:(i + 1) * P, :])
                b.wait("vector", sX[i % 2], vld)
                vr = b.op("vector", lambda e, o=stats[:, i:i + 1], s=sl:
                          e.reduce_sum(out=o, in_=s, axis=AX), sV)
                b.wait("scalar", sX[i % 2], vld)
                if i >= 1:
                    b.wait("scalar", sA, st1[i - 1][1])
                va = b.op("scalar", lambda e, o=sqst[:], s=sl,
                          a=stats[:, NTT + i:NTT + i + 1]:
                          e.activation(out=o, in_=s, func=AF.Square, accum_out=a), sA)
                st1.append((vr, va))
                gpump(1)

            sum_c = stats[:, 0:NTT]
            m2_c = stats[:, NTT:2 * NTT]
            mean_c = stats[:, 2 * NTT:3 * NTT]
            q_c = stats[:, 3 * NTT:4 * NTT]
            var_c = stats[:, 4 * NTT:5 * NTT]
            std_c = stats[:, 5 * NTT:6 * NTT]
            rstd_c = stats[:, 6 * NTT:7 * NTT]
            nmr_c = stats[:, 7 * NTT:8 * NTT]
            b.wait("vector", sA, sA.n)
            b.wait("vector", sV, st1[-1][0])
            v_mean = b.op("vector", lambda e: e.tensor_scalar(out=mean_c, in0=sum_c,
                          scalar1=1.0 / D, scalar2=None, op0=ALU.mult), sV)
            b.wait("vector", sV, v_mean)
            v_q = b.op("vector", lambda e: e.tensor_mul(out=q_c, in0=mean_c,
                       in1=mean_c), sV)
            b.wait("vector", sV, v_q)
            b.op("vector", lambda e: e.scalar_tensor_tensor(out=var_c, in0=m2_c,
                 scalar=1.0 / D, in1=q_c, op0=ALU.mult, op1=ALU.subtract), sV)
            b.wait("scalar", sV, sV.n)
            va_std = b.op("scalar", lambda e: e.activation(out=std_c, in_=var_c,
                          func=AF.Sqrt, bias=gfin[:, 15:16]), sA)
            b.wait("vector", sA, va_std)
            v_rec = b.op("vector", lambda e: e.reciprocal(out=rstd_c, in_=std_c), sV)
            b.op("vector", lambda e: e.tensor_copy(out=sqst[:, 0:NTT], in_=std_c))
            b.wait("vector", sV, v_rec)
            v_nmr = b.op("vector", lambda e: e.scalar_tensor_tensor(out=nmr_c,
                         in0=mean_c, scalar=-1.0, in1=rstd_c, op0=ALU.mult,
                         op1=ALU.mult), sV)

            ev_tr = []
            ap2 = []
            b.wait("sync", sV, sV.n)
            b.wait("sync", sA, sA.n)
            b.wait("tensor", sBulkH, v_id)
            b.wait("tensor", sBulkG, 16)
            for i in range(NTT):
                if i >= 2:
                    b.wait("sync", sV, ap2[i - 2][0])
                sl = xst[:, (i % 2) * D:(i % 2 + 1) * D]
                xnsl = xnt[:, (i % 2) * D:(i % 2 + 1) * D]
                vld = b.dma("sync", sX[i % 2], sl, dr["x"][i * P:(i + 1) * P, :])
                b.wait("vector", sX[i % 2], vld)
                b.wait("vector", sV, v_nmr)
                if i >= 2:
                    b.wait("vector", sPE, ap2[i - 2][1])
                va = b.op("vector", lambda e, o=xnsl, s=sl, r=rstd_c[:, i:i + 1],
                          nm=nmr_c[:, i:i + 1]:
                          e.tensor_scalar(out=o, in0=s, scalar1=r, scalar2=nm,
                                          op0=ALU.mult, op1=ALU.add), sV)
                b.wait("tensor", sV, va)
                vpe_last = 0
                for k in range(NKD):
                    idx = i * NKD + k
                    bank = px[idx % 2]
                    if idx >= 2:
                        b.wait("tensor", sA, ev_tr[idx - 2])
                    vpe = b.op("tensor", lambda e, o=bank[:, 0:P],
                               inp=xnsl[:, k * P:(k + 1) * P]:
                               e.transpose(out=o, in_=inp, identity=identf[:]), sPE)
                    b.wait("scalar", sPE, vpe)
                    ev = b.op("scalar", lambda e, o=xn_sl(k, i * P, P), pb=bank:
                              e.activation(out=o, in_=pb[:, 0:P], func=AF.Copy), sA)
                    ev_tr.append(ev)
                    vpe_last = vpe
                ap2.append((va, vpe_last))
                gpump(2)


            while gst["rd"] < len(gunits):
                gpump(1)
            b.wait("vector", sV, gst["vr"][len(gunits) - 1])
            v_r1 = b.op("vector", lambda e: e.reduce_sum(out=gacc[:, 40:41],
                        in_=gacc[:, 0:NKD], axis=AX), sV)
            v_ga = b.op("vector", lambda e: e.reduce_sum(out=gacc[:, 41:42],
                        in_=gacc[:, 8:8 + NFT], axis=AX), sV)
            b.wait("tensor", sV, v_ga)
            vmm = b.op("tensor", lambda e: e.matmul(out=pw[:, 0:2], lhsT=onesf[:],
                       rhs=gacc[:, 40:42], start=True, stop=True), sPE)
            b.wait("vector", sPE, vmm)
            v_gc = b.op("vector", lambda e: e.tensor_copy(out=gfin[:, 10:12],
                        in_=pw[:, 0:2]), sV)
            b.wait("vector", sV, v_gc)
            v_t1 = b.op("vector", lambda e: e.tensor_scalar(out=gfin[:, 0:1],
                 in0=gfin[:, 10:11], scalar1=0.5 / float(D * FF), scalar2=0.5 * EPSQ,
                 op0=ALU.mult, op1=ALU.add), sV)
            v_t2 = b.op("vector", lambda e: e.tensor_scalar(out=gfin[:, 1:2],
                 in0=gfin[:, 11:12], scalar1=0.5 / float(FF * D), scalar2=0.5 * EPSQ,
                 op0=ALU.mult, op1=ALU.add), sV)
            b.wait("vector", sV, v_t2)
            b.op("vector", lambda e: e.tensor_scalar(out=gfin[:, 2:3], in0=gfin[:, 0:1],
                 scalar1=-1.0, scalar2=None, op0=ALU.mult), sV)
            b.op("vector", lambda e: e.tensor_scalar(out=gfin[:, 3:4], in0=gfin[:, 1:2],
                 scalar1=-1.0, scalar2=None, op0=ALU.mult), sV)
            b.barrier(bar, [("sync", sBulkH), ("sync", sX[0]), ("sync", sX[1]), ("sync", sPreS[0]), ("sync", sPreS[1]), ("sync", sWin[0]), ("sync", sWin[1]), ("sync", sW2s[0]), ("sync", sW2s[1]), ("sync", sCatL[0]), ("sync", sCatL[1]), ("gpsimd", sBulkG), ("gpsimd", sGam[0]), ("gpsimd", sGam[1]), ("gpsimd", sPreL[0]), ("gpsimd", sPreL[1]), ("gpsimd", sBC), ("gpsimd", sBC2), ("gpsimd", sBCr), ("gpsimd", sBC2r), ("gpsimd", sQl), ("gpsimd", sQs), ("gpsimd", sCatG[0]), ("gpsimd", sCatG[1]), ("gpsimd", sXG[0]), ("gpsimd", sXG[1]), ("gpsimd", sWinG[0]), ("gpsimd", sWinG[1]), ("gpsimd", sCat[0]), ("gpsimd", sCat[1])])
            b.flush()

        with ExitStack() as p12:
            wslot = p12.enter_context(nc.sbuf_tensor("wslot", [P, 2 * D], BF16))
            wout_sb = p12.enter_context(nc.sbuf_tensor("wout_sb", [P, NET * D], BF16))
            wx_sb = p12.enter_context(nc.sbuf_tensor("wx_sb", [P, 2 * NET * 96], BF16))
            wdt_sb = p12.enter_context(nc.sbuf_tensor("wdt_sb", [RK, E], BF16))
            cvec = p12.enter_context(nc.sbuf_tensor("cvec", [P, 2 * 112], F32))
            atmp = p12.enter_context(nc.sbuf_tensor("atmp", [P, NET * NS], F32))
            A_sb = p12.enter_context(nc.sbuf_tensor("A_sb", [P, 2 * NET * NS], F32))
            u_ext = p12.enter_context(nc.sbuf_tensor("u_ext", [P, 2 * (TC + 16)], BF16))
            utail = p12.enter_context(nc.sbuf_tensor("utail", [P, 4 * NET], BF16))
            cacc = p12.enter_context(nc.sbuf_tensor("cacc", [P, 2 * TC], BF16))
            u_sil = p12.enter_context(nc.sbuf_tensor("u_sil", [P, NET * TC], BF16))
            z_sb = p12.enter_context(nc.sbuf_tensor("z_sb", [P, NET * TC], BF16))
            dbc_sb = p12.enter_context(nc.sbuf_tensor("dbc_sb", [P, TC], BF16))
            B_bc = p12.enter_context(nc.sbuf_tensor("B_bc", [P, NS * TC], BF16))
            C_bc = p12.enter_context(nc.sbuf_tensor("C_bc", [P, NS * TC], BF16))
            delta_sb = p12.enter_context(nc.sbuf_tensor("delta_sb", [P, 2 * TC], BF16))
            spe = p12.enter_context(nc.sbuf_tensor("spe", [P, TC], BF16))
            du_sb = p12.enter_context(nc.sbuf_tensor("du_sb", [P, 2 * TC], BF16))
            TCP = TC + 2
            dA_sl = p12.enter_context(nc.sbuf_tensor("dA_sl", [P, 2 * GRP * TCP], BF16))
            h_sl = p12.enter_context(nc.sbuf_tensor("h_sl", [P, 2 * GRP * TCP], BF16))
            dBu_sb = p12.enter_context(nc.sbuf_tensor("dBu_sb", [P, GRP * TCP], BF16))
            hcar = p12.enter_context(nc.sbuf_tensor("hcar", [P, NET * NS], F32))
            ytmp = p12.enter_context(nc.sbuf_tensor("ytmp", [P, TC], BF16))
            Y_sb = p12.enter_context(nc.sbuf_tensor("Y_sb", [P, NET * TC], BF16))
            cstage = p12.enter_context(nc.sbuf_tensor("cstage", [P, 2 * TC], BF16))
            qst = p12.enter_context(nc.sbuf_tensor("qst", [P, TC], F32))
            qsg = p12.enter_context(nc.sbuf_tensor("qsg", [P, 2 * TC], BF16))
            w1f = p12.enter_context(nc.sbuf_tensor("w1f", [P, 2 * NKD * P], BF16))
            h1st = p12.enter_context(nc.sbuf_tensor("h1st", [P, 2 * TC], BF16))
            b1c12 = p12.enter_context(nc.sbuf_tensor("b1c12", [P, NFT], F32))

            jobs = []
            for half, s in enumerate("fb"):
                crange = range(NCH) if s == "f" else range(NCH - 1, -1, -1)
                for c in crange:
                    jobs.append((s, c, half))

            qunits = [("w1", k, q) for k in range(NKD) for q in range(8)] + \
                     [("w2", jj, q) for jj in range(NFT) for q in range(2)]
            qstate = {"ld": 0, "cp": 0, "sign": 0, "store": 0, "vld": {}, "stv": {}}

            def q_src_dst(i):
                kind, a, q = qunits[i]
                if kind == "w1":
                    return (dr["w1t_sl"][a][:, q * TC:(q + 1) * TC],
                            w1q_scr[a][:, q * TC:(q + 1) * TC],
                            gfin[:, 2:3], gfin[:, 0:1])
                return (dr["w2t_sl"][a][:, q * TC:(q + 1) * TC],
                        w2q_scr[a][:, q * TC:(q + 1) * TC],
                        gfin[:, 3:4], gfin[:, 1:2])

            def pump_quant(nu):
                for _ in range(nu):
                    if qstate["cp"] < qstate["ld"]:
                        i = qstate["cp"]
                        qstate["cp"] = i + 1
                        srcp, dst, thn, thp = q_src_dst(i)
                        qsl = qst[:, 0:TC]
                        s1 = qsg[:, 0:TC]
                        s2 = qsg[:, TC:2 * TC]
                        b.wait("scalar", sQl, qstate["vld"][i])
                        b.wait("scalar", sQs, qstate["store"])
                        b.op("scalar", lambda e, o=s1, s=qsl, t=thn:
                             e.activation(out=o, in_=s, func=AF.Sign, bias=t))
                        qstate["sign"] = b.op("scalar", lambda e, o=s2, s=qsl, t=thp:
                                              e.activation(out=o, in_=s, func=AF.Sign,
                                                           bias=t), sA)
                        b.wait("vector", sA, qstate["sign"])
                        v_ad = b.op("vector", lambda e, o=s1, a1=s1, a2=s2:
                                    e.tensor_add(out=o, in0=a1, in1=a2), sV)
                        b.wait("gpsimd", sV, v_ad)
                        qstate["store"] = b.dma("gpsimd", sQs, dst, s1)
                        qstate["stv"][i] = qstate["store"]
                    if qstate["ld"] < len(qunits):
                        i = qstate["ld"]
                        qstate["ld"] = i + 1
                        srcp, dst, thn, thp = q_src_dst(i)
                        qsl = qst[:, 0:TC]
                        if i >= 1:
                            b.wait("gpsimd", sA, qstate["sign"])
                        qstate["vld"][i] = b.dma("gpsimd", sQl, qsl, srcp)

            h1s = {"qm": 0, "qe": 0, "vld": {}, "mmv": {}, "mmft": {},
                   "relu": {}, "stv": {}}

            def h1_evac():
                    if h1s["qe"] < h1s["qm"]:
                        q = h1s["qe"]
                        h1s["qe"] = q + 1
                        ft, c = q // NCH, q % NCH
                        b.wait("scalar", sPE, h1s["mmv"][q])
                        pst = h1s["stv"].get(q - 2)
                        if pst:
                            b.wait("scalar", sH1p[q % 2], pst)
                        h1s["relu"][q] = b.op("scalar", lambda e,
                            o=h1st[:, (q % 2) * TC:(q % 2 + 1) * TC], pb=pw,
                            bc=b1c12[:, ft:ft + 1]:
                            e.activation(out=o, in_=pb[:], func=AF.Relu, bias=bc,
                                         scale=0.5), sA)
                        b.wait("sync", sA, h1s["relu"][q])
                        h1s["stv"][q] = b.dma("sync", sH1p[q % 2],
                                              h1_scr[ft][:, c * TC:(c + 1) * TC],
                                              h1st[:, (q % 2) * TC:(q % 2 + 1) * TC])

            def pump_h1(nu):
                for _ in range(nu):
                    h1_evac()
                    q = h1s["qm"]
                    if q >= NFT * NCH:
                        continue
                    h1s["qm"] = q + 1
                    ft, c = q // NCH, q % NCH
                    fpar = ft % 2
                    if c == 0:
                        prev_mm = h1s["mmft"].get(ft - 2)
                        if prev_mm:
                            b.wait("sync", sPE, prev_mm)
                        b.wait("sync", sQs, qstate["stv"][7 * NKD + ft // 4])
                        h1s["vld"][ft] = b.dma("sync", sW1f[fpar],
                            w1f[:, fpar * NKD * P:(fpar + 1) * NKD * P],
                            w1q_scr[0:NKD, :, ft * P:(ft + 1) * P]
                            .transpose([1, 0, 2]))
                        b.wait("tensor", sW1f[fpar], h1s["vld"][ft])
                    pr = h1s["relu"].get(q - 1)
                    if pr:
                        b.wait("tensor", sA, pr)
                    wv = mk.get("pw_ev")
                    if wv:
                        b.wait("tensor", sV, wv)
                    for k in range(NKD):
                        lastk = (k == NKD - 1)
                        v = b.op("tensor", lambda e, o=pw[:],
                                 w=w1f[:, fpar * NKD * P + k * P:
                                       fpar * NKD * P + (k + 1) * P],
                                 r=xn_sl(k, c * TC, TC), st=(k == 0), sp=lastk:
                                 e.matmul(out=o, lhsT=w, rhs=r, start=st, stop=sp),
                                 sPE if lastk else None)
                    h1s["mmv"][q] = v
                    h1s["mmft"][ft] = v

            mk = {
                "xz_mm": {}, "xz_ev": {}, "conv": {}, "usil": {},
                "dbc_ev": {}, "bc_dma": {}, "minev": {}, "lnsp": {}, "dA": {},
                "scan": {}, "yacc": {}, "gate": {}, "wout_last": {}, "wout_ev": {},
                "cat_dma": {}, "delta_mm": {}, "zsil": {},
                "bc_users": {}, "prep": {}, "b_dma": {},
                "b_rest": {}, "bc_rest": {}, "wout_mm": {},
            }
            GG = {"n": 0}
            WINLD = {"cnt": [0, 0]}

            def dirpar(j):
                return jobs[j][2]

            def emit_scan_memsets(fwd):
                bk = 0 if fwd else (TC + 1)
                fl = 1 if fwd else TC
                v = None
                for gp in range(2):
                    base = gp * GRP * TCP
                    v = b.op("vector", lambda e, o=dA_sl[:, base + bk:base + bk
                             + (GRP - 1) * TCP + 1:TCP]: e.memset(o, 0.0), sV)
                    v = b.op("vector", lambda e, o=dA_sl[:, base + fl:base + fl
                             + (GRP - 1) * TCP + 1:TCP]: e.memset(o, 1.0), sV)
                v = b.op("vector", lambda e, o=dBu_sb[:, fl:fl
                         + (GRP - 1) * TCP + 1:TCP]: e.memset(o, 0.0), sV)
                return v

            def emit_dir_consts(s, half):
                b.dma("sync", sBulkH, cvec[:, half * 112:(half + 1) * 112],
                      dr[f"cvec_{s}"])
                v_al = b.dma("sync", sBulkH, atmp[:], dr[f"alog_{s}"])
                b.wait("scalar", sBulkH, v_al)
                asl = A_sb[:, half * NET * NS:(half + 1) * NET * NS]
                v_ex = b.op("scalar", lambda e, o=asl: e.activation(
                    out=o, in_=atmp[:], func=AF.Exp), sA)
                b.wait("vector", sA, v_ex)
                v_neg = b.op("vector", lambda e, o=asl: e.tensor_scalar(
                    out=o, in0=o, scalar1=-1.0, scalar2=None, op0=ALU.mult), sV)
                for jj in range(2):
                    b.dma("gpsimd", sBulkG,
                          wx_sb[:, (half * NET + jj * 8) * 96:(half * NET + (jj + 1) * 8) * 96],
                          dr[f"wx_sl_{s}"][8 * jj:8 * (jj + 1)].transpose([1, 0, 2]))
                return v_neg, sBulkH.n, sBulkG.n

            def emit_wdt_load(s, gate_pe):
                if gate_pe:
                    b.wait("gpsimd", sPE, gate_pe)
                b.dma("gpsimd", sBulkG, wdt_sb[:], dr[f"wdt_{s}"])
                return sBulkG.n

            def emit_wout_load(s, gate_pe):
                b.wait("gpsimd", sPE, gate_pe)
                for jj in range(NKD):
                    b.dma("gpsimd", sBulkG, wout_sb[:, jj * 2 * D:(jj + 1) * 2 * D],
                          dr[f"wout_sl_{s}"][2 * jj:2 * (jj + 1)].transpose([1, 0, 2]))
                return sBulkG.n

            def emit_A_mm(j, et):
                s, c, half = jobs[j]
                fwd = (s == "f")
                t0 = c * TC
                par = et % 2
                for xt, bank in ((et, px[par]), (NET + et, pdb)):
                    pk = WINLD["cnt"][par]
                    prev_use = mk["xz_mm"].get(("ld", par, pk - 1))
                    if prev_use:
                        b.wait("gpsimd", sPE, prev_use)
                    vld = b.dma("gpsimd", sWinG[par], wslot[:, par * D:(par + 1) * D],
                                dr[f"win_sl_{s}"][xt])
                    WINLD["cnt"][par] = pk + 1
                    b.wait("tensor", sWinG[par], vld)
                    if xt < NET:
                        prev_ev = mk["xz_ev"].get((j, et - 2)) or \
                            (mk["xz_ev"].get((j - 1, NET - 1)) if et < 2 else None)
                    else:
                        prev_ev = mk["zsil"].get((j, et - 1)) or \
                            (mk["zsil"].get((j - 1, NET - 1)) if et == 0 else None)
                        pz = mk.get("pdb_ev")
                        if pz and et == 0:
                            b.wait("tensor", sV, pz)
                    if prev_ev:
                        b.wait("tensor", sA, prev_ev)
                    for k in range(NKD):
                        last = (k == NKD - 1)
                        v = b.op("tensor", lambda e, o=bank[:],
                                 w=wslot[:, par * D + k * P:par * D + (k + 1) * P],
                                 r=xn_sl(k, t0, TC), st=(k == 0), sp=last:
                                 e.matmul(out=o, lhsT=w, rhs=r, start=st, stop=sp),
                                 sPE if last else None)
                    mk["xz_mm"][("ld", par, pk)] = v
                    mk["xz_mm"][("mm", j, xt)] = v

            def emit_A_evac_u(j, et):
                s, c, half = jobs[j]
                fwd = (s == "f")
                par = et % 2
                uo = u_ext[:, par * (TC + 16) + (3 if fwd else 0):
                           par * (TC + 16) + (3 if fwd else 0) + TC]
                b.wait("scalar", sPE, mk["xz_mm"][("mm", j, et)])
                prev_conv = mk["conv"].get((j, et - 2)) or \
                    (mk["conv"].get((j - 1, NET - 2 + et)) if et < 2 else None)
                if prev_conv:
                    b.wait("scalar", sV, prev_conv)
                if j > 0:
                    g8 = mk["gate"].get((j - 1, et))
                    if g8:
                        b.wait("scalar", sV, g8)
                mk["xz_ev"][(j, et)] = b.op("scalar", lambda e, o=uo,
                    pb=px[par]: e.activation(out=o, in_=pb[:], func=AF.Copy), sA)

            def emit_A_evac_z(j, et):
                b.wait("scalar", sPE, mk["xz_mm"][("mm", j, NET + et)])
                if j > 0:
                    g8 = mk["gate"].get((j - 1, et))
                    if g8:
                        b.wait("scalar", sV, g8)
                mk["zsil"][(j, et)] = b.op("scalar", lambda e,
                    o=z_sb[:, et * TC:(et + 1) * TC], pb=pdb:
                    e.activation(out=o, in_=pb[:], func=AF.Copy), sA)

            def emit_A_conv(j, et):
                s, c, half = jobs[j]
                fwd = (s == "f")
                par = et % 2
                b.wait("vector", sA, mk["xz_ev"][(j, et)])
                prev_sil = mk["usil"].get((j, et - 2)) or \
                    (mk["usil"].get((j - 1, NET - 2 + et)) if et < 2 else None)
                if prev_sil:
                    b.wait("vector", sA, prev_sil)
                ue0 = par * (TC + 16)
                tl = utail[:, et * 4:et * 4 + 3]
                if fwd:
                    v_h = b.op("vector", lambda e, o=u_ext[:, ue0:ue0 + 3], i=tl:
                               e.tensor_copy(out=o, in_=i), sV)
                else:
                    v_h = b.op("vector", lambda e, o=u_ext[:, ue0 + TC:ue0 + TC + 3],
                               i=tl: e.tensor_copy(out=o, in_=i), sV)
                ca = cacc[:, par * TC:(par + 1) * TC]
                wks = [half * 112 + 16 + et * 4 + k for k in range(4)]
                order = [0, 1, 2, 3] if fwd else [3, 2, 1, 0]
                b.wait("vector", sV, v_h)
                v_c = b.op("vector", lambda e, o=ca, i=u_ext[:, ue0:ue0 + TC],
                           w=cvec[:, wks[order[0]]:wks[order[0]] + 1],
                           bc=cvec[:, half * 112 + et:half * 112 + et + 1]:
                           e.tensor_scalar(out=o, in0=i, scalar1=w, scalar2=bc,
                                           op0=ALU.mult, op1=ALU.add), sV)
                for jj in (1, 2, 3):
                    b.wait("vector", sV, v_c)
                    dst = ca if jj < 3 else u_sil[:, et * TC:(et + 1) * TC]
                    v_c = b.op("vector", lambda e, o=dst,
                               i=u_ext[:, ue0 + jj:ue0 + jj + TC],
                               w=cvec[:, wks[order[jj]]:wks[order[jj]] + 1], acc=ca:
                               e.scalar_tensor_tensor(out=o, in0=i, scalar=w,
                                                      in1=acc, op0=ALU.mult,
                                                      op1=ALU.add), sV)
                b.wait("vector", sV, v_c)
                if fwd:
                    mk["conv"][(j, et)] = b.op("vector", lambda e, o=tl,
                        i=u_ext[:, ue0 + TC:ue0 + TC + 3]:
                        e.tensor_copy(out=o, in_=i), sV)
                else:
                    mk["conv"][(j, et)] = b.op("vector", lambda e, o=tl,
                        i=u_ext[:, ue0:ue0 + 3]:
                        e.tensor_copy(out=o, in_=i), sV)

            def emit_silu_batch(j, ets, zets):
                s, c, half = jobs[j]
                if ets:
                    b.wait("scalar", sV, mk["conv"][(j, ets[-1])])
                if zets:
                    b.wait("scalar", sA, mk["zsil"][(j, zets[-1])])
                if ets:
                    e0, e1 = ets[0], ets[-1]
                    assert ets == list(range(e0, e1 + 1))
                    vu = b.op("scalar", lambda e,
                              o=u_sil[:, e0 * TC:(e1 + 1) * TC]:
                              e.activation(out=o, in_=o, func=AF.Silu), sA)
                    for et in ets:
                        mk["usil"][(j, et)] = vu
                if zets:
                    e0, e1 = zets[0], zets[-1]
                    assert zets == list(range(e0, e1 + 1))
                    vz = b.op("scalar", lambda e,
                              o=z_sb[:, e0 * TC:(e1 + 1) * TC]:
                              e.activation(out=o, in_=o, func=AF.Silu), sA)
                    for et in zets:
                        mk["usil"][(j, NET + et)] = vz


            def emit_dbc(j):
                s, c, half = jobs[j]
                b.wait("tensor", sA, mk["usil"][(j, NET - 1)])
                b.wait("tensor", sA, mk["zsil"][(j, NET - 1)])
                pz = mk.get("pdb_ev")
                if pz:
                    b.wait("tensor", sV, pz)
                if j > 0:
                    b.wait("tensor", sA, mk["dbc_ev"][j - 1])
                for et in range(NET):
                    last = (et == NET - 1)
                    v = b.op("tensor", lambda e, o=pdb[0:96, :],
                             w=wx_sb[:, (half * NET + et) * 96:(half * NET + et + 1) * 96],
                             r=u_sil[:, et * TC:(et + 1) * TC],
                             st=(et == 0), sp=last:
                             e.matmul(out=o, lhsT=w, rhs=r, start=st, stop=sp),
                             sPE if last else None)
                b.wait("scalar", sPE, v)
                if j > 0:
                    b.wait("scalar", sBC2r, mk["bc_rest"][j - 1])
                mk["dbc_ev"][j] = b.op("scalar", lambda e:
                                       e.activation(out=dbc_sb[0:96, 0:TC],
                                                    in_=pdb[0:96, :], func=AF.Copy), sA)
                b.wait("gpsimd", sA, mk["dbc_ev"][j])
                if j > 0:
                    b.wait("gpsimd", sV, mk["bc_users"][j - 1])
                vf1 = b.dma("gpsimd", sBC, bc_scr[0:1, :], dbc_sb[RK:RK + NS, 0:TC])
                vf2 = b.dma("gpsimd", sBC, bc_scr[1:2, :],
                            dbc_sb[RK + NS:RK + 2 * NS, 0:TC])
                b.wait("gpsimd", sBC, vf2)
                Q1 = GRP * TC
                mk["b_dma"][j] = b.dma("gpsimd", sBC, B_bc[:, 0:Q1],
                                       bc_scr[0:1, 0:Q1]
                                       .unsqueeze(1).broadcast_to([1, P, Q1]))
                mk["bc_dma"][j] = b.dma("gpsimd", sBC2, C_bc[:, 0:Q1],
                                        bc_scr[1:2, 0:Q1]
                                        .unsqueeze(1).broadcast_to([1, P, Q1]))
                mk["b_rest"][j] = b.dma("gpsimd", sBCr, B_bc[:, Q1:NS * TC],
                                        bc_scr[0:1, Q1:NS * TC]
                                        .unsqueeze(1)
                                        .broadcast_to([1, P, NS * TC - Q1]))
                mk["bc_rest"][j] = b.dma("gpsimd", sBC2r, C_bc[:, Q1:NS * TC],
                                         bc_scr[1:2, Q1:NS * TC]
                                         .unsqueeze(1)
                                         .broadcast_to([1, P, NS * TC - Q1]))

            def emit_delta(j, et):
                s, c, half = jobs[j]
                par = et % 2
                b.wait("tensor", sA, mk["dbc_ev"][j])
                wl = mk.get("wdt_pend")
                if wl is not None and j >= 4:
                    b.wait("tensor", sBulkG, wl)
                    mk["wdt_pend"] = None
                prev = mk["minev"].get((j, et - 2)) or \
                    (mk["minev"].get((j - 1, NET - 2 + et)) if et < 2 else None)
                if prev:
                    b.wait("tensor", sV, prev)
                mk["delta_mm"][(j, et)] = b.op("tensor", lambda e, o=pd[par][:],
                    w=wdt_sb[:, et * P:(et + 1) * P],
                    r=dbc_sb[0:RK, 0:TC]:
                    e.matmul(out=o, lhsT=w, rhs=r, start=True, stop=True), sPE)

            def emit_prep_v(j, et):
                s, c, half = jobs[j]
                par = et % 2
                dsl = delta_sb[:, par * TC:(par + 1) * TC]
                b.wait("vector", sPE, mk["delta_mm"][(j, et)])
                prev_dA = mk["dA"].get((j, et - 2, NGR - 1)) or \
                    (mk["dA"].get((j - 1, NET - 2 + et, NGR - 1)) if et < 2 else None)
                if prev_dA:
                    b.wait("vector", sA, prev_dA)
                mk["minev"][(j, et)] = b.op("vector", lambda e, o=dsl, pb=pd[par],
                    bd=cvec[:, half * 112 + 80 + et:half * 112 + 81 + et]:
                    e.tensor_scalar(out=o, in0=pb[:], scalar1=bd, scalar2=25.0,
                                    op0=ALU.add, op1=ALU.min), sV)

            def emit_prep_a(j, et):
                par = et % 2
                dsl = delta_sb[:, par * TC:(par + 1) * TC]
                b.wait("scalar", sV, mk["minev"][(j, et)])
                v_se = b.op("scalar", lambda e, o=spe[:], i=dsl:
                            e.activation(out=o, in_=i, func=AF.Exp), sA)
                b.wait("scalar", sA, v_se)
                mk["lnsp"][(j, et)] = b.op("scalar", lambda e, o=dsl, i=spe[:]:
                                           e.activation(out=o, in_=i, func=AF.Ln,
                                                        bias=1.0), sA)

            def emit_B_et(j, et):
                s, c, half = jobs[j]
                fwd = (s == "f")
                par = et % 2
                dsl = delta_sb[:, par * TC:(par + 1) * TC]
                dusl = du_sb[:, par * TC:(par + 1) * TC]
                b.wait("vector", sA, mk["lnsp"][(j, et)])
                b.wait("scalar", sA, mk["lnsp"][(j, et)])
                v_du = b.op("vector", lambda e, o=dusl, a1=dsl,
                            a2=u_sil[:, et * TC:(et + 1) * TC]:
                            e.tensor_mul(out=o, in0=a1, in1=a2), sV)
                b.wait("vector", sV, v_du)
                for g in range(NGR):
                    gg = GG["n"]
                    GG["n"] += 1
                    gp = gg % 2
                    if gg >= 2:
                        b.wait("scalar", sV, mk["scan"][gg - 2])
                    doff = 2 if fwd else 0
                    for jj in range(GRP):
                        n = g * GRP + jj
                        last = (jj == GRP - 1)
                        sl0 = (gp * GRP + jj) * TCP + doff
                        v = b.op("scalar", lambda e,
                                 o=dA_sl[:, sl0:sl0 + TC],
                                 i=dsl, sc=A_sb[:, half * NET * NS + et * NS + n:
                                                half * NET * NS + et * NS + n + 1]:
                                 e.activation(out=o, in_=i, func=AF.Exp, scale=sc),
                                 sA if last else None)
                    mk["dA"][(j, et, g)] = v
                    b.wait("vector", sA, v)
                    if gg >= 2:
                        b.wait("vector", sPE, mk["yacc"][gg - 2])
                    if et == 0 and g == 0:
                        b.wait("vector", sBC, mk["b_dma"][j])
                    if et == 0 and g == 1:
                        b.wait("vector", sBCr, mk["b_rest"][j])
                    base = gp * GRP * TCP
                    v_db = b.op("vector", lambda e,
                                o=dBu_sb[:].rearrange("p (s t) -> p s t", s=GRP)
                                [:, :, doff:doff + TC],
                                a1=dusl.unsqueeze(1).broadcast_to([P, GRP, TC]),
                                a2=B_bc[:, g * GRP * TC:(g + 1) * GRP * TC]
                                .rearrange("p (s t) -> p s t", s=GRP):
                                e.tensor_mul(out=o, in0=a1, in1=a2), sV)
                    b.wait("vector", sV, v_db)
                    for jj in range(GRP):
                        n = g * GRP + jj
                        sl0 = (gp * GRP + jj) * TCP + doff
                        db0 = jj * TCP + doff
                        ic = hcar[:, et * NS + n:et * NS + n + 1]
                        if fwd:
                            v_sc = b.op("vector", lambda e, o=h_sl[:, sl0:sl0 + TC],
                                        d0=dA_sl[:, sl0:sl0 + TC],
                                        d1=dBu_sb[:, db0:db0 + TC], i=ic:
                                        e.tensor_tensor_scan(out=o, data0=d0, data1=d1,
                                        initial=i, op0=ALU.mult, op1=ALU.add),
                                        sV if jj == GRP - 1 else None)
                        else:
                            v_sc = b.op("vector", lambda e, o=h_sl[:, sl0:sl0 + TC],
                                        d0=dA_sl[:, sl0:sl0 + TC],
                                        d1=dBu_sb[:, db0:db0 + TC], i=ic:
                                        e.tensor_tensor_scan(out=o[:, ::-1],
                                        data0=d0[:, ::-1], data1=d1[:, ::-1],
                                        initial=i, op0=ALU.mult, op1=ALU.add),
                                        sV if jj == GRP - 1 else None)
                    b.wait("vector", sV, v_sc)
                    off = (doff + TC - 1) if fwd else doff
                    v_tl = b.op("vector", lambda e,
                                o=hcar[:, et * NS + g * GRP:et * NS + (g + 1) * GRP],
                                i=h_sl[:, base + off:base + off
                                       + (GRP - 1) * TCP + 1:TCP]:
                                e.tensor_copy(out=o, in_=i), sV)
                    b.wait("vector", sV, v_tl)
                    if et == 0 and g == 0:
                        b.wait("vector", sBC2, mk["bc_dma"][j])
                    if et == 0 and g == 1:
                        b.wait("vector", sBC2r, mk["bc_rest"][j])
                    v = b.op("vector", lambda e,
                             o=h_sl[:, base:base + GRP * TCP]
                             .rearrange("p (s t) -> p s t", s=GRP)[:, :, doff:doff + TC],
                             a2=C_bc[:, g * GRP * TC:(g + 1) * GRP * TC]
                             .rearrange("p (s t) -> p s t", s=GRP):
                             e.tensor_mul(out=o, in0=o, in1=a2), sV)
                    mk["scan"][gg] = v
                    b.wait("tensor", sV, mk["scan"][gg])
                    if g == 0:
                        prev_gate = mk["gate"].get((j, et - 2)) or \
                            (mk["gate"].get((j - 1, NET - 2 + et)) if et < 2 else None)
                        if prev_gate:
                            b.wait("tensor", sV, prev_gate)
                    for jj in range(GRP):
                        n = g * GRP + jj
                        last = (jj == GRP - 1)
                        sl0 = (gp * GRP + jj) * TCP + doff
                        v = b.op("tensor", lambda e, o=py[par][:],
                                 r=h_sl[:, sl0:sl0 + TC],
                                 st=(n == 0), sp=(n == NS - 1):
                                 e.matmul(out=o, lhsT=ident[:], rhs=r, start=st,
                                          stop=sp), sPE if last else None)
                    mk["yacc"][gg] = v
                    cp = mk.get("cpend")
                    if et == 0 and cp is not None:
                        if g < 3:
                            emit_C_ev(cp, 2 * g)
                            emit_C_ev(cp, 2 * g + 1)
                            emit_C_mm(cp, 2 * g + 2)
                            emit_C_mm(cp, 2 * g + 3)
                        else:
                            emit_C_ev(cp, NKD - 2)
                            emit_C_ev(cp, NKD - 1)
                            mk["cpend"] = None
                    if g == 0 and et + 1 < NET:
                        emit_prep_v(j, et + 1)
                    if g == 0 and et == 1 and j >= 4:
                        pump_h1(1)
                    if g == 1:
                        if j < 7 and et >= 1:
                            emit_A_evac_u(j + 1, et - 1)
                        pump_quant(1)
                        if j >= 4 and not (et == 0
                                           and mk.get("cpend") is not None):
                            pump_h1(1)
                    if g == 2:
                        if j < 7 and et >= 1:
                            emit_A_conv(j + 1, et - 1)
                        if et + 1 < NET:
                            emit_prep_a(j, et + 1)
                    if g == 3:
                        if j < 7 and et >= 1:
                            emit_A_evac_z(j + 1, et - 1)
                        if j >= 4:
                            pump_h1(1)
                        if j < 7 and et == NET - 1:
                            emit_silu_batch(j + 1, list(range(NET - 1)),
                                            list(range(NET - 1)))
                b.wait("vector", sPE, mk["yacc"][GG["n"] - 1])
                v_g1 = b.op("vector", lambda e, o=ytmp[:],
                            i=u_sil[:, et * TC:(et + 1) * TC],
                            dp=cvec[:, half * 112 + 96 + et:half * 112 + 97 + et],
                            pb=py[par]:
                            e.scalar_tensor_tensor(out=o, in0=i, scalar=dp, in1=pb[:],
                                                   op0=ALU.mult, op1=ALU.add), sV)
                b.wait("vector", sV, v_g1)
                mk["gate"][(j, et)] = b.op("vector", lambda e,
                    o=Y_sb[:, et * TC:(et + 1) * TC], a1=ytmp[:],
                    a2=z_sb[:, et * TC:(et + 1) * TC]:
                    e.tensor_mul(out=o, in0=a1, in1=a2), sV)
                mk["bc_users"][j] = mk["scan"][GG["n"] - 1]


            def emit_C_mm(j, dt):
                s, c, half = jobs[j]
                bank = pw if dt % 2 == 0 else pdb
                if dt == 0:
                    b.wait("tensor", sV, mk["gate"][(j, NET - 1)])
                    hv = h1s["relu"].get(h1s["qe"] - 1)
                    if hv:
                        b.wait("tensor", sA, hv)
                    if j > 0:
                        b.wait("tensor", sBulkG, mk["wout_ld"])
                if dt == 1:
                    b.wait("tensor", sA, mk["dbc_ev"][min(j + 1, 7)])
                pv = mk.get("pw_ev") if dt % 2 == 0 else mk.get("pdb_ev")
                if pv:
                    b.wait("tensor", sV, pv)
                for et in range(NET):
                    last = (et == NET - 1)
                    v = b.op("tensor", lambda e, o=bank[:],
                             w=wout_sb[:, et * D + dt * P:et * D + (dt + 1) * P],
                             r=Y_sb[:, et * TC:(et + 1) * TC],
                             st=(et == 0), sp=last:
                             e.matmul(out=o, lhsT=w, rhs=r, start=st, stop=sp),
                             sPE if last else None)
                mk["wout_mm"][dt] = v
                mk["wout_last"][j] = v

            def emit_C_ev(j, dt):
                s, c, half = jobs[j]
                fwd = (s == "f")
                t0 = c * TC
                row0 = 0 if fwd else D
                bank = pw if dt % 2 == 0 else pdb
                b.wait("vector", sPE, mk["wout_mm"][dt])
                if dt >= 2:
                    b.wait("vector", sCat[dt % 2], mk["cat_dma"][dt - 2])
                elif j > 0:
                    b.wait("vector", sCat[dt % 2], mk["cat_dma"][NKD - 2 + dt])
                vv = b.op("vector", lambda e,
                          o=cstage[:, (dt % 2) * TC:(dt % 2 + 1) * TC], pb=bank:
                          e.tensor_copy(out=o, in_=pb[:]), sV)
                mk["wout_ev"][dt] = vv
                if dt % 2 == 0:
                    mk["pw_ev"] = vv
                else:
                    mk["pdb_ev"] = vv
                b.wait("gpsimd", sV, vv)
                mk["cat_dma"][dt] = b.dma("gpsimd", sCat[dt % 2],
                    cat_scr[row0 + dt * P:row0 + (dt + 1) * P, t0:t0 + TC],
                    cstage[:, (dt % 2) * TC:(dt % 2 + 1) * TC])

            def emit_C(j):
                emit_C_mm(j, 0)
                emit_C_mm(j, 1)
                for dt in range(2, NKD):
                    emit_C_ev(j, dt - 2)
                    emit_C_mm(j, dt)
                emit_C_ev(j, NKD - 2)
                emit_C_ev(j, NKD - 1)

            b.dma("gpsimd", sBulkG, b1c12[:], dr["b1col"])
            v_neg, vbh, vbg = emit_dir_consts("f", 0)
            emit_wdt_load("f", 0)
            mk["wout_ld"] = emit_wout_load("f", 0)
            vbg = sBulkG.n
            for eng in ENGS:
                b.wait(eng, sBulkH, vbh)
                b.wait(eng, sBulkG, vbg)
                b.wait(eng, sV, v_neg)
            v_ms = b.op("vector", lambda e: e.memset(hcar[:], 0.0), sV)
            v_ms = b.op("vector", lambda e: e.memset(utail[:], 0.0), sV)
            v_ms = emit_scan_memsets(True)
            b.wait("scalar", sV, v_ms)
            b.wait("tensor", sV, v_ms)

            for et in range(NET):
                emit_A_mm(0, et)
                emit_A_evac_u(0, et)
                emit_A_evac_z(0, et)
                emit_A_conv(0, et)
            emit_silu_batch(0, list(range(NET)), list(range(NET)))
            emit_dbc(0)
            emit_delta(0, 0)
            emit_delta(0, 1)
            emit_prep_v(0, 0)
            emit_prep_a(0, 0)

            for j in range(8):
                b.wait("vector", sV, sV.n)
                b.wait("scalar", sA, sA.n)
                if j == 3:
                    v_neg, vbh, vbg = emit_dir_consts("b", 1)
                    for eng in ("vector", "scalar", "tensor", "sync", "gpsimd"):
                        b.wait(eng, sBulkH, vbh)
                        b.wait(eng, sBulkG, vbg)
                        b.wait(eng, sV, v_neg)
                    v_ms = b.op("vector", lambda e: e.memset(utail[:], 0.0), sV)
                    b.wait("scalar", sV, v_ms)
                    mk["bwd_ms"] = None
                if j >= 5:
                    h1_evac()
                if j >= 1:
                    emit_C_mm(j - 1, 0)
                    emit_C_mm(j - 1, 1)
                    mk["cpend"] = j - 1
                if j == 4:
                    v_ms = b.op("vector", lambda e: e.memset(hcar[:], 0.0), sV)
                    emit_scan_memsets(False)
                for et in range(NET):
                    emit_B_et(j, et)
                    if j == 4 and et == 0:
                        mk["wout_ld"] = emit_wout_load("b", mk["wout_last"][3])
                    if et + 2 < NET:
                        emit_delta(j, et + 2)
                    if j == 3 and et == 14:
                        mk["wdt_pend"] = emit_wdt_load("b",
                                                       mk["delta_mm"][(3, 15)])
                    if j < 7:
                        emit_A_mm(j + 1, et)
                if j < 7:
                    emit_A_evac_u(j + 1, NET - 1)
                    emit_A_evac_z(j + 1, NET - 1)
                    emit_A_conv(j + 1, NET - 1)
                    emit_silu_batch(j + 1, [NET - 1], [NET - 1])
                    emit_dbc(j + 1)
                    emit_delta(j + 1, 0)
                    emit_delta(j + 1, 1)
                    emit_prep_v(j + 1, 0)
                    emit_prep_a(j + 1, 0)
            pump_quant(4)
            pump_h1(1)
            emit_C(7)
            b.barrier(bar, [("sync", sBulkH), ("sync", sX[0]), ("sync", sX[1]), ("sync", sPreS[0]), ("sync", sPreS[1]), ("sync", sWin[0]), ("sync", sWin[1]), ("sync", sW2s[0]), ("sync", sW2s[1]), ("sync", sCatL[0]), ("sync", sCatL[1]), ("sync", sW1f[0]), ("sync", sW1f[1]), ("sync", sH1p[0]), ("sync", sH1p[1]), ("gpsimd", sBulkG), ("gpsimd", sGam[0]), ("gpsimd", sGam[1]), ("gpsimd", sPreL[0]), ("gpsimd", sPreL[1]), ("gpsimd", sBC), ("gpsimd", sBC2), ("gpsimd", sBCr), ("gpsimd", sBC2r), ("gpsimd", sQl), ("gpsimd", sQs), ("gpsimd", sCatG[0]), ("gpsimd", sCatG[1]), ("gpsimd", sXG[0]), ("gpsimd", sXG[1]), ("gpsimd", sWinG[0]), ("gpsimd", sWinG[1]), ("gpsimd", sCat[0]), ("gpsimd", sCat[1])])
            b.flush()
        with ExitStack() as p34:
            b2bc = p34.enter_context(nc.sbuf_tensor("b2bc", [P, D], F32))
            bpbc = p34.enter_context(nc.sbuf_tensor("bpbc", [P, D], F32))
            gstage = p34.enter_context(nc.sbuf_tensor("gstage", [P, 2 * TC], BF16))
            w2q_sb = p34.enter_context(nc.sbuf_tensor("w2q_sb", [P, NFT * D], BF16))
            h1sl = p34.enter_context(nc.sbuf_tensor("h1sl", [P, 8 * TC], BF16))
            g_sb = p34.enter_context(nc.sbuf_tensor("g_sb", [P, 4 * D], BF16))
            wproj_sb = p34.enter_context(nc.sbuf_tensor("wproj_sb", [P, NET * D], BF16))
            catsl = p34.enter_context(nc.sbuf_tensor("catsl", [P, 2 * NET * P], BF16))
            xsl = p34.enter_context(nc.sbuf_tensor("xsl", [P, 2 * D], F32))
            sstage = p34.enter_context(nc.sbuf_tensor("sstage", [P, 2 * TC], BF16))

            b.dma("gpsimd", sBulkG, b2bc[:],
                  dr["b2row"].unsqueeze(1).broadcast_to([1, P, D]))
            b.dma("gpsimd", sBulkG, bpbc[:],
                  dr["bprojrow"].unsqueeze(1).broadcast_to([1, P, D]))
            for jj in range(NKD):
                b.dma("gpsimd", sBulkG, wproj_sb[:, jj * 2 * D:(jj + 1) * 2 * D],
                      dr["wproj_sl"][2 * jj:2 * (jj + 1)].transpose([1, 0, 2]))
            v_wp = sBulkG.n
            w2half = []
            for hl in range(2):
                for fp in range(NFT // 4):
                    fpp = hl * (NFT // 4) + fp
                    v = b.dma("sync", sW2s[hl],
                              w2q_sb[:, 2 * fpp * D:(2 * fpp + 2) * D],
                              w2q_scr[2 * fpp:2 * fpp + 2].transpose([1, 0, 2]))
                w2half.append(v)

            HB = TC // 2
            h1mm = {}
            gsil = {}
            pj_mm = {}
            pj_out = {}
            pj_st = {}
            PASSES = [([pdb, py[0], py[1], pw], [(0, 0), (0, 1), (1, 0), (1, 1)],
                       sH1G),
                      ([px[0], px[1], pd[0], pd[1]],
                       [(2, 0), (2, 1), (3, 0), (3, 1)], sH1G2)]
            NG4 = NFT // 4
            for c in range(NCH):
                t0 = c * TC
                for ps, (banks, tiles, sems) in enumerate(PASSES):
                    if ps == 0:
                        for i4, (tt, hf) in enumerate(tiles):
                            pv = gsil.get((c - 1, tt, hf))
                            if pv:
                                b.wait("tensor", sA, pv)
                    else:
                        if c > 0:
                            b.wait("tensor", sV, pj_out[4 * (c - 1) + 2])
                            b.wait("tensor", sV, pj_out[4 * (c - 1) + 3])
                    for g4 in range(NG4):
                        par = g4 % 2
                        prev = h1mm.get((c, ps, g4 - 2)) or \
                               (h1mm.get((c - 1, ps, NG4 - 2 + g4))
                                if g4 < 2 and c > 0 else None)
                        if prev:
                            b.wait("gpsimd", sPE, prev)
                        vh = b.dma("gpsimd", sems[par],
                                   h1sl[:, (ps * 2 + par) * 4 * HB:
                                        (ps * 2 + par + 1) * 4 * HB],
                                   h1_scr[4 * g4:4 * (g4 + 1),
                                          :, t0 + ps * 2 * P:t0 + ps * 2 * P + 2 * P]
                                   .transpose([1, 0, 2]))
                        b.wait("tensor", sems[par], vh)
                        if c == 0 and ps == 0:
                            b.wait("tensor", sW2s[g4 // (NG4 // 2)],
                                   w2half[g4 // (NG4 // 2)])
                        for sub in range(4):
                            ft = 4 * g4 + sub
                            for i4, (tt, hf) in enumerate(tiles):
                                last = (i4 == 3 and sub == 3)
                                v = b.op("tensor", lambda e, o=banks[i4][:],
                                         w=h1sl[:, (ps * 2 + par) * 4 * HB
                                                + sub * HB + (tt - 2 * ps) * P:
                                                (ps * 2 + par) * 4 * HB
                                                + sub * HB + (tt - 2 * ps + 1) * P],
                                         r=w2q_sb[:, ft * D + hf * TC:
                                                  ft * D + (hf + 1) * TC],
                                         st=(ft == 0), sp=(ft == NFT - 1):
                                         e.matmul(out=o, lhsT=w, rhs=r, start=st,
                                                  stop=sp), sPE if last else None)
                        h1mm[(c, ps, g4)] = v
                    b.wait("vector", sPE, v)
                    if c == 0 and ps == 0:
                        b.wait("vector", sBulkG, v_wp)
                    for i4, (tt, hf) in enumerate(tiles):
                        gsl = gstage[:, hf * TC:(hf + 1) * TC]
                        ps_prev = gsil.get((c, tt - 1, hf)) if i4 >= 2 else \
                            (gsil.get((c - 1, 3, hf)) if ps == 0 and c > 0 else
                             gsil.get((c, 1, hf)) if ps == 1 else None)
                        if ps_prev:
                            b.wait("vector", sA, ps_prev)
                        vv = b.op("vector", lambda e, o=gsl, pb=banks[i4],
                                  bb=b2bc[:, hf * TC:(hf + 1) * TC]:
                                  e.scalar_tensor_tensor(out=o, in0=pb[:],
                                      scalar=0.5, in1=bb, op0=ALU.mult,
                                      op1=ALU.add), sV)
                        b.wait("scalar", sV, vv)
                        if c > 0:
                            b.wait("scalar", sV, pj_out[4 * (c - 1) + tt])
                        gsil[(c, tt, hf)] = b.op("scalar", lambda e,
                            o=g_sb[:, tt * D + hf * TC:tt * D + (hf + 1) * TC],
                            i=gsl:
                            e.activation(out=o, in_=i, func=AF.Silu), sA)
                b.wait("tensor", sBulkG, v_wp)
                for tl in range(4):
                    ti = 4 * c + tl
                    par = ti % 2
                    prev = pj_mm.get(ti - 2)
                    if prev:
                        b.wait("gpsimd", sPE, prev)
                    if ti >= 2:
                        b.wait("gpsimd", sV, pj_out[ti - 2])
                        b.wait("gpsimd", sSt[par], pj_st[ti - 2])
                    vc = b.dma("gpsimd", sCatG[par],
                               catsl[:, par * NET * P:(par + 1) * NET * P],
                               cat_scr[:, ti * P:(ti + 1) * P]
                               .rearrange("(k p) t -> p k t", p=P))
                    vx = b.dma("gpsimd", sXG[par], xsl[:, par * D:(par + 1) * D],
                               dr["x"][ti * P:(ti + 1) * P, :])
                    b.wait("tensor", sCatG[par], vc)
                    if ti >= 2:
                        b.wait("tensor", sV, pj_out[ti - 2])
                    pjbanks = [px[0], px[1]] if tl % 2 == 0 else [pd[0], pd[1]]
                    pjtt = 2 if tl % 2 == 0 else 3
                    for hf in range(2):
                        bank = pjbanks[hf]
                        b.wait("tensor", sA, gsil[(c, pjtt, hf)])
                        for k in range(16):
                            last = (k == 15)
                            v = b.op("tensor", lambda e, o=bank[:],
                                     w=catsl[:, par * NET * P + k * P:
                                             par * NET * P + (k + 1) * P],
                                     r=wproj_sb[:, k * D + hf * TC:
                                                k * D + (hf + 1) * TC],
                                     st=(k == 0), sp=last:
                                     e.matmul(out=o, lhsT=w, rhs=r, start=st,
                                              stop=sp),
                                     sPE if last else None)
                    pj_mm[ti] = v
                    b.wait("vector", sPE, v)
                    b.wait("vector", sXG[par], vx)
                    b.wait("vector", sA, gsil[(c, tl, 1)])
                    if ti >= 2:
                        b.wait("vector", sSt[par], pj_st[ti - 2])
                    for hf in range(2):
                        bank = pjbanks[hf]
                        ssl = sstage[:, hf * TC:(hf + 1) * TC]
                        v_a = b.op("vector", lambda e, o=ssl, pb=bank,
                                   bb=bpbc[:, hf * TC:(hf + 1) * TC]:
                                   e.tensor_add(out=o, in0=pb[:], in1=bb), sV)
                        b.wait("vector", sV, v_a)
                        v_m = b.op("vector", lambda e, o=ssl, a1=ssl,
                                   a2=g_sb[:, tl * D + hf * TC:
                                           tl * D + (hf + 1) * TC]:
                                   e.tensor_mul(out=o, in0=a1, in1=a2), sV)
                        b.wait("vector", sV, v_m)
                        v2 = b.op("vector", lambda e,
                                  o=xsl[:, par * D + hf * TC:par * D + (hf + 1) * TC],
                                  a1=ssl, a2=xsl[:, par * D + hf * TC:
                                                 par * D + (hf + 1) * TC]:
                                  e.tensor_add(out=o, in0=a1, in1=a2), sV)
                    pj_out[ti] = v2
                    b.wait("sync", sV, v2)
                    pj_st[ti] = b.dma("sync", sSt[par], y_out[ti * P:(ti + 1) * P, :],
                                      xsl[:, par * D:(par + 1) * D])
            b.wait("sync", sSt[0], sSt[0].n)
            b.wait("sync", sSt[1], sSt[1].n)
            b.flush()
    return nc


_PROG = None


def _get_prog():
    global _PROG
    if _PROG is None:
        _PROG = build_program()
    return _PROG


def kernel(**inputs):
    f32 = np.float32

    def cc(a):
        return np.ascontiguousarray(a, dtype=f32)

    ins = {k[len("L0_attn_"):]: v for k, v in inputs.items() if k.startswith("L0_attn_")}
    ins.update({k[len("L0_ffn_"):]: v for k, v in inputs.items() if k.startswith("L0_ffn_")})

    base = {}
    for s in "fb":
        win = np.asarray(ins[f"win_{s}"], dtype=f32)
        base[f"win_sl_{s}"] = cc(win.reshape(NXT, P, NKD, P).transpose(0, 3, 2, 1)
                                 .reshape(NXT, P, D))
        base[f"wout_sl_{s}"] = cc(np.asarray(ins[f"wout_{s}"], dtype=f32).T
                                  .reshape(NET, P, D))
        base[f"wx_sl_{s}"] = cc(np.asarray(ins[f"wx_{s}"], dtype=f32).T
                                .reshape(NET, P, 96))
        base[f"wdt_{s}"] = cc(np.asarray(ins[f"wdt_{s}"], dtype=f32).T)
        bconv = np.asarray(ins[f"bconv_{s}"], dtype=f32).reshape(NET, P).T
        wconv = np.asarray(ins[f"wconv_{s}"], dtype=f32)[:, 0, :] \
            .reshape(NET, P, 4).transpose(1, 0, 2).reshape(P, NET * 4)
        bdt = np.asarray(ins[f"bdt_{s}"], dtype=f32).reshape(NET, P).T
        Dp = np.asarray(ins[f"Dp_{s}"], dtype=f32).reshape(NET, P).T
        base[f"cvec_{s}"] = cc(np.concatenate([bconv, wconv, bdt, Dp], axis=1))
        base[f"alog_{s}"] = cc(np.asarray(ins[f"Alog_{s}"], dtype=f32)
                               .reshape(NET, P, NS).transpose(1, 0, 2).reshape(P, NET * NS))
    base["w1t_sl"] = cc(np.asarray(ins["W1"], dtype=f32).T.reshape(NKD, P, FF))
    base["w2t_sl"] = cc(np.asarray(ins["W2"], dtype=f32).T.reshape(NFT, P, D))
    base["wproj_sl"] = cc(np.asarray(ins["wproj"], dtype=f32).T.reshape(NET, P, D))
    base["b1col"] = cc(np.asarray(ins["b1"], dtype=f32).reshape(NFT, P).T)
    base["b2row"] = cc(np.asarray(ins["b2"], dtype=f32).reshape(1, D))
    base["bprojrow"] = cc(np.asarray(ins["bproj"], dtype=f32).reshape(1, D))
    base["identity"] = np.eye(P, dtype=f32)

    x_all = np.asarray(ins["x"], dtype=f32)
    n_cores = x_all.shape[0]
    in_maps = []
    for c in range(n_cores):
        m = dict(base)
        m["x"] = cc(x_all[c])
        in_maps.append(m)

    nc = _get_prog()
    res = run_bass_kernel_spmd(nc, in_maps, core_ids=list(range(n_cores)))
    out = np.stack([np.asarray(r["y"], dtype=np.float32) for r in res.results], axis=0)
    return out
